# version 17
# speedup vs baseline: 1.4052x; 1.0143x over previous
import contextlib
import numpy as np
import ml_dtypes
import concourse.bass as bass
import concourse.mybir as mybir
from concourse.bass_utils import run_bass_kernel_spmd
from concourse.alu_op_type import AluOpType as OP

F32 = mybir.dt.float32
BF16 = mybir.dt.bfloat16
AF = mybir.ActivationFunctionType
AX = mybir.AxisListType.X

B, T, D = 4, 8192, 1024
H, DK, SEG, NCH = 16, 64, 256, 32
NT = T // 128
NTC = T // 512
E, R, ER, GH = 8, 16, 128, 256
SCALE = float(DK) ** -0.5
EPS_LN, EPS_RMS, EPS_DEN = 1e-5, 1e-6, 1e-6
NEG_PAD = -5e29

DBG = []


class Ctr:
    def __init__(self, nc, ctx, name):
        self.h = ctx.enter_context(nc.semaphore(name))
        self.v = 0

    def inc(self, inst, k=1):
        inst.then_inc(self.h, k)
        self.v += k
        return self.v

    def inc16(self, inst):
        return self.inc(inst, 16)


def build_nc(dbg=()):
    nc = bass.Bass("TRN2", target_bir_lowering=False, debug=False)

    def din(name, shape, dtype=F32):
        return nc.dram_tensor(name, shape, dtype, kind="ExternalInput").ap()

    x = din("x", [T, D])
    wT = {n: din(n, [D, 512]) for n in ("wqT", "wkT", "wvT", "wuT", "woT",
                                        "fqT", "fkT", "fvT", "foT")}
    vcT = {p: din(f"vcT_{p}", [D, ER]) for p in "qkvo"}
    ucat = {p: din(f"ucat_{p}", [ER, 512]) for p in "qkvo"}
    bcol = {p: din(f"bcol_{p}", [128, 4]) for p in "qko"}
    vb_row = din("vb_row", [1, 512])
    srow = {p: din(f"srow_{p}", [1, ER]) for p in "qkvo"}
    nw_pa = din("nw_pa", [128, 8])
    gw1T, gb1 = din("gw1T", [D, GH]), din("gb1", [1, GH])
    gw2T, gb2 = din("gw2T", [GH, E]), din("gb2", [1, E])
    identb = din("identb", [128, 128], BF16)
    identf = din("identf", [128, 128], F32)
    triw = din("triw", [SEG, SEG])
    trik = din("trik", [SEG, SEG], BF16)
    cntinv = din("cntinv", [128, 2])
    e33 = din("e33", [33, 33 * 128], BF16)
    e16 = din("e16", [E, 128], BF16)

    out = nc.dram_tensor("out", [T, 512], F32, kind="ExternalOutput").ap()

    def dint(name, shape, dtype):
        return nc.dram_tensor(name, shape, dtype)

    xn_tm = dint("xn_tm", [T, D], BF16)
    qf = dint("qf", [4 * NTC * 128, 512], BF16)
    kf = dint("kf", [4 * NTC * 128, 512], BF16)
    vtok = dint("vtok", [4 * 64 * 128, 128], BF16)
    pqc = dint("pqc", [NCH * 4 * 128, 256], BF16)
    pre_in = dint("pre_in", [512, T], BF16)
    pre_out = dint("pre_out", [1024, T], BF16)
    ctx_in = dint("ctx_in", [512, T], BF16)
    ctx_out = dint("ctx_out", [1024, T], BF16)
    sc_in = dint("sc_in", [T, 33], F32)
    sc_out = dint("sc_out", [T, 33], F32)

    dbg_out = {}
    for name, shape in [
        ("xn_d", [T, D]), ("pooled_d", [128, 8]), ("gate_d", [1, 8]),
        ("rms_d", [1, T]), ("qf_d", [4 * NTC * 128, 512]),
        ("kf_d", [4 * NTC * 128, 512]), ("vt_d", [4 * 64 * 128, 128]),
        ("pre_d", [512, T]), ("sc_d", [T, 33]), ("ctx_d", [512, T]),
        ("hist_d", [128, 4 * NCH * 64]), ("gates_d", [128, 64 * 33]),
        ("s_d", [128, NCH * 1024]), ("g_d", [128, NCH * 1024]),
        ("pq_d", [NCH * 4 * 128, 256]), ("den_d", [128, NCH * 1024]),
        ("bc_d", [128, NCH * 1024]), ("num_d", [128, NCH * 1024]),
        ("zz_d", [128, NCH * 1024]), ("atm_d", [128, NCH * 1024]),
    ]:
        if name in dbg:
            dbg_out[name] = nc.dram_tensor(name, shape, F32, kind="ExternalOutput").ap()

    ctxG = contextlib.ExitStack()

    def sbG(name, shape, dtype):
        return ctxG.enter_context(nc.sbuf_tensor(name, shape, dtype))

    def psG(name, shape, dtype=F32):
        return ctxG.enter_context(nc.psum_tensor(name, shape, dtype))

    w = {n: sbG("w_" + n, [128, 8 * 512], BF16) for n in
         ("wq", "wk", "wv", "wu", "wo", "fq", "fk", "fv", "fo")}
    w_vc = {p: sbG(f"w_vc_{p}", [128, 8 * 128], BF16) for p in "qkvo"}
    w_uc = {p: sbG(f"w_uc_{p}", [128, 512], BF16) for p in "qkvo"}
    w_g1 = sbG("w_g1", [128, 8 * 256], BF16)
    w_g2 = sbG("w_g2", [128, 2 * 8], BF16)
    c_identb = sbG("c_identb", [128, 128], BF16)
    c_identf = sbG("c_identf", [128, 128], F32)
    c_triw = sbG("c_triw", [128, 2 * 256], F32)
    c_trik = sbG("c_trik", [128, 2 * 256], BF16)
    c_cnt = sbG("c_cnt", [128, 2], F32)
    c_e33 = sbG("c_e33", [33, 33 * 128], BF16)
    c_e16 = sbG("c_e16", [E, 128], BF16)
    b_q = sbG("b_q", [128, 4], F32)
    b_k = sbG("b_k", [128, 4], F32)
    b_o = sbG("b_o", [128, 4], F32)
    b_v = sbG("b_v", [1, 512], BF16)
    c_nw = sbG("c_nw", [128, 8], F32)
    c_gb1 = sbG("c_gb1", [1, GH], F32)
    c_gb2 = sbG("c_gb2", [1, E], F32)
    c_srow = {p: sbG(f"c_srow_{p}", [1, ER], F32) for p in "qkvo"}
    ones_col = sbG("ones_col", [128, 1], BF16)
    c_eps = sbG("c_eps", [128, 1], F32)
    ones_row = sbG("ones_row", [1, 128], BF16)
    ones_r2 = sbG("ones_r2", [65, 128], BF16)
    ones_sq = sbG("ones_sq", [128, 128], BF16)
    scl128 = {p: sbG(f"scl128_{p}", [128, 1], F32) for p in "qkvo"}
    ge128 = sbG("ge128", [128, 1], F32)
    rms_row = sbG("rms_row", [1, T], BF16)
    a_hist = sbG("a_hist", [128, 4 * NCH * 64], BF16)
    a_scum = sbG("a_scum", [128, 4 * 64], F32)
    a_scbf = sbG("a_scbf", [128, 4 * 64], BF16)
    a_z = sbG("a_z", [128, 4], F32)
    a_stat = sbG("a_stat", [128, 8], F32)
    st_sm = sbG("st_sm", [128, 12], F32)

    p_a = psG("p_a", [128, 1024])
    p_b = psG("p_b", [128, 1024])
    p_c = psG("p_c", [128, 512])
    p_d = psG("p_d", [128, 512])
    p_t = psG("p_t", [128, 512], BF16)
    p_tf = psG("p_tf", [128, 512])

    sems = {n: Ctr(nc, ctxG, n) for n in ("dS", "dG", "tT", "tA", "tV", "cc")}
    dS, dG, tT, tA, tV, cc = (sems[n] for n in ("dS", "dG", "tT", "tA", "tV", "cc"))

    ctxG.enter_context(nc.allow_low_precision(
        reason="bf16 matmul operand stream with fp32 accumulation by design; "
               "tolerance bar measured for bf16 matmuls"))
    blk = ctxG.enter_context(nc.Block())

    def barrier():
        return (tT.v, tA.v, tV.v, dS.v, dG.v)

    def wait_barrier(eng, bv):
        vt, va, vv, vs, vg = bv
        if vt: eng.wait_ge(tT.h, vt)
        if va: eng.wait_ge(tA.h, va)
        if vv: eng.wait_ge(tV.h, vv)
        if vs: eng.wait_ge(dS.h, vs)
        if vg: eng.wait_ge(dG.h, vg)

    def hsl(h):
        return 64 * (h % 2), h // 2

    @blk.gpsimd
    def _(g):
        names = {"wq": "wqT", "wk": "wkT", "wv": "wvT", "wu": "wuT",
                 "wo": "woT", "fq": "fqT", "fk": "fkT", "fv": "fvT", "fo": "foT"}
        for n, sn in names.items():
            dG.inc16(g.dma_start(out=w[n][:, :].rearrange("p (a f) -> p a f", a=8),
                                 in_=wT[sn].rearrange("(a p) f -> p a f", p=128)))
        for p in "qkvo":
            dG.inc16(g.dma_start(out=w_vc[p][:, :].rearrange("p (a f) -> p a f", a=8),
                                 in_=vcT[p].rearrange("(a p) f -> p a f", p=128)))
            dG.inc16(g.dma_start(out=w_uc[p][:, :], in_=ucat[p]))
            dG.inc16(g.dma_start(out=c_srow[p][:, :], in_=srow[p]))
        dG.inc16(g.dma_start(out=w_g1[:, :].rearrange("p (a f) -> p a f", a=8),
                             in_=gw1T.rearrange("(a p) f -> p a f", p=128)))
        dG.inc16(g.dma_start(out=w_g2[:, :].rearrange("p (a f) -> p a f", a=2),
                             in_=gw2T.rearrange("(a p) f -> p a f", p=128)))
        dG.inc16(g.dma_start(out=c_identb[:, :], in_=identb))
        dG.inc16(g.dma_start(out=c_identf[:, :], in_=identf))
        dG.inc16(g.dma_start(out=c_triw[:, :].rearrange("p (a c) -> p a c", a=2),
                             in_=triw.rearrange("(a p) c -> p a c", p=128)))
        dG.inc16(g.dma_start(out=c_trik[:, :].rearrange("p (a c) -> p a c", a=2),
                             in_=trik.rearrange("(a p) c -> p a c", p=128)))
        dG.inc16(g.dma_start(out=c_cnt[:, :], in_=cntinv))
        dG.inc16(g.dma_start(out=c_e33[:, :], in_=e33))
        dG.inc16(g.dma_start(out=c_e16[:, :], in_=e16))
        for dst, src in [(b_q, bcol["q"]), (b_k, bcol["k"]), (b_o, bcol["o"])]:
            dG.inc16(g.dma_start(out=dst[:, :], in_=src))
        dG.inc16(g.dma_start(out=b_v[:, :], in_=vb_row))
        dG.inc16(g.dma_start(out=c_nw[:, :], in_=nw_pa))
        dG.inc16(g.dma_start(out=c_gb1[:, :], in_=gb1))
        dG.inc16(g.dma_start(out=c_gb2[:, :], in_=gb2))
    v_w = dG.v

    @blk.vector
    def _(v):
        v.memset(ones_col[:, :], 1.0)
        v.memset(ones_row[:, :], 1.0)
        v.memset(ones_r2[:, :], 1.0)
        v.memset(ones_sq[:, :], 1.0)
        v.memset(a_scum[:, :], 0.0)
        v.memset(a_scbf[:, :], 0.0)
        v.memset(a_z[:, :], 0.0)
        v.memset(c_eps[:, :], EPS_LN)
        tV.inc(v.memset(p_c[:, 0:8], 0.0))
    v_ones = tV.v

    ctxA = contextlib.ExitStack()

    def sbA(name, shape, dtype):
        return ctxA.enter_context(nc.sbuf_tensor(name, shape, dtype))

    x_t = sbA("x_t", [128, 4 * 1024], F32)
    sq_scr = sbA("sq_scr", [128, 1024], BF16)
    xn_t = sbA("xn_t", [128, 4 * 1024], BF16)
    pool_bf = sbA("pool_bf", [128, 8], BF16)
    g_sc = sbA("g_sc", [1, 64], F32)
    g_h = sbA("g_h", [1, GH], F32)
    g_hb = sbA("g_hb", [128, 3], BF16)
    xf = sbA("xf", [128, 2 * 8 * 512], BF16)
    x2 = sbA("x2", [128, 8 * 512], BF16)
    xvs = sbA("xvs", [128, 512], BF16)
    prj = sbA("prj", [128, 4 * 512], BF16)
    prj2 = sbA("prj2", [128, 4 * 512], BF16)
    rms_scr = sbA("rms_scr", [1, 512], F32)
    rms_sq = sbA("rms_sq", [1, 512], F32)

    NB = 4
    v_xnw, v_xndma, v_pool = {}, {}, {}
    v_statdone = {}
    for it in range(NT // NB):
        @blk.sync
        def _(sy, it=it):
            if it >= 1:
                sy.wait_ge(tV.h, v_xnw[it - 1])
            for ti in range(NB):
                i = it * NB + ti
                dS.inc16(sy.dma_start(out=x_t[:, ti * 1024:(ti + 1) * 1024],
                                      in_=x[i * 128:(i + 1) * 128, :]))
        v_x = dS.v

        @blk.scalar
        def _(s, it=it, v_x=v_x):
            s.wait_ge(dS.h, v_x)
            if it == 0:
                s.wait_ge(tV.h, v_ones)
            if it >= 1:
                s.wait_ge(tV.h, v_statdone[it - 1])
            for ti in range(NB):
                inst = s.activation(sq_scr[:, :], x_t[:, ti * 1024:(ti + 1) * 1024],
                                    AF.Square, accum_out=st_sm[:, 4 + ti:5 + ti])
            tA.inc(inst)
        v_ssq = tA.v

        @blk.vector
        def _(v, it=it, v_x=v_x, v_ssq=v_ssq):
            v.wait_ge(dS.h, v_x)
            if it >= 1:
                v.wait_ge(dS.h, v_xndma[it - 1])
                v.wait_ge(tT.h, v_pool[it - 1])
            for ti in range(NB):
                inst = v.reduce_sum(st_sm[:, ti:ti + 1],
                                    x_t[:, ti * 1024:(ti + 1) * 1024], axis=AX)
            tV.inc(inst)
            v.wait_ge(tV.h, tV.v)
            tV.inc(v.tensor_scalar(st_sm[:, 0:4], st_sm[:, 0:4], 1.0 / D, None, OP.mult))
            v.wait_ge(tV.h, tV.v)
            tV.inc(v.tensor_mul(st_sm[:, 8:12], st_sm[:, 0:4], st_sm[:, 0:4]))
            v.wait_ge(tA.h, v_ssq)
            v.wait_ge(tV.h, tV.v)
            tV.inc(v.scalar_tensor_tensor(st_sm[:, 4:8], st_sm[:, 4:8], 1.0 / D,
                                          st_sm[:, 8:12], OP.mult, OP.subtract))
        v_var = tV.v

        @blk.scalar
        def _(s, v_var=v_var):
            s.wait_ge(tV.h, v_var)
            tA.inc(s.activation(st_sm[:, 4:8], st_sm[:, 4:8], AF.Sqrt, bias=c_eps[:, :]))
        v_sd = tA.v

        @blk.vector
        def _(v, it=it, v_sd=v_sd):
            v.wait_ge(tA.h, v_sd)
            tV.inc(v.reciprocal(st_sm[:, 8:12], st_sm[:, 4:8]))
            v.wait_ge(tV.h, tV.v)
            for ti in range(NB):
                inst = v.tensor_scalar(xn_t[:, ti * 1024:(ti + 1) * 1024],
                                       x_t[:, ti * 1024:(ti + 1) * 1024],
                                       st_sm[:, ti:ti + 1], st_sm[:, 8 + ti:9 + ti],
                                       OP.subtract, OP.mult)
            tV.inc(inst)
        v_xnw[it] = tV.v
        v_statdone[it] = tV.v

        @blk.tensor
        def _(t, it=it, vx=v_xnw[it]):
            t.wait_ge(tV.h, vx)
            for ti in range(NB):
                for ds in range(8):
                    inst = t.matmul(p_c[:, ds:ds + 1],
                                    xn_t[:, ti * 1024 + ds * 128: ti * 1024 + (ds + 1) * 128],
                                    ones_col[:, :], start=False,
                                    stop=(it == NT // NB - 1 and ti == NB - 1))
            tT.inc(inst)
        v_pool[it] = tT.v

        @blk.sync
        def _(sy, it=it, vx=v_xnw[it]):
            sy.wait_ge(tV.h, vx)
            dS.inc16(sy.dma_start(
                out=xn_tm.ap()[it * NB * 128:(it + 1) * NB * 128, :]
                    .rearrange("(ti p) c -> p ti c", p=128),
                in_=xn_t[:, :].rearrange("p (ti c) -> p ti c", ti=NB)))
        v_xndma[it] = dS.v
    v_xn_all = dS.v
    if "xn_d" in dbg_out:
        @blk.gpsimd
        def _(g):
            g.wait_ge(dS.h, v_xn_all)
            dG.inc16(g.dma_start(out=dbg_out["xn_d"], in_=xn_tm.ap()[:, :]))

    @blk.vector
    def _(v):
        v.wait_ge(tT.h, v_pool[NT // 4 - 1])
        v.wait_ge(dG.h, v_w)
        tV.inc(v.tensor_scalar(pool_bf[:, :], p_c[:, 0:8], 1.0 / T, None, OP.mult))
    v_poolbf = tV.v

    @blk.tensor
    def _(t):
        t.wait_ge(tV.h, v_poolbf)
        for ds in range(8):
            inst = t.matmul(p_d[0:1, 0:GH], pool_bf[:, ds:ds + 1],
                            w_g1[:, ds * 256:(ds + 1) * 256], start=(ds == 0), stop=(ds == 7))
        tT.inc(inst)
    v_g1 = tT.v

    @blk.vector
    def _(v):
        v.wait_ge(tT.h, v_g1)
        tV.inc(v.tensor_add(g_h[:, :], p_d[0:1, 0:GH], c_gb1[:, :]))
        v.wait_ge(tV.h, tV.v)
        tV.inc(v.tensor_relu(g_h[:, :], g_h[:, :]))
    v_h = tV.v

    @blk.tensor
    def _(t):
        t.wait_ge(tV.h, v_h)
        t.transpose(p_tf[:, 0:1], g_h[0:1, 0:128], c_identf[0:1, 0:1])
        tT.inc(t.transpose(p_tf[:, 1:2], g_h[0:1, 128:256], c_identf[0:1, 0:1]))
    v_ht = tT.v

    @blk.vector
    def _(v):
        v.wait_ge(tT.h, v_ht)
        tV.inc(v.tensor_copy(g_hb[:, 0:2], p_tf[:, 0:2]))
    v_hb = tV.v

    @blk.tensor
    def _(t):
        t.wait_ge(tV.h, v_hb)
        t.matmul(p_d[0:1, 256:256 + E], g_hb[:, 0:1], w_g2[:, 0:8], start=True, stop=False)
        tT.inc(t.matmul(p_d[0:1, 256:256 + E], g_hb[:, 1:2], w_g2[:, 8:16],
                        start=False, stop=True))
    v_g2 = tT.v

    @blk.vector
    def _(v):
        v.wait_ge(tT.h, v_g2)
        tV.inc(v.tensor_add(g_sc[:, 0:8], p_d[0:1, 256:256 + E], c_gb2[:, :]))
        v.wait_ge(tV.h, tV.v)
        tV.inc(v.reduce_max(g_sc[:, 8:9], g_sc[:, 0:8], axis=AX))
        v.wait_ge(tV.h, tV.v)
        tV.inc(v.tensor_scalar(g_sc[:, 16:24], g_sc[:, 0:8], g_sc[:, 8:9], None, OP.is_ge))
        v.wait_ge(tV.h, tV.v)
        tV.inc(v.scalar_tensor_tensor(g_sc[:, 24:32], g_sc[:, 16:24], -1e30,
                                      g_sc[:, 0:8], OP.mult, OP.add))
        v.wait_ge(tV.h, tV.v)
        tV.inc(v.reduce_max(g_sc[:, 9:10], g_sc[:, 24:32], axis=AX))
        v.wait_ge(tV.h, tV.v)
        tV.inc(v.tensor_scalar(g_sc[:, 24:32], g_sc[:, 24:32], g_sc[:, 9:10], None, OP.is_ge))
        tV.inc(v.tensor_sub(g_sc[:, 32:33], g_sc[:, 8:9], g_sc[:, 9:10]))
    v_top = tV.v

    @blk.scalar
    def _(s):
        s.wait_ge(tV.h, v_top)
        tA.inc(s.activation(g_sc[:, 32:33], g_sc[:, 32:33], AF.Sigmoid))
    v_sig = tA.v

    @blk.vector
    def _(v):
        v.wait_ge(tA.h, v_sig)
        tV.inc(v.tensor_scalar(g_sc[:, 40:48], g_sc[:, 16:24], g_sc[:, 32:33], None, OP.mult))
        tV.inc(v.scalar_tensor_tensor(g_sc[:, 48:56], g_sc[:, 24:32], g_sc[:, 32:33],
                                      g_sc[:, 24:32], OP.mult, OP.subtract))
        v.wait_ge(tV.h, tV.v)
        tV.inc(v.tensor_sub(g_sc[:, 40:48], g_sc[:, 40:48], g_sc[:, 48:56]))
    v_gate = tV.v

    @blk.tensor
    def _(t):
        t.wait_ge(tV.h, v_gate)
        tT.inc(t.transpose(p_tf[0:8, 2:3], g_sc[0:1, 40:48], c_identf[0:1, 0:1]))
        for ji, p in enumerate("qkvo"):
            tT.inc(t.transpose(p_tf[:, 3 + ji:4 + ji], c_srow[p][0:1, :],
                               c_identf[0:1, 0:1]))
    v_gt = tT.v

    @blk.vector
    def _(v):
        v.wait_ge(tT.h, v_gt)
        tV.inc(v.tensor_copy(g_hb[0:8, 2:3], p_tf[0:8, 2:3]))
    v_gtb = tV.v

    @blk.tensor
    def _(t):
        t.wait_ge(tV.h, v_gtb)
        tT.inc(t.matmul(p_tf[:, 7:8], c_e16[:, :], g_hb[0:E, 2:3], start=True, stop=True))
    v_ge = tT.v

    @blk.vector
    def _(v):
        v.wait_ge(tT.h, v_ge)
        tV.inc(v.tensor_copy(ge128[:, :], p_tf[:, 7:8]))
        v.wait_ge(tV.h, tV.v)
        for ji, p in enumerate("qkvo"):
            inst = v.tensor_mul(scl128[p][:, :], ge128[:, :], p_tf[:, 3 + ji:4 + ji])
        tV.inc(inst)
        if "gate_d" in dbg_out:
            tV.inc(v.tensor_copy(g_h[0:1, 0:8], g_sc[0:1, 40:48]))
    v_scl = tV.v

    if "pooled_d" in dbg_out or "gate_d" in dbg_out:
        @blk.gpsimd
        def _(g):
            if "pooled_d" in dbg_out:
                g.wait_ge(tV.h, v_poolbf)
                dG.inc16(g.dma_start(out=dbg_out["pooled_d"], in_=pool_bf[:, :]))
            if "gate_d" in dbg_out:
                g.wait_ge(tV.h, v_scl)
                dG.inc16(g.dma_start(out=dbg_out["gate_d"], in_=g_h[0:1, 0:8]))

    v_tc_pe, v_tc_ve, v_prjdma, v_prj2dma, v_rms = {}, {}, {}, {}, {}
    last_pd_read = v_top
    last_xvs = [v_scl]
    last_pab_evac = [v_scl]
    for tc in range(NTC):
        fa = tc % 2
        xo = fa * 4096

        @blk.sync
        def _(sy, tc=tc, xo=xo):
            if tc >= 2:
                sy.wait_ge(tT.h, v_tc_pe[tc - 2])
                sy.wait_ge(tV.h, v_tc_ve[tc - 2])
            sy.wait_ge(dS.h, v_xn_all)
            for dt in range(8):
                inst = sy.dma_start(
                    out=xf[:, xo + dt * 512: xo + (dt + 1) * 512],
                    in_=xn_tm.ap()[tc * 512:(tc + 1) * 512, dt * 128:(dt + 1) * 128],
                    transpose=True)
                dS.inc16(inst)
        v_xf = dS.v

        @blk.vector
        def _(v, tc=tc, xo=xo, v_xf=v_xf):
            v.wait_ge(dS.h, v_xf)
            if tc >= 1:
                v.wait_ge(tT.h, v_rms[tc - 1])
            tV.inc(v.tensor_mul(x2[:, :], xf[:, xo:xo + 4096], xf[:, xo:xo + 4096]))
        v_x2 = tV.v

        @blk.tensor
        def _(t, tc=tc, v_x2=v_x2, lpd=last_pd_read):
            t.wait_ge(tV.h, v_x2)
            t.wait_ge(tV.h, lpd)
            for dt in range(8):
                inst = t.matmul(p_d[0:1, 0:512], ones_col[:, :],
                                x2[:, dt * 512:(dt + 1) * 512],
                                start=(dt == 0), stop=(dt == 7))
            tT.inc(inst)
        v_rms[tc] = tT.v

        @blk.vector
        def _(v, vr=v_rms[tc]):
            v.wait_ge(tT.h, vr)
            tV.inc(v.tensor_scalar(rms_scr[:, :], p_d[0:1, 0:512], 1.0 / D, EPS_RMS,
                                   OP.mult, OP.add))
        v_rsc = tV.v
        last_pd_read = v_rsc

        @blk.scalar
        def _(s, v_rsc=v_rsc):
            s.wait_ge(tV.h, v_rsc)
            tA.inc(s.activation(rms_sq[:, :], rms_scr[:, :], AF.Sqrt))
        v_rq = tA.v

        @blk.vector
        def _(v, tc=tc, v_rq=v_rq):
            v.wait_ge(tA.h, v_rq)
            tV.inc(v.reciprocal(rms_row[0:1, tc * 512:(tc + 1) * 512], rms_sq[:, :]))

        for pi, p in enumerate("qkv"):
            wm = w["f" + p]

            @blk.tensor
            def _(t, tc=tc, xo=xo, p=p, pi=pi, lxv=last_xvs[0]):
                if tc == 0 and pi == 0:
                    t.wait_ge(tV.h, v_scl)
                    t.wait_ge(tV.h, v_poolbf)
                else:
                    t.wait_ge(tV.h, lxv)
                for dt in range(8):
                    inst = t.matmul(p_c[:, 0:512], w_vc[p][:, dt * 128:(dt + 1) * 128],
                                    xf[:, xo + dt * 512: xo + (dt + 1) * 512],
                                    start=(dt == 0), stop=(dt == 7))
                tT.inc(inst)
            v_xv = tT.v

            @blk.vector
            def _(v, p=p, v_xv=v_xv):
                v.wait_ge(tT.h, v_xv)
                tV.inc(v.tensor_scalar(xvs[:, :], p_c[:, 0:512], scl128[p][:, :],
                                       None, OP.mult))
            last_xvs = [tV.v]
            v_xvs = tV.v

            if p in "qk":
                stg = prj if p == "q" else prj2

                @blk.tensor
                def _(t, tc=tc, xo=xo, p=p, wm=wm, v_xvs=v_xvs, lpe=last_pab_evac[0]):
                    t.wait_ge(tV.h, v_xvs)
                    t.wait_ge(tV.h, lpe)
                    for fs in range(4):
                        pp = p_a if fs < 2 else p_b
                        po = (fs % 2) * 512
                        for dt in range(8):
                            t.matmul(pp[:, po:po + 512],
                                     wm[:, dt * 512 + fs * 128: dt * 512 + fs * 128 + 128],
                                     xf[:, xo + dt * 512: xo + (dt + 1) * 512],
                                     start=(dt == 0), stop=False)
                        inst = t.matmul(pp[:, po:po + 512],
                                        w_uc[p][:, fs * 128:(fs + 1) * 128], xvs[:, :],
                                        start=False, stop=True)
                    tT.inc(inst)
                v_mm = tT.v

                @blk.vector
                def _(v, tc=tc, p=p, stg=stg, v_mm=v_mm):
                    v.wait_ge(tT.h, v_mm)
                    if tc >= 1:
                        v.wait_ge(dS.h, (v_prjdma if p == "q" else v_prj2dma)[tc - 1])
                    for fs in range(4):
                        pp = p_a if fs < 2 else p_b
                        po = (fs % 2) * 512
                        if p == "q":
                            inst = v.tensor_scalar(stg[:, fs * 512:(fs + 1) * 512],
                                                   pp[:, po:po + 512], b_q[:, fs:fs + 1],
                                                   SCALE, OP.add, OP.mult)
                        else:
                            inst = v.tensor_scalar(stg[:, fs * 512:(fs + 1) * 512],
                                                   pp[:, po:po + 512], b_k[:, fs:fs + 1],
                                                   None, OP.add)
                    tV.inc(inst)
                v_stg = tV.v
                last_pab_evac = [v_stg]

                @blk.sync
                def _(sy, tc=tc, p=p, stg=stg, v_stg=v_stg):
                    sy.wait_ge(tV.h, v_stg)
                    dst = qf if p == "q" else kf
                    for fs in range(4):
                        inst = sy.dma_start(
                            out=dst.ap()[(fs * NTC + tc) * 128:(fs * NTC + tc) * 128 + 128, :],
                            in_=stg[:, fs * 512:(fs + 1) * 512])
                        dS.inc16(inst)
                if p == "q":
                    v_prjdma[tc] = dS.v
                else:
                    v_prj2dma[tc] = dS.v

                if ("qf_d" in dbg_out and p == "q") or ("kf_d" in dbg_out and p == "k"):
                    @blk.gpsimd
                    def _(g, tc=tc, p=p, stg=stg, v_stg=v_stg):
                        g.wait_ge(tV.h, v_stg)
                        dd = dbg_out["qf_d" if p == "q" else "kf_d"]
                        for fs in range(4):
                            dG.inc16(g.dma_start(
                                out=dd[(fs * NTC + tc) * 128:(fs * NTC + tc) * 128 + 128, :],
                                in_=stg[:, fs * 512:(fs + 1) * 512]))
            else:
                @blk.tensor
                def _(t, tc=tc, xo=xo, wm=wm, v_xvs=v_xvs, lpe=last_pab_evac[0]):
                    t.wait_ge(tV.h, v_xvs)
                    t.wait_ge(tV.h, lpe)
                    t.wait_ge(tV.h, v_ones)
                    for mt in range(4):
                        pp = p_a if mt < 2 else p_b
                        po = (mt % 2) * 512
                        for dt in range(8):
                            t.matmul(pp[:, po:po + 512],
                                     xf[:, xo + dt * 512 + mt * 128: xo + dt * 512 + mt * 128 + 128],
                                     wm[:, dt * 512:(dt + 1) * 512],
                                     start=(dt == 0), stop=False)
                        t.matmul(pp[:, po:po + 512], xvs[:, mt * 128:(mt + 1) * 128],
                                 w_uc["v"][:, :], start=False, stop=False)
                        inst = t.matmul(pp[:, po:po + 512], ones_row[:, :], b_v[:, :],
                                        start=False, stop=True)
                    tT.inc(inst)
                v_vmm = tT.v

                @blk.vector
                def _(v, tc=tc, v_vmm=v_vmm):
                    v.wait_ge(tT.h, v_vmm)
                    v.wait_ge(dS.h, v_prjdma[tc])
                    for mt in range(4):
                        pp = p_a if mt < 2 else p_b
                        po = (mt % 2) * 512
                        inst = v.tensor_copy(prj[:, mt * 512:(mt + 1) * 512], pp[:, po:po + 512])
                    tV.inc(inst)
                v_vst = tV.v
                last_pab_evac = [v_vst]

                @blk.sync
                def _(sy, tc=tc, v_vst=v_vst):
                    sy.wait_ge(tV.h, v_vst)
                    for pr in range(4):
                        inst = sy.dma_start(
                            out=vtok.ap()[(pr * 64 + tc * 4) * 128:(pr * 64 + tc * 4 + 4) * 128, :]
                                .rearrange("(mt p) f -> p mt f", p=128),
                            in_=prj[:, :].rearrange("p (mt f) -> p mt f", mt=4)[
                                :, :, pr * 128:(pr + 1) * 128])
                        dS.inc16(inst)

                if "vt_d" in dbg_out:
                    @blk.gpsimd
                    def _(g, tc=tc, v_vst=v_vst):
                        g.wait_ge(tV.h, v_vst)
                        for pr in range(4):
                            dG.inc16(g.dma_start(
                                out=dbg_out["vt_d"][(pr * 64 + tc * 4) * 128:(pr * 64 + tc * 4 + 4) * 128, :]
                                    .rearrange("(mt p) f -> p mt f", p=128),
                                in_=prj[:, :].rearrange("p (mt f) -> p mt f", mt=4)[
                                    :, :, pr * 128:(pr + 1) * 128]))
        v_tc_pe[tc] = tT.v
        v_tc_ve[tc] = tV.v

    if "rms_d" in dbg_out:
        @blk.gpsimd
        def _(g):
            g.wait_ge(tV.h, tV.v)
            dG.inc16(g.dma_start(out=dbg_out["rms_d"], in_=rms_row[:, :]))

    bA = barrier()
    ctxA.close()

    ctxB = contextlib.ExitStack()

    def sbB(name, shape, dtype):
        return ctxB.enter_context(nc.sbuf_tensor(name, shape, dtype))

    a_xn = sbB("a_xn", [128, 2 * 8 * 256], BF16)
    a_xn2 = sbB("a_xn2", [128, 8 * 256], BF16)
    a_rbc = sbB("a_rbc", [128, 256], BF16)
    a_pq = sbB("a_pq", [128, 4 * 256], BF16)
    a_pk = sbB("a_pk", [128, 4 * 256], BF16)
    a_vc = sbB("a_vc", [128, 4 * 256], BF16)
    a_el = sbB("a_el", [128, 4 * 256], BF16)
    a_pkT = sbB("a_pkT", [128, 2 * 512], BF16)
    a_vcT = sbB("a_vcT", [128, 2 * 512], BF16)
    a_atm = sbB("a_atm", [128, 2 * 512], BF16)
    a_zbf = sbB("a_zbf", [128, 4 * 256], BF16)
    a_pt = sbB("a_pt", [128, 4 * 256], BF16)
    a_u = sbB("a_u", [128, 4 * 256], BF16)
    a_den = sbB("a_den", [128, 4 * 256], F32)
    a_den2 = sbB("a_den2", [128, 4 * 256], BF16)
    a_den3 = sbB("a_den3", [128, 4 * 256], F32)
    a_bc = sbB("a_bc", [128, 4 * 256], BF16)
    a_pre = sbB("a_pre", [128, 4 * 256], BF16)
    a_ms = sbB("a_ms", [128, 4 * 32], BF16)
    a_scst = sbB("a_scst", [128, 2 * 33], F32)
    a_wm = sbB("a_wm", [128, 2 * 256], F32)

    v_ch = {}
    v_axn_d = {}
    v_axn_cons = {}
    v_rbcd, v_xn2d = {}, {}
    v_prem_hist = [0]
    v_u_hist = [0]

    def emit_head(mm, v_pc_free):
        @blk.tensor
        def _(t, mm=mm, v_pc_free=v_pc_free):
            if mm == 0:
                wait_barrier(t, bA)
            elif v_pc_free:
                t.wait_ge(tV.h, v_pc_free)
            tT.inc(t.matmul(p_c[:, 0:256], ones_row[:, :],
                            rms_row[0:1, mm * 256:(mm + 1) * 256],
                            start=True, stop=True))
        v_rbcm = tT.v
        v_rbcd[mm] = v_rbcm

        @blk.vector
        def _(v, mm=mm, v_rbcm=v_rbcm):
            if mm == 0:
                wait_barrier(v, bA)
            v.wait_ge(tT.h, v_rbcm)
            tV.inc(v.tensor_copy(a_rbc[:, :], p_c[:, 0:256]))
            v.wait_ge(dS.h, v_axn_d[mm])
            v.wait_ge(tV.h, tV.v)
            if mm >= 1:
                v.wait_ge(tT.h, v_vmm_hist[0])
            xb_ = (mm % 2) * 2048
            for dt in range(8):
                inst = v.scalar_tensor_tensor(a_xn2[:, dt * 256:(dt + 1) * 256],
                                              a_xn[:, xb_ + dt * 256: xb_ + (dt + 1) * 256],
                                              c_nw[:, dt:dt + 1], a_rbc[:, :],
                                              OP.mult, OP.mult)
            tV.inc(inst)
        v_xn2d[mm] = tV.v
    v_vmm_hist = [0]

    @blk.sync
    def _(sy):
        wait_barrier(sy, bA)
        for dt in range(8):
            inst = sy.dma_start(
                out=a_xn[:, dt * 256:(dt + 1) * 256],
                in_=xn_tm.ap()[0:256, dt * 128:(dt + 1) * 128],
                transpose=True)
            dS.inc16(inst)
    v_axn_d[0] = dS.v

    for m in range(NCH):
        xb = (m % 2) * 2048

        @blk.sync
        def _(sy, m=m):
            if m + 1 < NCH:
                xb2 = ((m + 1) % 2) * 2048
                if m >= 1:
                    ca, cb = v_axn_cons[m - 1]
                    sy.wait_ge(tT.h, ca)
                    sy.wait_ge(tV.h, cb)
                for dt in range(8):
                    inst = sy.dma_start(
                        out=a_xn[:, xb2 + dt * 256: xb2 + (dt + 1) * 256],
                        in_=xn_tm.ap()[(m + 1) * 256:(m + 2) * 256, dt * 128:(dt + 1) * 128],
                        transpose=True)
                    dS.inc16(inst)
                v_axn_d[m + 1] = dS.v
        v_axn = v_axn_d[m]

        if m == 0:
            emit_head(0, None)
        v_rbc = v_rbcd[m]
        v_xn2 = v_xn2d[m]

        @blk.tensor
        def _(t, v_xn2=v_xn2, vp=v_prem_hist[0]):
            t.wait_ge(tV.h, v_xn2)
            if vp:
                t.wait_ge(tV.h, vp)
            for fs in range(4):
                for dt in range(8):
                    inst = t.matmul(p_a[:, fs * 256:(fs + 1) * 256],
                                    w["wq"][:, dt * 512 + fs * 128: dt * 512 + fs * 128 + 128],
                                    a_xn2[:, dt * 256:(dt + 1) * 256],
                                    start=(dt == 0), stop=(dt == 7))
            tT.inc(inst)
        v_qmm = tT.v

        @blk.vector
        def _(v, v_qmm=v_qmm):
            v.wait_ge(tT.h, v_qmm)
            tV.inc(v.tensor_scalar(a_el[:, :], p_a[:, :], 0.0, None, OP.min))
        v_min = tV.v

        @blk.scalar
        def _(s, m=m, v_min=v_min):
            if m == 0:
                wait_barrier(s, bA)
            s.wait_ge(tV.h, v_min)
            tA.inc(s.activation(a_el[:, :], a_el[:, :], AF.Exp))
        v_exp = tA.v

        @blk.vector
        def _(v, m=m, v_exp=v_exp):
            v.wait_ge(tA.h, v_exp)
            if m >= 1:
                v.wait_ge(dS.h, v_ch[m - 1][3])
            tV.inc(v.scalar_tensor_tensor(a_pq[:, :], p_a[:, :], 0.0, a_el[:, :],
                                          OP.max, OP.add))
        v_pq = tV.v

        @blk.tensor
        def _(t, v_xn2=v_xn2, vu=v_u_hist[0]):
            if vu:
                t.wait_ge(tV.h, vu)
            for fs in range(4):
                for dt in range(8):
                    inst = t.matmul(p_b[:, fs * 256:(fs + 1) * 256],
                                    w["wk"][:, dt * 512 + fs * 128: dt * 512 + fs * 128 + 128],
                                    a_xn2[:, dt * 256:(dt + 1) * 256],
                                    start=(dt == 0), stop=(dt == 7))
            tT.inc(inst)
        v_kmm = tT.v

        @blk.vector
        def _(v, v_kmm=v_kmm, v_pq=v_pq):
            v.wait_ge(tT.h, v_kmm)
            v.wait_ge(tV.h, v_pq)
            tV.inc(v.tensor_scalar(a_el[:, :], p_b[:, :], 0.0, None, OP.min))
        v_mink = tV.v

        @blk.scalar
        def _(s, v_mink=v_mink):
            s.wait_ge(tV.h, v_mink)
            tA.inc(s.activation(a_el[:, :], a_el[:, :], AF.Exp))
        v_expk = tA.v

        @blk.vector
        def _(v, v_expk=v_expk):
            v.wait_ge(tA.h, v_expk)
            tV.inc(v.scalar_tensor_tensor(a_pk[:, :], p_b[:, :], 0.0, a_el[:, :],
                                          OP.max, OP.add))
        v_pk = tV.v

        @blk.tensor
        def _(t, v_pq=v_pq, v_xn2=v_xn2):
            t.wait_ge(tV.h, v_pq)
            for fs in range(4):
                for dt in range(8):
                    inst = t.matmul(p_a[:, fs * 256:(fs + 1) * 256],
                                    w["wv"][:, dt * 512 + fs * 128: dt * 512 + fs * 128 + 128],
                                    a_xn2[:, dt * 256:(dt + 1) * 256],
                                    start=(dt == 0), stop=(dt == 7))
            tT.inc(inst)
        v_vmm = tT.v

        @blk.vector
        def _(v, v_vmm=v_vmm):
            v.wait_ge(tT.h, v_vmm)
            tV.inc(v.tensor_copy(a_vc[:, :], p_a[:, :]))
        v_vc = tV.v
        v_vmm_hist[0] = v_vmm
        if m + 1 < NCH:
            emit_head(m + 1, v_vc)

        tr_jobs = [("pk", fs, ct) for ct in range(2) for fs in range(4)] + \
                  [("vc", fs, ct) for ct in range(2) for fs in range(4)]
        v_trev = []
        for r in range(4):
            @blk.tensor
            def _(t, r=r, v_pk=v_pk, v_vc=v_vc):
                t.wait_ge(tV.h, v_pk if r < 2 else v_vc)
                if r == 2:
                    t.wait_ge(tV.h, v_vc)
                if r >= 1:
                    t.wait_ge(tV.h, v_trev[r - 1])
                for ji in range(4):
                    which, fs, ct = tr_jobs[r * 4 + ji]
                    src = a_pk if which == "pk" else a_vc
                    inst = t.transpose(p_t[:, ji * 128:(ji + 1) * 128],
                                       src[:, fs * 256 + ct * 128: fs * 256 + ct * 128 + 128],
                                       c_identb[:, :])
                tT.inc(inst)
            v_trm = tT.v

            @blk.vector
            def _(v, r=r, v_trm=v_trm):
                v.wait_ge(tT.h, v_trm)
                for ji in range(4):
                    which, fs, ct = tr_jobs[r * 4 + ji]
                    dst = a_pkT if which == "pk" else a_vcT
                    inst = v.tensor_copy(dst[:, ct * 512 + fs * 128: ct * 512 + fs * 128 + 128],
                                         p_t[:, ji * 128:(ji + 1) * 128])
                tV.inc(inst)
            v_trev.append(tV.v)
        v_tr = v_trev[-1]

        @blk.tensor
        def _(t, v_vc=v_vc, v_tr=v_tr):
            t.wait_ge(tV.h, v_vc)
            t.wait_ge(tV.h, v_tr)
            for h in range(8):
                po, j = hsl(h)
                for ct in range(2):
                    inst = t.matmul(p_a[po:po + 64, j * 256:(j + 1) * 256],
                                    a_pkT[:, ct * 512 + j * 128 + po: ct * 512 + j * 128 + po + 64],
                                    c_trik[:, ct * 256:(ct + 1) * 256],
                                    start=(ct == 0), stop=(ct == 1))
            tT.inc(inst)
        v_zmm = tT.v

        @blk.vector
        def _(v, v_zmm=v_zmm):
            v.wait_ge(tT.h, v_zmm)
            for j in range(4):
                inst = v.tensor_scalar(a_zbf[:, j * 256:(j + 1) * 256],
                                       p_a[:, j * 256:(j + 1) * 256],
                                       a_z[:, j:j + 1], None, OP.add)
            tV.inc(inst)
            v.wait_ge(tV.h, tV.v)
            tV.inc(v.memset(p_a[:, :], 0.0))
            v.wait_ge(tV.h, tV.v)
            tV.inc(v.tensor_mul(a_pt[:, :], a_pq[:, :], a_zbf[:, :]))
            v.wait_ge(tV.h, tV.v)
            for j in range(4):
                inst = v.reduce_sum(a_stat[:, j:j + 1], a_pk[:, j * 256:(j + 1) * 256], axis=AX)
            tV.inc(inst)
            v.wait_ge(tV.h, tV.v)
            tV.inc(v.tensor_add(a_z[:, 0:4], a_z[:, 0:4], a_stat[:, 0:4]))
        v_zup = tV.v

        @blk.tensor
        def _(t, v_zup=v_zup, v_rbc=v_rbc):
            t.wait_ge(tV.h, v_zup)
            for h in range(8):
                po, j = hsl(h)
                inst = t.matmul(p_b[po:po + 1, j * 256:(j + 1) * 256],
                                ones_col[po:po + 64, 0:1],
                                a_pt[po:po + 64, j * 256:(j + 1) * 256],
                                start=True, stop=True,
                                tile_position=(po, po))
            tT.inc(inst)
            t.wait_ge(tV.h, v_rbc)
            for h in range(8):
                po, j = hsl(h)
                for ct in range(2):
                    inst = t.matmul(p_tf[po:po + 64, j * 64:(j + 1) * 64],
                                    a_pkT[:, ct * 512 + j * 128 + po: ct * 512 + j * 128 + po + 64],
                                    a_vcT[:, ct * 512 + j * 128 + po: ct * 512 + j * 128 + po + 64],
                                    start=(ct == 0), stop=(ct == 1))
            tT.inc(inst)
        v_sch = tT.v
        v_den_mm = tT.v - 1

        @blk.vector
        def _(v, v_den_mm=v_den_mm):
            v.wait_ge(tT.h, v_den_mm)
            tV.inc(v.tensor_scalar(a_den[:, :], p_b[:, :], EPS_DEN, None, OP.add))
        v_den_ev = tV.v

        @blk.tensor
        def _(t, v_zup=v_zup):
            t.wait_ge(tV.h, v_zup)
            for h in range(8):
                po, j = hsl(h)
                inst = t.matmul(p_a[po:po + 64, j * 256:(j + 1) * 256],
                                a_scbf[po:po + 64, j * 64:(j + 1) * 64],
                                a_pq[po:po + 64, j * 256:(j + 1) * 256],
                                start=False, stop=False, skip_group_check=True)
            tT.inc(inst)
        v_inter = tT.v

        @blk.vector
        def _(v, m=m, v_sch=v_sch, v_inter=v_inter):
            v.wait_ge(tT.h, v_sch)
            v.wait_ge(tT.h, v_inter)
            tV.inc(v.tensor_add(a_scum[:, :], a_scum[:, :], p_tf[:, 0:256]))
            v.wait_ge(tV.h, tV.v)
            tV.inc(v.tensor_copy(a_scbf[:, :], a_scum[:, :]))
            v.wait_ge(tV.h, tV.v)
            for j in range(4):
                inst = v.tensor_copy(a_hist[:, (j * NCH + m) * 64:(j * NCH + m) * 64 + 64],
                                     a_scum[:, j * 64:(j + 1) * 64])
            tV.inc(inst)
        v_scup = tV.v

        v_at_ev = [None, None]
        horder = [3, 2, 1, 0, 7, 6, 5, 4]

        def emit_atm(h):
            po, j = hsl(h)
            reg = h % 2

            @blk.tensor
            def _(t, h=h, po=po, j=j, reg=reg, v_den_ev=v_den_ev):
                if v_at_ev[reg] is None:
                    t.wait_ge(tV.h, v_den_ev)
                else:
                    t.wait_ge(tV.h, v_at_ev[reg])
                for ds in range(2):
                    inst = t.matmul(p_b[:, reg * 512 + ds * 256: reg * 512 + (ds + 1) * 256],
                                    a_pk[po:po + 64, j * 256 + ds * 128: j * 256 + ds * 128 + 128],
                                    a_pq[po:po + 64, j * 256:(j + 1) * 256],
                                    start=True, stop=True)
                tT.inc(inst)
            v_atm = tT.v

            @blk.vector
            def _(v, reg=reg, v_atm=v_atm):
                v.wait_ge(tT.h, v_atm)
                tV.inc(v.tensor_mul(a_atm[:, reg * 512:(reg + 1) * 512],
                                    p_b[:, reg * 512:(reg + 1) * 512],
                                    c_trik[:, 0:512]))
            v_at_ev[reg] = tV.v

        def emit_vct(h):
            po, j = hsl(h)
            reg = h % 2

            @blk.tensor
            def _(t, h=h, po=po, j=j, reg=reg, vae=v_at_ev[reg]):
                t.wait_ge(tV.h, vae)
                for ds in range(2):
                    inst = t.matmul(p_a[po:po + 64, j * 256:(j + 1) * 256],
                                    a_vcT[:, ds * 512 + j * 128 + po: ds * 512 + j * 128 + po + 64],
                                    a_atm[:, reg * 512 + ds * 256: reg * 512 + (ds + 1) * 256],
                                    start=False, stop=(ds == 1), skip_group_check=True)
                tT.inc(inst)

        emit_atm(horder[0])
        for i in range(1, 8):
            emit_atm(horder[i])
            emit_vct(horder[i - 1])
        emit_vct(horder[7])
        v_intra = tT.v

        @blk.vector
        def _(v, v_den_ev=v_den_ev):
            v.wait_ge(tV.h, v_den_ev)
            tV.inc(v.reciprocal(a_den2[:, :], a_den[:, :]))
        v_den = tV.v

        @blk.tensor
        def _(t, v_den=v_den, v_scup=v_scup):
            t.wait_ge(tV.h, v_den)
            t.wait_ge(tV.h, v_scup)
            for h in range(8):
                po, j = hsl(h)
                pp, co = (p_tf, (j % 2) * 256) if h < 4 else (p_c, (j % 2) * 256)
                inst = t.matmul(pp[po:po + 64, co:co + 256],
                                ones_sq[po:po + 1, 0:64],
                                a_den2[po:po + 1, j * 256:(j + 1) * 256],
                                start=True, stop=True)
            tT.inc(inst)
        v_bcm = tT.v

        @blk.vector
        def _(v, v_bcm=v_bcm):
            v.wait_ge(tT.h, v_bcm)
            for h in range(8):
                po, j = hsl(h)
                pp, co = (p_tf, (j % 2) * 256) if h < 4 else (p_c, (j % 2) * 256)
                inst = v.tensor_copy(a_bc[po:po + 64, j * 256:(j + 1) * 256],
                                     pp[po:po + 64, co:co + 256])
            tV.inc(inst)
        v_bc = tV.v

        @blk.vector
        def _(v, m=m, v_intra=v_intra, v_bc=v_bc):
            v.wait_ge(tT.h, v_intra)
            v.wait_ge(tV.h, v_bc)
            if m >= 1:
                v.wait_ge(dS.h, v_ch[m - 1][3])
            if "num_d" in dbg_out:
                tV.inc(v.tensor_copy(a_den[:, :], p_a[:, :]))
            tV.inc(v.tensor_mul(a_pre[:, :], p_a[:, :], a_bc[:, :]))
        v_prem = tV.v
        if "num_d" in dbg_out:
            @blk.gpsimd
            def _(g, m=m, v_prem=v_prem):
                g.wait_ge(tV.h, v_prem)
                dG.inc16(g.dma_start(out=dbg_out["num_d"][:, m * 1024:(m + 1) * 1024],
                                     in_=a_den[:, :]))

        @blk.tensor
        def _(t, va0=v_at_ev[0], va1=v_at_ev[1], v_axn=v_axn, xb=xb):
            t.wait_ge(tV.h, va0)
            t.wait_ge(tV.h, va1)
            for fs in range(4):
                for dt in range(8):
                    inst = t.matmul(p_b[:, fs * 256:(fs + 1) * 256],
                                    w["wu"][:, dt * 512 + fs * 128: dt * 512 + fs * 128 + 128],
                                    a_xn[:, xb + dt * 256: xb + (dt + 1) * 256],
                                    start=(dt == 0), stop=(dt == 7))
            tT.inc(inst)
        v_umm = tT.v

        @blk.vector
        def _(v, m=m, v_umm=v_umm, xb=xb):
            v.wait_ge(tT.h, v_umm)
            tV.inc(v.tensor_copy(a_u[:, :], p_b[:, :]))
            for j in range(4):
                inst = v.reduce_sum(a_ms[:, j * 32 + m: j * 32 + m + 1],
                                    a_xn[:, xb + j * 256: xb + (j + 1) * 256], axis=AX)
            tV.inc(inst)
        v_u = tV.v

        @blk.tensor
        def _(t, m=m, v_u=v_u, v_bc=v_bc, xb=xb):
            t.wait_ge(tV.h, v_u)
            t.wait_ge(tV.h, v_bc)
            if m > 0:
                for cs in range(2):
                    for fs in range(4):
                        inst = t.matmul(p_tf[0:128, cs * 32: cs * 32 + m],
                                        a_u[:, fs * 256 + cs * 128: fs * 256 + cs * 128 + 128],
                                        a_ms[:, fs * 32: fs * 32 + m],
                                        start=(fs == 0), stop=(fs == 3))
                tT.inc(inst)
            for cs in range(2):
                for fs in range(4):
                    inst = t.matmul(p_d[:, cs * 256:(cs + 1) * 256],
                                    a_u[:, fs * 256 + cs * 128: fs * 256 + cs * 128 + 128],
                                    a_xn[:, xb + fs * 256: xb + (fs + 1) * 256],
                                    start=(fs == 0), stop=(fs == 3))
            tT.inc(inst)
        v_wmm = tT.v

        @blk.vector
        def _(v, m=m, v_wmm=v_wmm):
            if m >= 1:
                v.wait_ge(dS.h, v_ch[m - 1][4])
            v.memset(a_scst[:, :], NEG_PAD)
            v.wait_ge(tT.h, v_wmm)
            if m > 0:
                for cs in range(2):
                    inst = v.tensor_scalar(a_scst[:, cs * 33: cs * 33 + m],
                                           p_tf[:, cs * 32: cs * 32 + m],
                                           1.0 / 8192.0, None, OP.mult)
                tV.inc(inst)
            tV.inc(v.tensor_mul(a_wm[:, :], p_d[:, :], c_triw[:, 0:512]))
            v.wait_ge(tV.h, tV.v)
            for cs in range(2):
                inst = v.reduce_sum(a_stat[:, 4 + cs:5 + cs],
                                    a_wm[:, cs * 256:(cs + 1) * 256], axis=AX)
            tV.inc(inst)
            v.wait_ge(tV.h, tV.v)
            for cs in range(2):
                inst = v.tensor_scalar(a_scst[:, cs * 33 + 32: cs * 33 + 33],
                                       a_stat[:, 4 + cs:5 + cs],
                                       c_cnt[:, cs:cs + 1], 1.0 / 32.0, OP.mult, OP.mult)
            tV.inc(inst)
        v_sc = tV.v

        @blk.sync
        def _(sy, m=m, v_prem=v_prem, v_pq=v_pq):
            sy.wait_ge(tV.h, v_prem)
            dS.inc16(sy.dma_start(
                out=pre_in.ap()[:, m * 256:(m + 1) * 256].rearrange("(j p) c -> p j c", p=128),
                in_=a_pre[:, :].rearrange("p (j c) -> p j c", j=4)))
            sy.wait_ge(tV.h, v_pq)
            dS.inc16(sy.dma_start(
                out=pqc.ap()[m * 4 * 128:(m * 4 + 4) * 128, :].rearrange("(j p) c -> p j c", p=128),
                in_=a_pq[:, :].rearrange("p (j c) -> p j c", j=4)))
        v_predma = dS.v

        @blk.sync
        def _(sy, m=m, v_sc=v_sc):
            sy.wait_ge(tV.h, v_sc)
            dS.inc16(sy.dma_start(
                out=sc_in.ap()[m * 256:(m + 1) * 256, :].rearrange("(cs p) n -> p cs n", p=128),
                in_=a_scst[:, :].rearrange("p (cs n) -> p cs n", cs=2)))
        v_scdma = dS.v

        if "pre_d" in dbg_out:
            @blk.gpsimd
            def _(g, m=m, v_prem=v_prem):
                g.wait_ge(tV.h, v_prem)
                dG.inc16(g.dma_start(
                    out=dbg_out["pre_d"][:, m * 256:(m + 1) * 256].rearrange("(j p) c -> p j c", p=128),
                    in_=a_pre[:, :].rearrange("p (j c) -> p j c", j=4)))
        if "bc_d" in dbg_out:
            @blk.gpsimd
            def _(g, m=m, v_bc=v_bc):
                g.wait_ge(tV.h, v_bc)
                dG.inc16(g.dma_start(out=dbg_out["bc_d"][:, m * 1024:(m + 1) * 1024],
                                     in_=a_bc[:, :]))
        if "den_d" in dbg_out:
            @blk.gpsimd
            def _(g, m=m, v_den=v_den, v_prem=v_prem):
                g.wait_ge(tV.h, v_den)
                dG.inc16(g.dma_start(out=dbg_out["den_d"][:, m * 1024:(m + 1) * 1024],
                                     in_=a_den2[:, :]))
        if "zz_d" in dbg_out:
            @blk.gpsimd
            def _(g, m=m, v_zup=v_zup):
                g.wait_ge(tV.h, v_zup)
                dG.inc16(g.dma_start(out=dbg_out["zz_d"][:, m * 1024:(m + 1) * 1024],
                                     in_=a_zbf[:, :]))
        if "atm_d" in dbg_out:
            @blk.gpsimd
            def _(g, m=m, v_intra=v_intra):
                g.wait_ge(tT.h, v_intra)
                dG.inc16(g.dma_start(out=dbg_out["atm_d"][:, m * 1024:(m + 1) * 1024],
                                     in_=a_atm[:, :]))

        v_prem_hist[0] = v_prem
        v_u_hist[0] = v_u
        v_axn_cons[m] = (v_wmm, v_u)
        v_ch[m] = (tT.v, tA.v, tV.v, v_predma, v_scdma)

    if "hist_d" in dbg_out:
        @blk.gpsimd
        def _(g):
            g.wait_ge(tV.h, tV.v)
            dG.inc16(g.dma_start(out=dbg_out["hist_d"], in_=a_hist[:, :]))
    if "pq_d" in dbg_out:
        @blk.gpsimd
        def _(g):
            g.wait_ge(dS.h, dS.v)
            dG.inc16(g.dma_start(out=dbg_out["pq_d"], in_=pqc.ap()[:, :]))
    if "sc_d" in dbg_out:
        @blk.gpsimd
        def _(g):
            g.wait_ge(dS.h, dS.v)
            dG.inc16(g.dma_start(out=dbg_out["sc_d"], in_=sc_in.ap()[:, :]))

    bB = barrier()
    ctxB.close()

    @blk.gpsimd
    def _(g):
        g.wait_ge(dS.h, bB[3])
        cc.inc(g.collective_compute(
            "AllGather", OP.bypass,
            replica_groups=[[0, 1], [2, 3], [4, 5], [6, 7]],
            ins=[pre_in.ap().opt()], outs=[pre_out.ap().opt()]))
        cc.inc(g.collective_compute(
            "AllReduce", OP.add,
            replica_groups=[[0, 1], [2, 3], [4, 5], [6, 7]],
            ins=[sc_in.ap().opt()], outs=[sc_out.ap().opt()]))
    v_cc2 = cc.v

    ctxC = contextlib.ExitStack()

    def sbC(name, shape, dtype):
        return ctxC.enter_context(nc.sbuf_tensor(name, shape, dtype))

    f_k = sbC("f_k", [128, T], BF16)
    f_v = sbC("f_v", [128, 64 * 130], BF16)
    f_q = sbC("f_q", [128, NTC * 512], BF16)
    f_nq = sbC("f_nq", [1, 16], F32)
    f_b2 = sbC("f_b2", [1, 16], BF16)
    f_bcall = sbC("f_bcall", [128, 16], F32)
    f_q2 = sbC("f_q2", [128, 512], BF16)
    f_nk = sbC("f_nk", [1, 16], F32)
    f_rb = sbC("f_rb", [65, 2048], BF16)
    f_dbc = sbC("f_dbc", [64, 512], BF16)
    f_ctx = sbC("f_ctx", [128, 512], BF16)
    f_cx = sbC("f_cx", [128, 2048], F32)
    f_pt = sbC("f_pt", [128, 2 * 1024], BF16)

    qk_v, exp_v, pv_v = {}, {}, {}
    ki = 0
    v_ctxev = None
    v_ctxdma = [0]
    v_B_ctd = [0]
    prev_B = None
    pair_done = None

    for pr in range(4):
        @blk.sync
        def _(sy, pr=pr, pd=pair_done):
            if pr == 0:
                wait_barrier(sy, bA)
                wait_barrier(sy, bB)
            else:
                sy.wait_ge(tT.h, pd[0])
                sy.wait_ge(tV.h, pd[1])
            dS.inc16(sy.dma_start(
                out=f_k[:, :].rearrange("p (tc c) -> p tc c", tc=NTC),
                in_=kf.ap()[pr * NTC * 128:(pr + 1) * NTC * 128, :]
                    .rearrange("(tc p) c -> p tc c", p=128)))
            dS.inc16(sy.dma_start(
                out=f_v[:, :].rearrange("p (kt s) -> p kt s", s=130)[:, :, 0:64],
                in_=vtok.ap()[pr * 64 * 128:(pr + 1) * 64 * 128, :]
                    .rearrange("(kt p) f -> p kt f", p=128)[:, :, 0:64]))
            dS.inc16(sy.dma_start(
                out=f_v[:, :].rearrange("p (kt s) -> p kt s", s=130)[:, :, 65:129],
                in_=vtok.ap()[pr * 64 * 128:(pr + 1) * 64 * 128, :]
                    .rearrange("(kt p) f -> p kt f", p=128)[:, :, 64:128]))
            dS.inc16(sy.dma_start(
                out=f_q[:, :].rearrange("p (tc c) -> p tc c", tc=NTC),
                in_=qf.ap()[pr * NTC * 128:(pr + 1) * NTC * 128, :]
                    .rearrange("(tc p) c -> p tc c", p=128)))
        v_kv = dS.v

        @blk.vector
        def _(v, pr=pr, v_kv=v_kv):
            if pr == 0:
                wait_barrier(v, bB)
            v.wait_ge(dS.h, v_kv)
            v.memset(f_v[:, :].rearrange("p (kt s) -> p kt s", s=130)[:, :, 64:65], 1.0)
            tV.inc(v.memset(f_v[:, :].rearrange("p (kt s) -> p kt s", s=130)[:, :, 129:130], 1.0))
        v_vones = tV.v

        v_nkmm = [None]
        v_nkred = [None]
        for tc in range(NTC):
            @blk.vector
            def _(v, pr=pr, tc=tc, v_kv=v_kv, pd=pair_done):
                v.wait_ge(dS.h, v_kv)
                if tc >= 1:
                    v.wait_ge(tT.h, v_nkmm[0])
                elif pd is not None:
                    v.wait_ge(tT.h, pd[0])
                tV.inc(v.tensor_mul(f_q2[:, :], f_k[:, tc * 512:(tc + 1) * 512],
                                    f_k[:, tc * 512:(tc + 1) * 512]))
            v_k2 = tV.v

            @blk.tensor
            def _(t, pr=pr, tc=tc, v_k2=v_k2, pd=pair_done):
                if pr == 0 and tc == 0:
                    wait_barrier(t, bB)
                t.wait_ge(tV.h, v_k2)
                if tc >= 1:
                    t.wait_ge(tV.h, v_nkred[0])
                elif pr > 0:
                    t.wait_ge(tV.h, pd[1])
                tT.inc(t.matmul(p_tf[0:1, 0:512], ones_col[:, :], f_q2[:, :],
                                start=True, stop=True))
            v_nkmm[0] = tT.v

            @blk.vector
            def _(v, tc=tc, vm=v_nkmm[0]):
                v.wait_ge(tT.h, vm)
                tV.inc(v.reduce_max(f_nk[0:1, tc:tc + 1], p_tf[0:1, 0:512], axis=AX))
            v_nkred[0] = tV.v

        @blk.vector
        def _(v):
            v.wait_ge(tV.h, tV.v)
            tV.inc(v.reduce_max(a_stat[0:1, 4:5], f_nk[0:1, 0:16], axis=AX))
        v_bk2 = tV.v

        v_nqs = {}
        for gi2 in range(NTC):
            @blk.vector
            def _(v, gi2=gi2, v_kv=v_kv):
                v.wait_ge(dS.h, v_kv)
                if gi2 >= 1:
                    v.wait_ge(tT.h, v_nqs[gi2 - 1])
                tV.inc(v.tensor_mul(f_q2[:, :], f_q[:, gi2 * 512:(gi2 + 1) * 512],
                                    f_q[:, gi2 * 512:(gi2 + 1) * 512]))
            v_q2m = tV.v

            @blk.tensor
            def _(t, gi2=gi2, v_q2m=v_q2m):
                t.wait_ge(tV.h, v_q2m)
                if gi2 >= 1:
                    t.wait_ge(tV.h, v_nqr[0])
                tT.inc(t.matmul(p_tf[0:1, 0:512], ones_col[:, :], f_q2[:, :],
                                start=True, stop=True))
            v_nqs[gi2] = tT.v

            @blk.vector
            def _(v, gi2=gi2, vm=v_nqs[gi2]):
                v.wait_ge(tT.h, vm)
                tV.inc(v.reduce_max(f_nq[0:1, gi2:gi2 + 1], p_tf[0:1, 0:512], axis=AX))
            v_nqr = [tV.v]

        @blk.vector
        def _(v, v_bk2=v_bk2):
            v.wait_ge(tV.h, tV.v)
            v.wait_ge(tV.h, v_bk2)
            tV.inc(v.tensor_scalar(a_stat[0:1, 5:6], a_stat[0:1, 4:5], -1.0 / 16.0,
                                   None, OP.mult))
            v.wait_ge(tV.h, tV.v)
            tV.inc(v.tensor_scalar(f_b2[0:1, :], f_nq[0:1, :], -4.0,
                                   a_stat[0:1, 5:6], OP.mult, OP.add))
        v_brow = tV.v

        @blk.tensor
        def _(t, v_brow=v_brow):
            t.wait_ge(tV.h, v_brow)
            tT.inc(t.matmul(p_tf[:, 256:272], ones_row[:, :], f_b2[0:1, :],
                            start=True, stop=True))
        v_bcm = tT.v

        @blk.vector
        def _(v, v_bcm=v_bcm, va=tA.v):
            v.wait_ge(tT.h, v_bcm)
            if va:
                v.wait_ge(tA.h, va)
            tV.inc(v.tensor_copy(f_bcall[:, :], p_tf[:, 256:272]))
        v_bcall = tV.v

        v_bcf = [None]
        exp_base_pair = tA.v
        ki_pair0 = ki
        for kt2 in range(NTC * 64):
            exp_v[ki_pair0 + kt2] = exp_base_pair + kt2 + 1
        for gi in range(NTC):
            ga = gi

            ki0 = ki

            @blk.tensor
            def _(t, ga=ga, ki0=ki0, vb=v_bcall, vce=v_ctxev, vvo=v_vones):
                for kt in range(64):
                    ki_ = ki0 + kt
                    pi = ki_ % 2
                    P = p_a if pi == 0 else p_b
                    if kt == 0:
                        t.wait_ge(tV.h, vb)
                    if ki_ >= 2:
                        t.wait_ge(tA.h, exp_v[ki_ - 2])
                    t.matmul(P[:, 0:512], f_k[0:64, kt * 128:(kt + 1) * 128],
                             f_q[0:64, ga * 512:(ga + 1) * 512], start=True, stop=True,
                             tile_position=(0, 0))
                    tT.inc(t.matmul(P[:, 512:1024], f_k[64:128, kt * 128:(kt + 1) * 128],
                                    f_q[64:128, ga * 512:(ga + 1) * 512], start=True,
                                    stop=True, tile_position=(64, 0)))
                    qk_v[ki_] = tT.v
                    if kt >= 1:
                        kp = ki0 + kt - 1
                        pp = kp % 2
                        t.wait_ge(tA.h, exp_v[kp])
                        if kt == 1:
                            t.wait_ge(tV.h, vvo)
                            if vce is not None:
                                t.wait_ge(tV.h, vce)
                        t.matmul(p_c[0:65, :], f_v[:, (kt - 1) * 130: (kt - 1) * 130 + 65],
                                 f_pt[:, pp * 1024: pp * 1024 + 512],
                                 start=(kt == 1), stop=False)
                        tT.inc(t.matmul(p_d[0:65, :],
                                        f_v[:, (kt - 1) * 130 + 65: (kt - 1) * 130 + 130],
                                        f_pt[:, pp * 1024 + 512:(pp + 1) * 1024],
                                        start=(kt == 1), stop=False))
                        pv_v[kp] = tT.v
                kp = ki0 + 63
                pp = kp % 2
                t.wait_ge(tA.h, exp_v[kp])
                t.matmul(p_c[0:65, :], f_v[:, 63 * 130: 63 * 130 + 65],
                         f_pt[:, pp * 1024: pp * 1024 + 512], start=False, stop=True)
                tT.inc(t.matmul(p_d[0:65, :], f_v[:, 63 * 130 + 65: 63 * 130 + 130],
                                f_pt[:, pp * 1024 + 512:(pp + 1) * 1024],
                                start=False, stop=True))
                pv_v[kp] = tT.v

            ki += 64

            pg = gi % 2
            @blk.vector
            def _(v, kk=ki, pg=pg):
                v.wait_ge(tT.h, pv_v[kk - 1])
                v.tensor_copy(f_cx[0:65, pg * 1024: pg * 1024 + 512], p_c[0:65, :])
                tV.inc(v.tensor_copy(f_cx[0:65, pg * 1024 + 512:(pg + 1) * 1024],
                                     p_d[0:65, :]))
            v_ctxev = tV.v

            @blk.vector
            def _(v, pg=pg, ve=v_ctxev):
                v.wait_ge(tV.h, ve)
                tV.inc(v.reciprocal(f_rb[0:1, pg * 1024: pg * 1024 + 512],
                                    f_cx[64:65, pg * 1024: pg * 1024 + 512]))
                tV.inc(v.reciprocal(f_rb[64:65, pg * 1024 + 512:(pg + 1) * 1024],
                                    f_cx[64:65, pg * 1024 + 512:(pg + 1) * 1024]))
            v_rp1, v_rp2 = tV.v - 1, tV.v

            if prev_B is not None:
                prev_B()

            def mk_B(pr=pr, gi=gi, pg=pg, v_rp1=v_rp1, v_rp2=v_rp2):
                def emit():
                    vprev = v_B_ctd[0]

                    @blk.tensor
                    def _(t):
                        t.wait_ge(tV.h, v_rp1)
                        if vprev:
                            t.wait_ge(tV.h, vprev)
                        tT.inc(t.matmul(p_tf[0:64, 0:512], ones_row[0:1, 0:64],
                                        f_rb[0:1, pg * 1024: pg * 1024 + 512],
                                        start=True, stop=True))
                    v_db1 = tT.v

                    @blk.tensor
                    def _(t):
                        t.wait_ge(tV.h, v_rp2)
                        tT.inc(t.matmul(p_tf[64:128, 0:512], ones_r2[64:65, 0:64],
                                        f_rb[64:65, pg * 1024 + 512:(pg + 1) * 1024],
                                        start=True, stop=True))
                    v_db2 = tT.v

                    @blk.vector
                    def _(v, vcd=v_ctxdma[0]):
                        v.wait_ge(tT.h, v_db1)
                        if vcd:
                            v.wait_ge(dS.h, vcd)
                        tV.inc(v.tensor_copy(f_dbc[:, :], p_tf[0:64, 0:512]))
                        v.wait_ge(tV.h, tV.v)
                        tV.inc(v.tensor_mul(f_ctx[0:64, :],
                                            f_cx[0:64, pg * 1024: pg * 1024 + 512],
                                            f_dbc[:, :]))
                        v.wait_ge(tT.h, v_db2)
                        v.wait_ge(tV.h, tV.v)
                        tV.inc(v.tensor_copy(f_dbc[:, :], p_tf[64:128, 0:512]))
                        v.wait_ge(tV.h, tV.v)
                        tV.inc(v.tensor_mul(f_ctx[64:128, :],
                                            f_cx[0:64, pg * 1024 + 512:(pg + 1) * 1024],
                                            f_dbc[:, :]))
                    v_ctd = tV.v
                    v_B_ctd[0] = v_ctd

                    @blk.sync
                    def _(sy):
                        sy.wait_ge(tV.h, v_ctd)
                        dS.inc16(sy.dma_start(
                            out=ctx_in.ap()[pr * 128:(pr + 1) * 128,
                                            gi * 512:(gi + 1) * 512],
                            in_=f_ctx[:, :]))
                    v_ctxdma[0] = dS.v
                return emit
            prev_B = mk_B()
        prev_B()
        prev_B = None
        @blk.scalar
        def _(s, pr=pr, ki_pair0=ki_pair0):
            if pr == 0:
                wait_barrier(s, bB)
            for kt2 in range(NTC * 64):
                ki_ = ki_pair0 + kt2
                gi2 = kt2 // 64
                pi = ki_ % 2
                P = p_a if pi == 0 else p_b
                s.wait_ge(tT.h, qk_v[ki_])
                if ki_ >= 2:
                    assert pv_v[ki_ - 2] <= qk_v[ki_]
                inst = s.activation(f_pt[:, pi * 1024:(pi + 1) * 1024], P[:, :],
                                    AF.Exp, bias=f_bcall[:, gi2:gi2 + 1])
                tA.inc(inst)
                assert tA.v == exp_v[ki_]
        pair_done = (tT.v, tV.v)

    if "ctx_d" in dbg_out:
        @blk.gpsimd
        def _(g):
            g.wait_ge(dS.h, dS.v)
            dG.inc16(g.dma_start(out=dbg_out["ctx_d"], in_=ctx_in.ap()[:, :]))

    bC = barrier()
    ctxC.close()

    @blk.gpsimd
    def _(g):
        g.wait_ge(dS.h, bC[3])
        cc.inc(g.collective_compute(
            "AllGather", OP.bypass,
            replica_groups=[[0, 1], [2, 3], [4, 5], [6, 7]],
            ins=[ctx_in.ap().opt()], outs=[ctx_out.ap().opt()]))
    v_cc3 = cc.v

    ctxD = contextlib.ExitStack()

    def sbD(name, shape, dtype):
        return ctxD.enter_context(nc.sbuf_tensor(name, shape, dtype))

    g_scs = sbD("g_scs", [128, 64 * 33], F32)
    g_exp = sbD("g_exp", [128, 2 * 33], F32)
    g_mx = sbD("g_mx", [128, 64], F32)
    g_gst = sbD("g_gst", [128, 64 * 33], BF16)
    g_rt = sbD("g_rt", [33, 256], BF16)
    c_pre = sbD("c_pre", [128, 2 * 8 * 256], BF16)
    c_xnf = sbD("c_xnf", [128, 2 * 4 * 256], BF16)
    c_pq = sbD("c_pq", [128, 2 * 4 * 256], BF16)
    c_gbc = sbD("c_gbc", [128, 256], BF16)
    c_spq = sbD("c_spq", [128, 2 * 4 * 256], BF16)
    c_xo = sbD("c_xo", [128, 4 * 256], F32)
    c_s = sbD("c_s", [128, 4 * 256], F32)
    c_ctx = sbD("c_ctx", [128, 2 * 8 * 256], BF16)
    c_xvo = sbD("c_xvo", [128, 256], BF16)
    c_g = sbD("c_g", [128, 4 * 256], F32)
    c_xr = sbD("c_xr", [128, 2 * 2 * 512], F32)
    c_out = sbD("c_out", [128, 2 * 512], F32)

    @blk.sync
    def _(sy):
        wait_barrier(sy, bC)
        sy.wait_ge(cc.h, v_cc2)
        dS.inc16(sy.dma_start(out=g_scs[:, :].rearrange("p (a n) -> p a n", a=NT),
                              in_=sc_out.ap().rearrange("(a p) n -> p a n", p=128)))
    v_scg = dS.v

    v_gt_done = {}
    @blk.vector
    def _(v):
        wait_barrier(v, bC)
        v.wait_ge(dS.h, v_scg)
        for a in range(NT):
            inst = v.reduce_max(g_mx[:, a:a + 1], g_scs[:, a * 33:(a + 1) * 33],
                                axis=AX, negate=True)
        tV.inc(inst)
    v_mxall = tV.v
    dve_base = tV.v
    exp_base = tA.v
    gexp_v = {a: exp_base + a + 1 for a in range(NT)}
    gdve_v = {a: dve_base + 3 * (a + 1) for a in range(NT)}

    @blk.scalar
    def _(s):
        wait_barrier(s, bC)
        s.wait_ge(tV.h, v_mxall)
        for a in range(NT):
            if a >= 2:
                s.wait_ge(tV.h, gdve_v[a - 2])
            inst = s.activation(g_exp[:, (a % 2) * 33:(a % 2) * 33 + 33],
                                g_scs[:, a * 33:(a + 1) * 33], AF.Exp,
                                bias=g_mx[:, a:a + 1])
            tA.inc(inst)
            assert tA.v == gexp_v[a]

    @blk.vector
    def _(v):
        for a in range(NT):
            v.wait_ge(tA.h, gexp_v[a])
            sl = (a % 2) * 33
            tV.inc(v.reduce_sum(st_sm[:, 1:2], g_exp[:, sl:sl + 33], axis=AX))
            v.wait_ge(tV.h, tV.v)
            tV.inc(v.reciprocal(st_sm[:, 2:3], st_sm[:, 1:2]))
            v.wait_ge(tV.h, tV.v)
            tV.inc(v.tensor_scalar(g_gst[:, a * 33:(a + 1) * 33], g_exp[:, sl:sl + 33],
                                   st_sm[:, 2:3], None, OP.mult))
            assert tV.v == gdve_v[a]
            v_gt_done[a] = tV.v

    if "gates_d" in dbg_out:
        @blk.gpsimd
        def _(g):
            g.wait_ge(tV.h, tV.v)
            dG.inc16(g.dma_start(out=dbg_out["gates_d"], in_=g_gst[:, :]))

    v_cm = {}
    v_outdma = [0, 0]
    fin = {}
    v_ldm, v_ldcm = {}, {}

    def emit_ph6_loads(mm, first):
        pm = mm % 2

        @blk.sync
        def _(sy, mm=mm, pm=pm, first=first):
            if first:
                wait_barrier(sy, bC)
                sy.wait_ge(cc.h, v_cc2)
            elif mm >= 2:
                sy.wait_ge(tT.h, fin["wo"])
                sy.wait_ge(tV.h, fin["xo"])
                sy.wait_ge(tV.h, fin["spq"])
                sy.wait_ge(tV.h, fin["out"])
            for dt in range(8):
                dS.inc16(sy.dma_start(
                    out=c_pre[:, pm * 2048 + dt * 256: pm * 2048 + (dt + 1) * 256],
                    in_=pre_out.ap()[dt * 128:(dt + 1) * 128, mm * 256:(mm + 1) * 256]))
            for dt in range(4):
                dS.inc16(sy.dma_start(
                    out=c_xnf[:, pm * 1024 + dt * 256: pm * 1024 + (dt + 1) * 256],
                    in_=xn_tm.ap()[mm * 256:(mm + 1) * 256, dt * 128:(dt + 1) * 128],
                    transpose=True))
            dS.inc16(sy.dma_start(
                out=c_pq[:, pm * 1024:(pm + 1) * 1024].rearrange("p (j c) -> p j c", j=4),
                in_=pqc.ap()[mm * 4 * 128:(mm * 4 + 4) * 128, :]
                    .rearrange("(j p) c -> p j c", p=128)))
            for cs in range(2):
                dS.inc16(sy.dma_start(
                    out=c_xr[:, pm * 1024 + cs * 512: pm * 1024 + (cs + 1) * 512],
                    in_=x[mm * 256 + cs * 128: mm * 256 + (cs + 1) * 128, 0:512]))
        v_ldm[mm] = dS.v

        @blk.sync
        def _(sy, mm=mm, pm=pm, first=first):
            sy.wait_ge(cc.h, v_cc3)
            if not first and mm >= 2:
                sy.wait_ge(tT.h, fin["omm"])
            for dt in range(8):
                dS.inc16(sy.dma_start(
                    out=c_ctx[:, pm * 2048 + dt * 256: pm * 2048 + (dt + 1) * 256],
                    in_=ctx_out.ap()[dt * 128:(dt + 1) * 128, mm * 256:(mm + 1) * 256]))
        v_ldcm[mm] = dS.v

    emit_ph6_loads(0, True)
    for m in range(NCH):
        pm = m % 2
        if m + 1 < NCH:
            emit_ph6_loads(m + 1, False)
        v_ld = v_ldm[m]
        v_ldc = v_ldcm[m]

        @blk.tensor
        def _(t, m=m):
            if m == 0:
                wait_barrier(t, bC)
            t.wait_ge(tV.h, v_gt_done[2 * m + 1])
            if m >= 1:
                t.wait_ge(tV.h, fin["grt_tv"])
            for tt in range(2):
                inst = t.transpose(p_t[0:33, tt * 128:(tt + 1) * 128],
                                   g_gst[:, (2 * m + tt) * 33:(2 * m + tt + 1) * 33],
                                   c_identb[:, :])
            tT.inc(inst)
        v_gtm = tT.v

        @blk.vector
        def _(v, m=m, v_gtm=v_gtm):
            if m == 0:
                wait_barrier(v, bC)
            v.wait_ge(tT.h, v_gtm)
            if m >= 1:
                v.wait_ge(tT.h, fin["grt"])
            tV.inc(v.tensor_copy(g_rt[:, :], p_t[0:33, 0:256]))
            if m > 0:
                v.wait_ge(tV.h, tV.v)
                if m >= 1 and "g" in fin:
                    v.wait_ge(tA.h, fin["g"])
                tV.inc(v.memset(p_b[:, :], 0.0))
        v_grt = tV.v
        v_pbz = tV.v
        fin["grt_tv"] = v_grt

        @blk.tensor
        def _(t, m=m, v_ld=v_ld, pm=pm):
            t.wait_ge(dS.h, v_ld)
            if m >= 1:
                t.wait_ge(tV.h, fin["xo"])
            for fs in range(4):
                for dt in range(8):
                    inst = t.matmul(p_a[:, fs * 256:(fs + 1) * 256],
                                    w["wo"][:, dt * 512 + fs * 128: dt * 512 + fs * 128 + 128],
                                    c_pre[:, pm * 2048 + dt * 256: pm * 2048 + (dt + 1) * 256],
                                    start=(dt == 0), stop=(dt == 7))
            tT.inc(inst)
        v_wo = tT.v
        fin["wo"] = v_wo

        v_hr = None
        v_gbcev = {}
        v_selv = {}
        v_hrv = {}
        if m == 0:
            fin["spq"] = tV.v
        for n in range(m):
            sel = p_c if n % 2 == 0 else p_d
            sb = (n % 2) * 1024

            @blk.tensor
            def _(t, m=m, n=n, sel=sel, v_grt=v_grt, vz=v_pbz):
                t.wait_ge(tV.h, v_grt)
                if n >= 2:
                    t.wait_ge(tV.h, v_gbcev[n - 2])
                tT.inc(t.matmul(sel[:, 0:256], c_e33[:, n * 128:(n + 1) * 128],
                                g_rt[:, :], start=True, stop=True))
                v_selv[n] = tT.v
                if n >= 1:
                    nn = n - 1
                    t.wait_ge(tV.h, v_gbcev[nn])
                    if nn == 0:
                        t.wait_ge(tV.h, vz)
                    for h in range(8):
                        po, j = hsl(h)
                        inst = t.matmul(p_b[po:po + 64, j * 256:(j + 1) * 256],
                                        a_hist[po:po + 64, (j * NCH + nn) * 64:(j * NCH + nn) * 64 + 64],
                                        c_spq[po:po + 64, (nn % 2) * 1024 + j * 256:
                                              (nn % 2) * 1024 + (j + 1) * 256],
                                        start=False, stop=False, skip_group_check=True)
                    tT.inc(inst)
                    v_hrv[nn] = tT.v

            @blk.vector
            def _(v, m=m, n=n, sel=sel, vs=v_selv[n], pm=pm, sb=sb):
                v.wait_ge(tT.h, vs)
                if n >= 2:
                    v.wait_ge(tT.h, v_hrv[n - 2])
                for j in range(4):
                    inst = v.tensor_mul(c_spq[:, sb + j * 256: sb + (j + 1) * 256],
                                        c_pq[:, pm * 1024 + j * 256: pm * 1024 + (j + 1) * 256],
                                        sel[:, 0:256])
                tV.inc(inst)
            v_gbcev[n] = tV.v
            fin["spq"] = tV.v

        if m > 0:
            @blk.tensor
            def _(t, m=m, vg=v_gbcev[m - 1], vz=v_pbz):
                t.wait_ge(tV.h, vg)
                if m == 1:
                    t.wait_ge(tV.h, vz)
                for h in range(8):
                    po, j = hsl(h)
                    inst = t.matmul(p_b[po:po + 64, j * 256:(j + 1) * 256],
                                    a_hist[po:po + 64, ((j * NCH) + m - 1) * 64:((j * NCH) + m - 1) * 64 + 64],
                                    c_spq[po:po + 64, ((m - 1) % 2) * 1024 + j * 256:
                                          ((m - 1) % 2) * 1024 + (j + 1) * 256],
                                    start=False, stop=True, skip_group_check=True)
                tT.inc(inst)
            v_hr = tT.v

        @blk.vector
        def _(v, m=m, v_wo=v_wo, v_ld=v_ld, pm=pm):
            v.wait_ge(tT.h, v_wo)
            v.wait_ge(dS.h, v_ld)
            tV.inc(v.tensor_add(c_xo[:, :], p_a[:, :],
                                c_xnf[:, pm * 1024:(pm + 1) * 1024]))
        v_xo = tV.v
        fin["xo"] = v_xo

        @blk.tensor
        def _(t, m=m, v_grt=v_grt, v_xo=v_xo):
            t.wait_ge(tV.h, v_grt)
            if m >= 2:
                t.wait_ge(tV.h, v_gbcev.get(m - 2, v_xo))
            sel = p_c if m % 2 == 0 else p_d
            tT.inc(t.matmul(sel[:, 0:256], c_e33[:, 32 * 128:33 * 128], g_rt[:, :],
                            start=True, stop=True))
        v_gcm = tT.v
        fin["grt"] = v_gcm

        @blk.vector
        def _(v, m=m, v_gcm=v_gcm, v_xo=v_xo, v_hr=v_hr):
            v.wait_ge(tT.h, v_gcm)
            if m >= 1:
                v.wait_ge(tT.h, fin["sgtr"])
            sel = p_c if m % 2 == 0 else p_d
            tV.inc(v.tensor_copy(c_gbc[:, :], sel[:, 0:256]))
            v.wait_ge(tV.h, tV.v)
            v.wait_ge(tV.h, v_xo)
            for j in range(4):
                inst = v.tensor_mul(c_s[:, j * 256:(j + 1) * 256],
                                    c_xo[:, j * 256:(j + 1) * 256], c_gbc[:, :])
            tV.inc(inst)
            if m > 0:
                v.wait_ge(tT.h, v_hr)
                v.wait_ge(tV.h, tV.v)
                tV.inc(v.tensor_add(c_s[:, :], c_s[:, :], p_b[:, :]))
        v_s = tV.v

        @blk.tensor
        def _(t, m=m, v_ldc=v_ldc, v_s=v_s, pm=pm):
            t.wait_ge(dS.h, v_ldc)
            t.wait_ge(tV.h, v_s)
            sel = p_d if m % 2 == 0 else p_c
            for dt in range(8):
                inst = t.matmul(sel[:, 0:256], w_vc["o"][:, dt * 128:(dt + 1) * 128],
                                c_ctx[:, pm * 2048 + dt * 256: pm * 2048 + (dt + 1) * 256],
                                start=(dt == 0), stop=(dt == 7))
            tT.inc(inst)
        v_xvo = tT.v

        @blk.vector
        def _(v, m=m, v_xvo=v_xvo):
            v.wait_ge(tT.h, v_xvo)
            sel = p_d if m % 2 == 0 else p_c
            tV.inc(v.tensor_scalar(c_xvo[:, :], sel[:, 0:256], scl128["o"][:, :],
                                   None, OP.mult))
        v_xvoe = tV.v

        @blk.tensor
        def _(t, m=m, v_xvoe=v_xvoe, v_s=v_s, pm=pm):
            t.wait_ge(tV.h, v_xvoe)
            t.wait_ge(tV.h, v_s)
            for fs in range(4):
                for dt in range(8):
                    t.matmul(p_b[:, fs * 256:(fs + 1) * 256],
                             w["fo"][:, dt * 512 + fs * 128: dt * 512 + fs * 128 + 128],
                             c_ctx[:, pm * 2048 + dt * 256: pm * 2048 + (dt + 1) * 256],
                             start=(dt == 0), stop=False)
                inst = t.matmul(p_b[:, fs * 256:(fs + 1) * 256],
                                w_uc["o"][:, fs * 128:(fs + 1) * 128], c_xvo[:, :],
                                start=False, stop=True)
            tT.inc(inst)
        v_omm = tT.v
        fin["omm"] = v_omm

        @blk.scalar
        def _(s, m=m, v_omm=v_omm):
            if m == 0:
                wait_barrier(s, bC)
            s.wait_ge(tT.h, v_omm)
            for fs in range(4):
                inst = s.activation(c_g[:, fs * 256:(fs + 1) * 256],
                                    p_b[:, fs * 256:(fs + 1) * 256], AF.Silu,
                                    bias=b_o[:, fs:fs + 1])
            tA.inc(inst)
        v_g = tA.v
        fin["g"] = v_g

        @blk.vector
        def _(v, m=m, v_g=v_g, v_s=v_s):
            v.wait_ge(tA.h, v_g)
            tV.inc(v.tensor_mul(c_s[:, :], c_s[:, :], c_g[:, :]))
        v_sg = tV.v

        if "s_d" in dbg_out or "g_d" in dbg_out:
            @blk.gpsimd
            def _(g, m=m, v_sg=v_sg):
                g.wait_ge(tV.h, v_sg)
                if "s_d" in dbg_out:
                    dG.inc16(g.dma_start(out=dbg_out["s_d"][:, m * 1024:(m + 1) * 1024],
                                         in_=c_s[:, :]))
                if "g_d" in dbg_out:
                    dG.inc16(g.dma_start(out=dbg_out["g_d"][:, m * 1024:(m + 1) * 1024],
                                         in_=c_g[:, :]))

        v_oev = []
        for cs in range(2):
            @blk.tensor
            def _(t, m=m, cs=cs, v_sg=v_sg):
                t.wait_ge(tV.h, v_sg)
                if cs == 1:
                    t.wait_ge(tV.h, v_oev[0])
                for j in range(4):
                    inst = t.transpose(p_tf[:, j * 128:(j + 1) * 128],
                                       c_s[:, j * 256 + cs * 128: j * 256 + cs * 128 + 128],
                                       c_identf[:, :])
                tT.inc(inst)
            v_trm = tT.v
            fin["sgtr"] = v_trm

            @blk.vector
            def _(v, m=m, cs=cs, v_trm=v_trm, v_ld=v_ld, pm=pm):
                v.wait_ge(tT.h, v_trm)
                v.wait_ge(dS.h, v_ld)
                if v_outdma[cs]:
                    v.wait_ge(dS.h, v_outdma[cs])
                tV.inc(v.tensor_add(c_out[:, cs * 512:(cs + 1) * 512], p_tf[:, 0:512],
                                    c_xr[:, pm * 1024 + cs * 512: pm * 1024 + (cs + 1) * 512]))
            v_oev.append(tV.v)

            @blk.sync
            def _(sy, m=m, cs=cs, vo=v_oev[cs]):
                sy.wait_ge(tV.h, vo)
                dS.inc16(sy.dma_start(
                    out=out[m * 256 + cs * 128: m * 256 + (cs + 1) * 128, :],
                    in_=c_out[:, cs * 512:(cs + 1) * 512]))
            v_outdma[cs] = dS.v

        fin["out"] = v_oev[1]
        v_cm[m] = (tT.v, tA.v, tV.v)

    @blk.sync
    def _(sy):
        sy.wait_ge(dS.h, dS.v)
        sy.wait_ge(dG.h, dG.v)
        sy.wait_ge(tT.h, tT.v)
        sy.wait_ge(tA.h, tA.v)
        sy.wait_ge(tV.h, tV.v)
        sy.wait_ge(cc.h, cc.v)

    ctxD.close()
    ctxG.close()
    return nc, dbg_out


def _host_inputs(inputs, c):
    b, hg = c // 2, c % 2
    F = np.arange(512 * hg, 512 * hg + 512)
    notF = np.arange(512 * (1 - hg), 512 * (1 - hg) + 512)
    perm = np.concatenate([F, notF])
    g = lambda n: np.asarray(inputs[n], np.float32)
    d = {}
    d["x"] = np.ascontiguousarray(g("L0_attn_x")[b][:, perm])
    d["wqT"] = np.ascontiguousarray(g("L0_attn_Wq").T[perm][:, F])
    d["wkT"] = np.ascontiguousarray(g("L0_attn_Wk").T[perm][:, F])
    d["wvT"] = np.ascontiguousarray(g("L0_attn_Wv").T[perm][:, F])
    d["wuT"] = np.ascontiguousarray(g("L0_attn_Wu").T[perm][:, F])
    d["woT"] = np.ascontiguousarray(g("L0_attn_Wo").T[:, F])
    for p in "qkv":
        d[f"f{p}T"] = np.ascontiguousarray(g(f"L0_ffn_{p}Wm").T[perm][:, F])
        d[f"vcT_{p}"] = np.ascontiguousarray(g(f"L0_ffn_{p}V").reshape(ER, D).T[perm])
    d["foT"] = np.ascontiguousarray(g("L0_ffn_oWm").T[:, F])
    d["vcT_o"] = np.ascontiguousarray(g("L0_ffn_oV").reshape(ER, D).T)
    for p in "qkvo":
        d[f"ucat_{p}"] = np.ascontiguousarray(
            g(f"L0_ffn_{p}U").transpose(0, 2, 1).reshape(ER, D)[:, F])
        d[f"srow_{p}"] = np.ascontiguousarray(g(f"L0_ffn_{p}S").reshape(1, ER))
    for p in "qko":
        d[f"bcol_{p}"] = np.ascontiguousarray(g(f"L0_ffn_{p}b")[F].reshape(4, 128).T)
    d["vb_row"] = np.ascontiguousarray(g("L0_ffn_vb")[F].reshape(1, 512))
    d["nw_pa"] = np.ascontiguousarray(g("L0_attn_norm_w")[perm].reshape(8, 128).T)
    d["gw1T"] = np.ascontiguousarray(g("L0_ffn_gw1").T[perm])
    d["gb1"] = np.ascontiguousarray(g("L0_ffn_gb1").reshape(1, GH))
    d["gw2T"] = np.ascontiguousarray(g("L0_ffn_gw2").T)
    d["gb2"] = np.ascontiguousarray(g("L0_ffn_gb2").reshape(1, E))
    d["identb"] = np.eye(128, dtype=ml_dtypes.bfloat16)
    d["identf"] = np.eye(128, dtype=np.float32)
    tri = (np.arange(SEG)[:, None] >= np.arange(SEG)[None, :])
    d["triw"] = tri.astype(np.float32)
    d["trik"] = np.ascontiguousarray(tri.T.astype(ml_dtypes.bfloat16))
    d["cntinv"] = np.ascontiguousarray(
        (1.0 / np.arange(1, SEG + 1, dtype=np.float32)).reshape(2, 128).T)
    e33a = np.zeros((33, 33 * 128), ml_dtypes.bfloat16)
    for n in range(33):
        e33a[n, n * 128:(n + 1) * 128] = 1
    d["e33"] = e33a
    e16a = np.zeros((E, 128), ml_dtypes.bfloat16)
    for e in range(E):
        e16a[e, e * 16:(e + 1) * 16] = 1
    d["e16"] = e16a
    return d


_CACHE = {}


def kernel(**inputs):
    key = tuple(DBG)
    if key not in _CACHE:
        _CACHE[key] = build_nc(dbg=DBG)
    nc, dbg_out = _CACHE[key]
    in_maps = [_host_inputs(inputs, c) for c in range(8)]
    res = run_bass_kernel_spmd(nc, in_maps, core_ids=list(range(8)))
    outa = np.empty((B, T, D), np.float32)
    for c in range(8):
        b, hg = c // 2, c % 2
        outa[b, :, 512 * hg:512 * hg + 512] = res.results[c]["out"]
    kernel.last_results = res.results
    return outa


# version 18
# speedup vs baseline: 1.4057x; 1.0003x over previous
import contextlib
import numpy as np
import ml_dtypes
import concourse.bass as bass
import concourse.mybir as mybir
from concourse.bass_utils import run_bass_kernel_spmd
from concourse.alu_op_type import AluOpType as OP

F32 = mybir.dt.float32
BF16 = mybir.dt.bfloat16
AF = mybir.ActivationFunctionType
AX = mybir.AxisListType.X

B, T, D = 4, 8192, 1024
H, DK, SEG, NCH = 16, 64, 256, 32
NT = T // 128
NTC = T // 512
E, R, ER, GH = 8, 16, 128, 256
SCALE = float(DK) ** -0.5
EPS_LN, EPS_RMS, EPS_DEN = 1e-5, 1e-6, 1e-6
NEG_PAD = -5e29

DBG = []


class Ctr:
    def __init__(self, nc, ctx, name):
        self.h = ctx.enter_context(nc.semaphore(name))
        self.v = 0

    def inc(self, inst, k=1):
        inst.then_inc(self.h, k)
        self.v += k
        return self.v

    def inc16(self, inst):
        return self.inc(inst, 16)


def build_nc(dbg=()):
    nc = bass.Bass("TRN2", target_bir_lowering=False, debug=False)

    def din(name, shape, dtype=F32):
        return nc.dram_tensor(name, shape, dtype, kind="ExternalInput").ap()

    x = din("x", [T, D])
    wT = {n: din(n, [D, 512]) for n in ("wqT", "wkT", "wvT", "wuT", "woT",
                                        "fqT", "fkT", "fvT", "foT")}
    vcT = {p: din(f"vcT_{p}", [D, ER]) for p in "qkvo"}
    ucat = {p: din(f"ucat_{p}", [ER, 512]) for p in "qkvo"}
    bcol = {p: din(f"bcol_{p}", [128, 4]) for p in "qko"}
    vb_row = din("vb_row", [1, 512])
    srow = {p: din(f"srow_{p}", [1, ER]) for p in "qkvo"}
    nw_pa = din("nw_pa", [128, 8])
    gw1T, gb1 = din("gw1T", [D, GH]), din("gb1", [1, GH])
    gw2T, gb2 = din("gw2T", [GH, E]), din("gb2", [1, E])
    identb = din("identb", [128, 128], BF16)
    identf = din("identf", [128, 128], F32)
    triw = din("triw", [SEG, SEG])
    trik = din("trik", [SEG, SEG], BF16)
    cntinv = din("cntinv", [128, 2])
    e33 = din("e33", [33, 33 * 128], BF16)
    e16 = din("e16", [E, 128], BF16)

    out = nc.dram_tensor("out", [T, 512], F32, kind="ExternalOutput").ap()

    def dint(name, shape, dtype):
        return nc.dram_tensor(name, shape, dtype)

    xn_tm = dint("xn_tm", [T, D], BF16)
    qf = dint("qf", [4 * NTC * 128, 512], BF16)
    kf = dint("kf", [4 * NTC * 128, 512], BF16)
    vtok = dint("vtok", [4 * 64 * 128, 128], BF16)
    pqc = dint("pqc", [NCH * 4 * 128, 256], BF16)
    pre_in = dint("pre_in", [512, T], BF16)
    pre_out = dint("pre_out", [1024, T], BF16)
    ctx_in = dint("ctx_in", [512, T], BF16)
    ctx_out = dint("ctx_out", [1024, T], BF16)
    sc_in = dint("sc_in", [T, 33], F32)
    sc_out = dint("sc_out", [T, 33], F32)

    dbg_out = {}
    for name, shape in [
        ("xn_d", [T, D]), ("pooled_d", [128, 8]), ("gate_d", [1, 8]),
        ("rms_d", [1, T]), ("qf_d", [4 * NTC * 128, 512]),
        ("kf_d", [4 * NTC * 128, 512]), ("vt_d", [4 * 64 * 128, 128]),
        ("pre_d", [512, T]), ("sc_d", [T, 33]), ("ctx_d", [512, T]),
        ("hist_d", [128, 4 * NCH * 64]), ("gates_d", [128, 64 * 33]),
        ("s_d", [128, NCH * 1024]), ("g_d", [128, NCH * 1024]),
        ("pq_d", [NCH * 4 * 128, 256]), ("den_d", [128, NCH * 1024]),
        ("bc_d", [128, NCH * 1024]), ("num_d", [128, NCH * 1024]),
        ("zz_d", [128, NCH * 1024]), ("atm_d", [128, NCH * 1024]),
    ]:
        if name in dbg:
            dbg_out[name] = nc.dram_tensor(name, shape, F32, kind="ExternalOutput").ap()

    ctxG = contextlib.ExitStack()

    def sbG(name, shape, dtype):
        return ctxG.enter_context(nc.sbuf_tensor(name, shape, dtype))

    def psG(name, shape, dtype=F32):
        return ctxG.enter_context(nc.psum_tensor(name, shape, dtype))

    w = {n: sbG("w_" + n, [128, 8 * 512], BF16) for n in
         ("wq", "wk", "wv", "wu", "wo", "fq", "fk", "fv", "fo")}
    w_vc = {p: sbG(f"w_vc_{p}", [128, 8 * 128], BF16) for p in "qkvo"}
    w_uc = {p: sbG(f"w_uc_{p}", [128, 512], BF16) for p in "qkvo"}
    w_g1 = sbG("w_g1", [128, 8 * 256], BF16)
    w_g2 = sbG("w_g2", [128, 2 * 8], BF16)
    c_identb = sbG("c_identb", [128, 128], BF16)
    c_identf = sbG("c_identf", [128, 128], F32)
    c_triw = sbG("c_triw", [128, 2 * 256], F32)
    c_trik = sbG("c_trik", [128, 2 * 256], BF16)
    c_cnt = sbG("c_cnt", [128, 2], F32)
    c_e33 = sbG("c_e33", [33, 33 * 128], BF16)
    c_e16 = sbG("c_e16", [E, 128], BF16)
    b_q = sbG("b_q", [128, 4], F32)
    b_k = sbG("b_k", [128, 4], F32)
    b_o = sbG("b_o", [128, 4], F32)
    b_v = sbG("b_v", [1, 512], BF16)
    c_nw = sbG("c_nw", [128, 8], F32)
    c_gb1 = sbG("c_gb1", [1, GH], F32)
    c_gb2 = sbG("c_gb2", [1, E], F32)
    c_srow = {p: sbG(f"c_srow_{p}", [1, ER], F32) for p in "qkvo"}
    ones_col = sbG("ones_col", [128, 1], BF16)
    c_eps = sbG("c_eps", [128, 1], F32)
    ones_row = sbG("ones_row", [1, 128], BF16)
    ones_r2 = sbG("ones_r2", [65, 128], BF16)
    ones_sq = sbG("ones_sq", [128, 128], BF16)
    scl128 = {p: sbG(f"scl128_{p}", [128, 1], F32) for p in "qkvo"}
    ge128 = sbG("ge128", [128, 1], F32)
    rms_row = sbG("rms_row", [1, T], BF16)
    a_hist = sbG("a_hist", [128, 4 * NCH * 64], BF16)
    a_scum = sbG("a_scum", [128, 4 * 64], F32)
    a_scbf = sbG("a_scbf", [128, 4 * 64], BF16)
    a_z = sbG("a_z", [128, 4], F32)
    a_stat = sbG("a_stat", [128, 8], F32)
    st_sm = sbG("st_sm", [128, 12], F32)

    p_a = psG("p_a", [128, 1024])
    p_b = psG("p_b", [128, 1024])
    p_c = psG("p_c", [128, 512])
    p_d = psG("p_d", [128, 512])
    p_t = psG("p_t", [128, 512], BF16)
    p_tf = psG("p_tf", [128, 512])

    sems = {n: Ctr(nc, ctxG, n) for n in ("dS", "dG", "tT", "tA", "tV", "cc")}
    dS, dG, tT, tA, tV, cc = (sems[n] for n in ("dS", "dG", "tT", "tA", "tV", "cc"))

    ctxG.enter_context(nc.allow_low_precision(
        reason="bf16 matmul operand stream with fp32 accumulation by design; "
               "tolerance bar measured for bf16 matmuls"))
    blk = ctxG.enter_context(nc.Block())

    def barrier():
        return (tT.v, tA.v, tV.v, dS.v, dG.v)

    def wait_barrier(eng, bv):
        vt, va, vv, vs, vg = bv
        if vt: eng.wait_ge(tT.h, vt)
        if va: eng.wait_ge(tA.h, va)
        if vv: eng.wait_ge(tV.h, vv)
        if vs: eng.wait_ge(dS.h, vs)
        if vg: eng.wait_ge(dG.h, vg)

    def hsl(h):
        return 64 * (h % 2), h // 2

    @blk.gpsimd
    def _(g):
        names = {"wq": "wqT", "wk": "wkT", "wv": "wvT", "wu": "wuT",
                 "wo": "woT", "fq": "fqT", "fk": "fkT", "fv": "fvT", "fo": "foT"}
        for n, sn in names.items():
            dG.inc16(g.dma_start(out=w[n][:, :].rearrange("p (a f) -> p a f", a=8),
                                 in_=wT[sn].rearrange("(a p) f -> p a f", p=128)))
        for p in "qkvo":
            dG.inc16(g.dma_start(out=w_vc[p][:, :].rearrange("p (a f) -> p a f", a=8),
                                 in_=vcT[p].rearrange("(a p) f -> p a f", p=128)))
            dG.inc16(g.dma_start(out=w_uc[p][:, :], in_=ucat[p]))
            dG.inc16(g.dma_start(out=c_srow[p][:, :], in_=srow[p]))
        dG.inc16(g.dma_start(out=w_g1[:, :].rearrange("p (a f) -> p a f", a=8),
                             in_=gw1T.rearrange("(a p) f -> p a f", p=128)))
        dG.inc16(g.dma_start(out=w_g2[:, :].rearrange("p (a f) -> p a f", a=2),
                             in_=gw2T.rearrange("(a p) f -> p a f", p=128)))
        dG.inc16(g.dma_start(out=c_identb[:, :], in_=identb))
        dG.inc16(g.dma_start(out=c_identf[:, :], in_=identf))
        dG.inc16(g.dma_start(out=c_triw[:, :].rearrange("p (a c) -> p a c", a=2),
                             in_=triw.rearrange("(a p) c -> p a c", p=128)))
        dG.inc16(g.dma_start(out=c_trik[:, :].rearrange("p (a c) -> p a c", a=2),
                             in_=trik.rearrange("(a p) c -> p a c", p=128)))
        dG.inc16(g.dma_start(out=c_cnt[:, :], in_=cntinv))
        dG.inc16(g.dma_start(out=c_e33[:, :], in_=e33))
        dG.inc16(g.dma_start(out=c_e16[:, :], in_=e16))
        for dst, src in [(b_q, bcol["q"]), (b_k, bcol["k"]), (b_o, bcol["o"])]:
            dG.inc16(g.dma_start(out=dst[:, :], in_=src))
        dG.inc16(g.dma_start(out=b_v[:, :], in_=vb_row))
        dG.inc16(g.dma_start(out=c_nw[:, :], in_=nw_pa))
        dG.inc16(g.dma_start(out=c_gb1[:, :], in_=gb1))
        dG.inc16(g.dma_start(out=c_gb2[:, :], in_=gb2))
    v_w = dG.v

    @blk.vector
    def _(v):
        v.memset(ones_col[:, :], 1.0)
        v.memset(ones_row[:, :], 1.0)
        v.memset(ones_r2[:, :], 1.0)
        v.memset(ones_sq[:, :], 1.0)
        v.memset(a_scum[:, :], 0.0)
        v.memset(a_scbf[:, :], 0.0)
        v.memset(a_z[:, :], 0.0)
        v.memset(c_eps[:, :], EPS_LN)
        tV.inc(v.memset(p_c[:, 0:8], 0.0))
    v_ones = tV.v

    ctxA = contextlib.ExitStack()

    def sbA(name, shape, dtype):
        return ctxA.enter_context(nc.sbuf_tensor(name, shape, dtype))

    x_t = sbA("x_t", [128, 4 * 1024], F32)
    sq_scr = sbA("sq_scr", [128, 1024], BF16)
    xn_t = sbA("xn_t", [128, 4 * 1024], BF16)
    pool_bf = sbA("pool_bf", [128, 8], BF16)
    g_sc = sbA("g_sc", [1, 64], F32)
    g_h = sbA("g_h", [1, GH], F32)
    g_hb = sbA("g_hb", [128, 3], BF16)
    xf = sbA("xf", [128, 2 * 8 * 512], BF16)
    x2 = sbA("x2", [128, 8 * 512], BF16)
    xvs = sbA("xvs", [128, 512], BF16)
    prj = sbA("prj", [128, 4 * 512], BF16)
    prj2 = sbA("prj2", [128, 4 * 512], BF16)
    rms_scr = sbA("rms_scr", [1, 512], F32)
    rms_sq = sbA("rms_sq", [1, 512], F32)

    NB = 4
    v_xnw, v_xndma, v_pool = {}, {}, {}
    v_statdone = {}
    for it in range(NT // NB):
        @blk.sync
        def _(sy, it=it):
            if it >= 1:
                sy.wait_ge(tV.h, v_xnw[it - 1])
            for ti in range(NB):
                i = it * NB + ti
                dS.inc16(sy.dma_start(out=x_t[:, ti * 1024:(ti + 1) * 1024],
                                      in_=x[i * 128:(i + 1) * 128, :]))
        v_x = dS.v

        @blk.scalar
        def _(s, it=it, v_x=v_x):
            s.wait_ge(dS.h, v_x)
            if it == 0:
                s.wait_ge(tV.h, v_ones)
            if it >= 1:
                s.wait_ge(tV.h, v_statdone[it - 1])
            for ti in range(NB):
                inst = s.activation(sq_scr[:, :], x_t[:, ti * 1024:(ti + 1) * 1024],
                                    AF.Square, accum_out=st_sm[:, 4 + ti:5 + ti])
            tA.inc(inst)
        v_ssq = tA.v

        @blk.vector
        def _(v, it=it, v_x=v_x, v_ssq=v_ssq):
            v.wait_ge(dS.h, v_x)
            if it >= 1:
                v.wait_ge(dS.h, v_xndma[it - 1])
                v.wait_ge(tT.h, v_pool[it - 1])
            for ti in range(NB):
                inst = v.reduce_sum(st_sm[:, ti:ti + 1],
                                    x_t[:, ti * 1024:(ti + 1) * 1024], axis=AX)
            tV.inc(inst)
            v.wait_ge(tV.h, tV.v)
            tV.inc(v.tensor_scalar(st_sm[:, 0:4], st_sm[:, 0:4], 1.0 / D, None, OP.mult))
            v.wait_ge(tV.h, tV.v)
            tV.inc(v.tensor_mul(st_sm[:, 8:12], st_sm[:, 0:4], st_sm[:, 0:4]))
            v.wait_ge(tA.h, v_ssq)
            v.wait_ge(tV.h, tV.v)
            tV.inc(v.scalar_tensor_tensor(st_sm[:, 4:8], st_sm[:, 4:8], 1.0 / D,
                                          st_sm[:, 8:12], OP.mult, OP.subtract))
        v_var = tV.v

        @blk.scalar
        def _(s, v_var=v_var):
            s.wait_ge(tV.h, v_var)
            tA.inc(s.activation(st_sm[:, 4:8], st_sm[:, 4:8], AF.Sqrt, bias=c_eps[:, :]))
        v_sd = tA.v

        @blk.vector
        def _(v, it=it, v_sd=v_sd):
            v.wait_ge(tA.h, v_sd)
            tV.inc(v.reciprocal(st_sm[:, 8:12], st_sm[:, 4:8]))
            v.wait_ge(tV.h, tV.v)
            for ti in range(NB):
                inst = v.tensor_scalar(xn_t[:, ti * 1024:(ti + 1) * 1024],
                                       x_t[:, ti * 1024:(ti + 1) * 1024],
                                       st_sm[:, ti:ti + 1], st_sm[:, 8 + ti:9 + ti],
                                       OP.subtract, OP.mult)
            tV.inc(inst)
        v_xnw[it] = tV.v
        v_statdone[it] = tV.v

        @blk.tensor
        def _(t, it=it, vx=v_xnw[it]):
            t.wait_ge(tV.h, vx)
            for ti in range(NB):
                for ds in range(8):
                    inst = t.matmul(p_c[:, ds:ds + 1],
                                    xn_t[:, ti * 1024 + ds * 128: ti * 1024 + (ds + 1) * 128],
                                    ones_col[:, :], start=False,
                                    stop=(it == NT // NB - 1 and ti == NB - 1))
            tT.inc(inst)
        v_pool[it] = tT.v

        @blk.sync
        def _(sy, it=it, vx=v_xnw[it]):
            sy.wait_ge(tV.h, vx)
            dS.inc16(sy.dma_start(
                out=xn_tm.ap()[it * NB * 128:(it + 1) * NB * 128, :]
                    .rearrange("(ti p) c -> p ti c", p=128),
                in_=xn_t[:, :].rearrange("p (ti c) -> p ti c", ti=NB)))
        v_xndma[it] = dS.v
    v_xn_all = dS.v
    if "xn_d" in dbg_out:
        @blk.gpsimd
        def _(g):
            g.wait_ge(dS.h, v_xn_all)
            dG.inc16(g.dma_start(out=dbg_out["xn_d"], in_=xn_tm.ap()[:, :]))

    @blk.vector
    def _(v):
        v.wait_ge(tT.h, v_pool[NT // 4 - 1])
        v.wait_ge(dG.h, v_w)
        tV.inc(v.tensor_scalar(pool_bf[:, :], p_c[:, 0:8], 1.0 / T, None, OP.mult))
    v_poolbf = tV.v

    @blk.tensor
    def _(t):
        t.wait_ge(tV.h, v_poolbf)
        for ds in range(8):
            inst = t.matmul(p_d[0:1, 0:GH], pool_bf[:, ds:ds + 1],
                            w_g1[:, ds * 256:(ds + 1) * 256], start=(ds == 0), stop=(ds == 7))
        tT.inc(inst)
    v_g1 = tT.v

    @blk.vector
    def _(v):
        v.wait_ge(tT.h, v_g1)
        tV.inc(v.tensor_add(g_h[:, :], p_d[0:1, 0:GH], c_gb1[:, :]))
        v.wait_ge(tV.h, tV.v)
        tV.inc(v.tensor_relu(g_h[:, :], g_h[:, :]))
    v_h = tV.v

    @blk.tensor
    def _(t):
        t.wait_ge(tV.h, v_h)
        t.transpose(p_tf[:, 0:1], g_h[0:1, 0:128], c_identf[0:1, 0:1])
        tT.inc(t.transpose(p_tf[:, 1:2], g_h[0:1, 128:256], c_identf[0:1, 0:1]))
    v_ht = tT.v

    @blk.vector
    def _(v):
        v.wait_ge(tT.h, v_ht)
        tV.inc(v.tensor_copy(g_hb[:, 0:2], p_tf[:, 0:2]))
    v_hb = tV.v

    @blk.tensor
    def _(t):
        t.wait_ge(tV.h, v_hb)
        t.matmul(p_d[0:1, 256:256 + E], g_hb[:, 0:1], w_g2[:, 0:8], start=True, stop=False)
        tT.inc(t.matmul(p_d[0:1, 256:256 + E], g_hb[:, 1:2], w_g2[:, 8:16],
                        start=False, stop=True))
    v_g2 = tT.v

    @blk.vector
    def _(v):
        v.wait_ge(tT.h, v_g2)
        tV.inc(v.tensor_add(g_sc[:, 0:8], p_d[0:1, 256:256 + E], c_gb2[:, :]))
        v.wait_ge(tV.h, tV.v)
        tV.inc(v.reduce_max(g_sc[:, 8:9], g_sc[:, 0:8], axis=AX))
        v.wait_ge(tV.h, tV.v)
        tV.inc(v.tensor_scalar(g_sc[:, 16:24], g_sc[:, 0:8], g_sc[:, 8:9], None, OP.is_ge))
        v.wait_ge(tV.h, tV.v)
        tV.inc(v.scalar_tensor_tensor(g_sc[:, 24:32], g_sc[:, 16:24], -1e30,
                                      g_sc[:, 0:8], OP.mult, OP.add))
        v.wait_ge(tV.h, tV.v)
        tV.inc(v.reduce_max(g_sc[:, 9:10], g_sc[:, 24:32], axis=AX))
        v.wait_ge(tV.h, tV.v)
        tV.inc(v.tensor_scalar(g_sc[:, 24:32], g_sc[:, 24:32], g_sc[:, 9:10], None, OP.is_ge))
        tV.inc(v.tensor_sub(g_sc[:, 32:33], g_sc[:, 8:9], g_sc[:, 9:10]))
    v_top = tV.v

    @blk.scalar
    def _(s):
        s.wait_ge(tV.h, v_top)
        tA.inc(s.activation(g_sc[:, 32:33], g_sc[:, 32:33], AF.Sigmoid))
    v_sig = tA.v

    @blk.vector
    def _(v):
        v.wait_ge(tA.h, v_sig)
        tV.inc(v.tensor_scalar(g_sc[:, 40:48], g_sc[:, 16:24], g_sc[:, 32:33], None, OP.mult))
        tV.inc(v.scalar_tensor_tensor(g_sc[:, 48:56], g_sc[:, 24:32], g_sc[:, 32:33],
                                      g_sc[:, 24:32], OP.mult, OP.subtract))
        v.wait_ge(tV.h, tV.v)
        tV.inc(v.tensor_sub(g_sc[:, 40:48], g_sc[:, 40:48], g_sc[:, 48:56]))
    v_gate = tV.v

    @blk.tensor
    def _(t):
        t.wait_ge(tV.h, v_gate)
        tT.inc(t.transpose(p_tf[0:8, 2:3], g_sc[0:1, 40:48], c_identf[0:1, 0:1]))
        for ji, p in enumerate("qkvo"):
            tT.inc(t.transpose(p_tf[:, 3 + ji:4 + ji], c_srow[p][0:1, :],
                               c_identf[0:1, 0:1]))
    v_gt = tT.v

    @blk.vector
    def _(v):
        v.wait_ge(tT.h, v_gt)
        tV.inc(v.tensor_copy(g_hb[0:8, 2:3], p_tf[0:8, 2:3]))
    v_gtb = tV.v

    @blk.tensor
    def _(t):
        t.wait_ge(tV.h, v_gtb)
        tT.inc(t.matmul(p_tf[:, 7:8], c_e16[:, :], g_hb[0:E, 2:3], start=True, stop=True))
    v_ge = tT.v

    @blk.vector
    def _(v):
        v.wait_ge(tT.h, v_ge)
        tV.inc(v.tensor_copy(ge128[:, :], p_tf[:, 7:8]))
        v.wait_ge(tV.h, tV.v)
        for ji, p in enumerate("qkvo"):
            inst = v.tensor_mul(scl128[p][:, :], ge128[:, :], p_tf[:, 3 + ji:4 + ji])
        tV.inc(inst)
        if "gate_d" in dbg_out:
            tV.inc(v.tensor_copy(g_h[0:1, 0:8], g_sc[0:1, 40:48]))
    v_scl = tV.v

    if "pooled_d" in dbg_out or "gate_d" in dbg_out:
        @blk.gpsimd
        def _(g):
            if "pooled_d" in dbg_out:
                g.wait_ge(tV.h, v_poolbf)
                dG.inc16(g.dma_start(out=dbg_out["pooled_d"], in_=pool_bf[:, :]))
            if "gate_d" in dbg_out:
                g.wait_ge(tV.h, v_scl)
                dG.inc16(g.dma_start(out=dbg_out["gate_d"], in_=g_h[0:1, 0:8]))

    v_tc_pe, v_tc_ve, v_prjdma, v_prj2dma, v_rms = {}, {}, {}, {}, {}
    last_pd_read = v_top
    last_xvs = [v_scl]
    last_pab_evac = [v_scl]
    for tc in range(NTC):
        fa = tc % 2
        xo = fa * 4096

        @blk.sync
        def _(sy, tc=tc, xo=xo):
            if tc >= 2:
                sy.wait_ge(tT.h, v_tc_pe[tc - 2])
                sy.wait_ge(tV.h, v_tc_ve[tc - 2])
            sy.wait_ge(dS.h, v_xn_all)
            for dt in range(8):
                inst = sy.dma_start(
                    out=xf[:, xo + dt * 512: xo + (dt + 1) * 512],
                    in_=xn_tm.ap()[tc * 512:(tc + 1) * 512, dt * 128:(dt + 1) * 128],
                    transpose=True)
                dS.inc16(inst)
        v_xf = dS.v

        @blk.vector
        def _(v, tc=tc, xo=xo, v_xf=v_xf):
            v.wait_ge(dS.h, v_xf)
            if tc >= 1:
                v.wait_ge(tT.h, v_rms[tc - 1])
            tV.inc(v.tensor_mul(x2[:, :], xf[:, xo:xo + 4096], xf[:, xo:xo + 4096]))
        v_x2 = tV.v

        @blk.tensor
        def _(t, tc=tc, v_x2=v_x2, lpd=last_pd_read):
            t.wait_ge(tV.h, v_x2)
            t.wait_ge(tV.h, lpd)
            for dt in range(8):
                inst = t.matmul(p_d[0:1, 0:512], ones_col[:, :],
                                x2[:, dt * 512:(dt + 1) * 512],
                                start=(dt == 0), stop=(dt == 7))
            tT.inc(inst)
        v_rms[tc] = tT.v

        @blk.vector
        def _(v, vr=v_rms[tc]):
            v.wait_ge(tT.h, vr)
            tV.inc(v.tensor_scalar(rms_scr[:, :], p_d[0:1, 0:512], 1.0 / D, EPS_RMS,
                                   OP.mult, OP.add))
        v_rsc = tV.v
        last_pd_read = v_rsc

        @blk.scalar
        def _(s, v_rsc=v_rsc):
            s.wait_ge(tV.h, v_rsc)
            tA.inc(s.activation(rms_sq[:, :], rms_scr[:, :], AF.Sqrt))
        v_rq = tA.v

        @blk.vector
        def _(v, tc=tc, v_rq=v_rq):
            v.wait_ge(tA.h, v_rq)
            tV.inc(v.reciprocal(rms_row[0:1, tc * 512:(tc + 1) * 512], rms_sq[:, :]))

        for pi, p in enumerate("qkv"):
            wm = w["f" + p]

            @blk.tensor
            def _(t, tc=tc, xo=xo, p=p, pi=pi, lxv=last_xvs[0]):
                if tc == 0 and pi == 0:
                    t.wait_ge(tV.h, v_scl)
                    t.wait_ge(tV.h, v_poolbf)
                else:
                    t.wait_ge(tV.h, lxv)
                for dt in range(8):
                    inst = t.matmul(p_c[:, 0:512], w_vc[p][:, dt * 128:(dt + 1) * 128],
                                    xf[:, xo + dt * 512: xo + (dt + 1) * 512],
                                    start=(dt == 0), stop=(dt == 7))
                tT.inc(inst)
            v_xv = tT.v

            @blk.vector
            def _(v, p=p, v_xv=v_xv):
                v.wait_ge(tT.h, v_xv)
                tV.inc(v.tensor_scalar(xvs[:, :], p_c[:, 0:512], scl128[p][:, :],
                                       None, OP.mult))
            last_xvs = [tV.v]
            v_xvs = tV.v

            if p in "qk":
                stg = prj if p == "q" else prj2

                @blk.tensor
                def _(t, tc=tc, xo=xo, p=p, wm=wm, v_xvs=v_xvs, lpe=last_pab_evac[0]):
                    t.wait_ge(tV.h, v_xvs)
                    t.wait_ge(tV.h, lpe)
                    for fs in range(4):
                        pp = p_a if fs < 2 else p_b
                        po = (fs % 2) * 512
                        for dt in range(8):
                            t.matmul(pp[:, po:po + 512],
                                     wm[:, dt * 512 + fs * 128: dt * 512 + fs * 128 + 128],
                                     xf[:, xo + dt * 512: xo + (dt + 1) * 512],
                                     start=(dt == 0), stop=False)
                        inst = t.matmul(pp[:, po:po + 512],
                                        w_uc[p][:, fs * 128:(fs + 1) * 128], xvs[:, :],
                                        start=False, stop=True)
                    tT.inc(inst)
                v_mm = tT.v

                @blk.vector
                def _(v, tc=tc, p=p, stg=stg, v_mm=v_mm):
                    v.wait_ge(tT.h, v_mm)
                    if tc >= 1:
                        v.wait_ge(dS.h, (v_prjdma if p == "q" else v_prj2dma)[tc - 1])
                    for fs in range(4):
                        pp = p_a if fs < 2 else p_b
                        po = (fs % 2) * 512
                        if p == "q":
                            inst = v.tensor_scalar(stg[:, fs * 512:(fs + 1) * 512],
                                                   pp[:, po:po + 512], b_q[:, fs:fs + 1],
                                                   SCALE, OP.add, OP.mult)
                        else:
                            inst = v.tensor_scalar(stg[:, fs * 512:(fs + 1) * 512],
                                                   pp[:, po:po + 512], b_k[:, fs:fs + 1],
                                                   None, OP.add)
                    tV.inc(inst)
                v_stg = tV.v
                last_pab_evac = [v_stg]

                @blk.sync
                def _(sy, tc=tc, p=p, stg=stg, v_stg=v_stg):
                    sy.wait_ge(tV.h, v_stg)
                    dst = qf if p == "q" else kf
                    for fs in range(4):
                        inst = sy.dma_start(
                            out=dst.ap()[(fs * NTC + tc) * 128:(fs * NTC + tc) * 128 + 128, :],
                            in_=stg[:, fs * 512:(fs + 1) * 512])
                        dS.inc16(inst)
                if p == "q":
                    v_prjdma[tc] = dS.v
                else:
                    v_prj2dma[tc] = dS.v

                if ("qf_d" in dbg_out and p == "q") or ("kf_d" in dbg_out and p == "k"):
                    @blk.gpsimd
                    def _(g, tc=tc, p=p, stg=stg, v_stg=v_stg):
                        g.wait_ge(tV.h, v_stg)
                        dd = dbg_out["qf_d" if p == "q" else "kf_d"]
                        for fs in range(4):
                            dG.inc16(g.dma_start(
                                out=dd[(fs * NTC + tc) * 128:(fs * NTC + tc) * 128 + 128, :],
                                in_=stg[:, fs * 512:(fs + 1) * 512]))
            else:
                @blk.tensor
                def _(t, tc=tc, xo=xo, wm=wm, v_xvs=v_xvs, lpe=last_pab_evac[0]):
                    t.wait_ge(tV.h, v_xvs)
                    t.wait_ge(tV.h, lpe)
                    t.wait_ge(tV.h, v_ones)
                    for mt in range(4):
                        pp = p_a if mt < 2 else p_b
                        po = (mt % 2) * 512
                        for dt in range(8):
                            t.matmul(pp[:, po:po + 512],
                                     xf[:, xo + dt * 512 + mt * 128: xo + dt * 512 + mt * 128 + 128],
                                     wm[:, dt * 512:(dt + 1) * 512],
                                     start=(dt == 0), stop=False)
                        t.matmul(pp[:, po:po + 512], xvs[:, mt * 128:(mt + 1) * 128],
                                 w_uc["v"][:, :], start=False, stop=False)
                        inst = t.matmul(pp[:, po:po + 512], ones_row[:, :], b_v[:, :],
                                        start=False, stop=True)
                    tT.inc(inst)
                v_vmm = tT.v

                @blk.vector
                def _(v, tc=tc, v_vmm=v_vmm):
                    v.wait_ge(tT.h, v_vmm)
                    v.wait_ge(dS.h, v_prjdma[tc])
                    for mt in range(4):
                        pp = p_a if mt < 2 else p_b
                        po = (mt % 2) * 512
                        inst = v.tensor_copy(prj[:, mt * 512:(mt + 1) * 512], pp[:, po:po + 512])
                    tV.inc(inst)
                v_vst = tV.v
                last_pab_evac = [v_vst]

                @blk.sync
                def _(sy, tc=tc, v_vst=v_vst):
                    sy.wait_ge(tV.h, v_vst)
                    for pr in range(4):
                        inst = sy.dma_start(
                            out=vtok.ap()[(pr * 64 + tc * 4) * 128:(pr * 64 + tc * 4 + 4) * 128, :]
                                .rearrange("(mt p) f -> p mt f", p=128),
                            in_=prj[:, :].rearrange("p (mt f) -> p mt f", mt=4)[
                                :, :, pr * 128:(pr + 1) * 128])
                        dS.inc16(inst)

                if "vt_d" in dbg_out:
                    @blk.gpsimd
                    def _(g, tc=tc, v_vst=v_vst):
                        g.wait_ge(tV.h, v_vst)
                        for pr in range(4):
                            dG.inc16(g.dma_start(
                                out=dbg_out["vt_d"][(pr * 64 + tc * 4) * 128:(pr * 64 + tc * 4 + 4) * 128, :]
                                    .rearrange("(mt p) f -> p mt f", p=128),
                                in_=prj[:, :].rearrange("p (mt f) -> p mt f", mt=4)[
                                    :, :, pr * 128:(pr + 1) * 128]))
        v_tc_pe[tc] = tT.v
        v_tc_ve[tc] = tV.v

    if "rms_d" in dbg_out:
        @blk.gpsimd
        def _(g):
            g.wait_ge(tV.h, tV.v)
            dG.inc16(g.dma_start(out=dbg_out["rms_d"], in_=rms_row[:, :]))

    bA = barrier()
    ctxA.close()

    ctxB = contextlib.ExitStack()

    def sbB(name, shape, dtype):
        return ctxB.enter_context(nc.sbuf_tensor(name, shape, dtype))

    a_xn = sbB("a_xn", [128, 2 * 8 * 256], BF16)
    a_xn2 = sbB("a_xn2", [128, 8 * 256], BF16)
    a_rbc = sbB("a_rbc", [128, 256], BF16)
    a_pq = sbB("a_pq", [128, 4 * 256], BF16)
    a_pk = sbB("a_pk", [128, 4 * 256], BF16)
    a_vc = sbB("a_vc", [128, 4 * 256], BF16)
    a_el = sbB("a_el", [128, 4 * 256], BF16)
    a_pkT = sbB("a_pkT", [128, 2 * 512], BF16)
    a_vcT = sbB("a_vcT", [128, 2 * 512], BF16)
    a_atm = sbB("a_atm", [128, 2 * 512], BF16)
    a_zbf = sbB("a_zbf", [128, 4 * 256], BF16)
    a_pt = sbB("a_pt", [128, 4 * 256], BF16)
    a_u = sbB("a_u", [128, 4 * 256], BF16)
    a_den = sbB("a_den", [128, 4 * 256], F32)
    a_den2 = sbB("a_den2", [128, 4 * 256], BF16)
    a_den3 = sbB("a_den3", [128, 4 * 256], F32)
    a_bc = sbB("a_bc", [128, 4 * 256], BF16)
    a_pre = sbB("a_pre", [128, 4 * 256], BF16)
    a_ms = sbB("a_ms", [128, 4 * 32], BF16)
    a_scst = sbB("a_scst", [128, 2 * 33], F32)
    a_wm = sbB("a_wm", [128, 2 * 256], F32)

    v_ch = {}
    v_axn_d = {}
    v_axn_cons = {}
    v_rbcd, v_xn2d = {}, {}
    v_prem_hist = [0]
    v_u_hist = [0]

    def emit_head(mm, v_pc_free):
        @blk.tensor
        def _(t, mm=mm, v_pc_free=v_pc_free):
            if mm == 0:
                wait_barrier(t, bA)
            elif v_pc_free:
                t.wait_ge(tV.h, v_pc_free)
            tT.inc(t.matmul(p_c[:, 0:256], ones_row[:, :],
                            rms_row[0:1, mm * 256:(mm + 1) * 256],
                            start=True, stop=True))
        v_rbcm = tT.v
        v_rbcd[mm] = v_rbcm

        @blk.vector
        def _(v, mm=mm, v_rbcm=v_rbcm):
            if mm == 0:
                wait_barrier(v, bA)
            v.wait_ge(tT.h, v_rbcm)
            tV.inc(v.tensor_copy(a_rbc[:, :], p_c[:, 0:256]))
            v.wait_ge(dS.h, v_axn_d[mm])
            v.wait_ge(tV.h, tV.v)
            if mm >= 1:
                v.wait_ge(tT.h, v_vmm_hist[0])
            xb_ = (mm % 2) * 2048
            for dt in range(8):
                inst = v.scalar_tensor_tensor(a_xn2[:, dt * 256:(dt + 1) * 256],
                                              a_xn[:, xb_ + dt * 256: xb_ + (dt + 1) * 256],
                                              c_nw[:, dt:dt + 1], a_rbc[:, :],
                                              OP.mult, OP.mult)
            tV.inc(inst)
        v_xn2d[mm] = tV.v
    v_vmm_hist = [0]

    @blk.sync
    def _(sy):
        wait_barrier(sy, bA)
        for dt in range(8):
            inst = sy.dma_start(
                out=a_xn[:, dt * 256:(dt + 1) * 256],
                in_=xn_tm.ap()[0:256, dt * 128:(dt + 1) * 128],
                transpose=True)
            dS.inc16(inst)
    v_axn_d[0] = dS.v

    for m in range(NCH):
        xb = (m % 2) * 2048

        @blk.sync
        def _(sy, m=m):
            if m + 1 < NCH:
                xb2 = ((m + 1) % 2) * 2048
                if m >= 1:
                    ca, cb = v_axn_cons[m - 1]
                    sy.wait_ge(tT.h, ca)
                    sy.wait_ge(tV.h, cb)
                for dt in range(8):
                    inst = sy.dma_start(
                        out=a_xn[:, xb2 + dt * 256: xb2 + (dt + 1) * 256],
                        in_=xn_tm.ap()[(m + 1) * 256:(m + 2) * 256, dt * 128:(dt + 1) * 128],
                        transpose=True)
                    dS.inc16(inst)
                v_axn_d[m + 1] = dS.v
        v_axn = v_axn_d[m]

        if m == 0:
            emit_head(0, None)
        v_rbc = v_rbcd[m]
        v_xn2 = v_xn2d[m]

        @blk.tensor
        def _(t, v_xn2=v_xn2, vp=v_prem_hist[0]):
            t.wait_ge(tV.h, v_xn2)
            if vp:
                t.wait_ge(tV.h, vp)
            for fs in range(4):
                for dt in range(8):
                    inst = t.matmul(p_a[:, fs * 256:(fs + 1) * 256],
                                    w["wq"][:, dt * 512 + fs * 128: dt * 512 + fs * 128 + 128],
                                    a_xn2[:, dt * 256:(dt + 1) * 256],
                                    start=(dt == 0), stop=(dt == 7))
            tT.inc(inst)
        v_qmm = tT.v

        @blk.vector
        def _(v, v_qmm=v_qmm):
            v.wait_ge(tT.h, v_qmm)
            tV.inc(v.tensor_scalar(a_el[:, :], p_a[:, :], 0.0, None, OP.min))
        v_min = tV.v

        @blk.scalar
        def _(s, m=m, v_min=v_min):
            if m == 0:
                wait_barrier(s, bA)
            s.wait_ge(tV.h, v_min)
            tA.inc(s.activation(a_el[:, :], a_el[:, :], AF.Exp))
        v_exp = tA.v

        @blk.vector
        def _(v, m=m, v_exp=v_exp):
            v.wait_ge(tA.h, v_exp)
            if m >= 1:
                v.wait_ge(dS.h, v_ch[m - 1][3])
            tV.inc(v.scalar_tensor_tensor(a_pq[:, :], p_a[:, :], 0.0, a_el[:, :],
                                          OP.max, OP.add))
        v_pq = tV.v

        @blk.tensor
        def _(t, v_xn2=v_xn2, vu=v_u_hist[0]):
            if vu:
                t.wait_ge(tV.h, vu)
            for fs in range(4):
                for dt in range(8):
                    inst = t.matmul(p_b[:, fs * 256:(fs + 1) * 256],
                                    w["wk"][:, dt * 512 + fs * 128: dt * 512 + fs * 128 + 128],
                                    a_xn2[:, dt * 256:(dt + 1) * 256],
                                    start=(dt == 0), stop=(dt == 7))
            tT.inc(inst)
        v_kmm = tT.v

        @blk.vector
        def _(v, v_kmm=v_kmm, v_pq=v_pq):
            v.wait_ge(tT.h, v_kmm)
            v.wait_ge(tV.h, v_pq)
            tV.inc(v.tensor_scalar(a_el[:, :], p_b[:, :], 0.0, None, OP.min))
        v_mink = tV.v

        @blk.scalar
        def _(s, v_mink=v_mink):
            s.wait_ge(tV.h, v_mink)
            tA.inc(s.activation(a_el[:, :], a_el[:, :], AF.Exp))
        v_expk = tA.v

        @blk.vector
        def _(v, v_expk=v_expk):
            v.wait_ge(tA.h, v_expk)
            tV.inc(v.scalar_tensor_tensor(a_pk[:, :], p_b[:, :], 0.0, a_el[:, :],
                                          OP.max, OP.add))
        v_pk = tV.v

        @blk.tensor
        def _(t, v_pq=v_pq, v_xn2=v_xn2):
            t.wait_ge(tV.h, v_pq)
            for fs in range(4):
                for dt in range(8):
                    inst = t.matmul(p_a[:, fs * 256:(fs + 1) * 256],
                                    w["wv"][:, dt * 512 + fs * 128: dt * 512 + fs * 128 + 128],
                                    a_xn2[:, dt * 256:(dt + 1) * 256],
                                    start=(dt == 0), stop=(dt == 7))
            tT.inc(inst)
        v_vmm = tT.v

        @blk.vector
        def _(v, v_vmm=v_vmm):
            v.wait_ge(tT.h, v_vmm)
            tV.inc(v.tensor_copy(a_vc[:, :], p_a[:, :]))
        v_vc = tV.v
        v_vmm_hist[0] = v_vmm
        if m + 1 < NCH:
            emit_head(m + 1, v_vc)

        tr_jobs = [("pk", fs, ct) for ct in range(2) for fs in range(4)] + \
                  [("vc", fs, ct) for ct in range(2) for fs in range(4)]
        v_trev = []
        v_zmm_l = [None]
        for r in range(4):
            @blk.tensor
            def _(t, r=r, v_pk=v_pk, v_vc=v_vc):
                t.wait_ge(tV.h, v_pk if r < 2 else v_vc)
                if r == 2:
                    t.wait_ge(tV.h, v_vc)
                if r >= 1:
                    t.wait_ge(tV.h, v_trev[r - 1])
                for ji in range(4):
                    which, fs, ct = tr_jobs[r * 4 + ji]
                    src = a_pk if which == "pk" else a_vc
                    inst = t.transpose(p_t[:, ji * 128:(ji + 1) * 128],
                                       src[:, fs * 256 + ct * 128: fs * 256 + ct * 128 + 128],
                                       c_identb[:, :])
                tT.inc(inst)
            v_trm = tT.v

            @blk.vector
            def _(v, r=r, v_trm=v_trm):
                v.wait_ge(tT.h, v_trm)
                for ji in range(4):
                    which, fs, ct = tr_jobs[r * 4 + ji]
                    dst = a_pkT if which == "pk" else a_vcT
                    inst = v.tensor_copy(dst[:, ct * 512 + fs * 128: ct * 512 + fs * 128 + 128],
                                         p_t[:, ji * 128:(ji + 1) * 128])
                tV.inc(inst)
            v_trev.append(tV.v)
            if r == 1:
                @blk.tensor
                def _(t, v_vc=v_vc, v_tr=v_trev[-1]):
                    t.wait_ge(tV.h, v_vc)
                    t.wait_ge(tV.h, v_tr)
                    for h in range(8):
                        po, j = hsl(h)
                        for ct in range(2):
                            inst = t.matmul(p_a[po:po + 64, j * 256:(j + 1) * 256],
                                            a_pkT[:, ct * 512 + j * 128 + po: ct * 512 + j * 128 + po + 64],
                                            c_trik[:, ct * 256:(ct + 1) * 256],
                                            start=(ct == 0), stop=(ct == 1))
                    tT.inc(inst)
                v_zmm_l[0] = tT.v
        v_tr = v_trev[-1]
        v_zmm = v_zmm_l[0]

        @blk.vector
        def _(v, v_zmm=v_zmm):
            v.wait_ge(tT.h, v_zmm)
            for j in range(4):
                inst = v.tensor_scalar(a_zbf[:, j * 256:(j + 1) * 256],
                                       p_a[:, j * 256:(j + 1) * 256],
                                       a_z[:, j:j + 1], None, OP.add)
            tV.inc(inst)
            v.wait_ge(tV.h, tV.v)
            tV.inc(v.memset(p_a[:, :], 0.0))
            v.wait_ge(tV.h, tV.v)
            tV.inc(v.tensor_mul(a_pt[:, :], a_pq[:, :], a_zbf[:, :]))
            v.wait_ge(tV.h, tV.v)
            for j in range(4):
                inst = v.reduce_sum(a_stat[:, j:j + 1], a_pk[:, j * 256:(j + 1) * 256], axis=AX)
            tV.inc(inst)
            v.wait_ge(tV.h, tV.v)
            tV.inc(v.tensor_add(a_z[:, 0:4], a_z[:, 0:4], a_stat[:, 0:4]))
        v_zup = tV.v

        @blk.tensor
        def _(t, v_zup=v_zup, v_rbc=v_rbc, v_tr=v_tr):
            t.wait_ge(tV.h, v_zup)
            t.wait_ge(tV.h, v_tr)
            for h in range(8):
                po, j = hsl(h)
                inst = t.matmul(p_b[po:po + 1, j * 256:(j + 1) * 256],
                                ones_col[po:po + 64, 0:1],
                                a_pt[po:po + 64, j * 256:(j + 1) * 256],
                                start=True, stop=True,
                                tile_position=(po, po))
            tT.inc(inst)
            t.wait_ge(tV.h, v_rbc)
            for h in range(8):
                po, j = hsl(h)
                for ct in range(2):
                    inst = t.matmul(p_tf[po:po + 64, j * 64:(j + 1) * 64],
                                    a_pkT[:, ct * 512 + j * 128 + po: ct * 512 + j * 128 + po + 64],
                                    a_vcT[:, ct * 512 + j * 128 + po: ct * 512 + j * 128 + po + 64],
                                    start=(ct == 0), stop=(ct == 1))
            tT.inc(inst)
        v_sch = tT.v
        v_den_mm = tT.v - 1

        @blk.vector
        def _(v, v_den_mm=v_den_mm):
            v.wait_ge(tT.h, v_den_mm)
            tV.inc(v.tensor_scalar(a_den[:, :], p_b[:, :], EPS_DEN, None, OP.add))
        v_den_ev = tV.v

        @blk.tensor
        def _(t, v_zup=v_zup):
            t.wait_ge(tV.h, v_zup)
            for h in range(8):
                po, j = hsl(h)
                inst = t.matmul(p_a[po:po + 64, j * 256:(j + 1) * 256],
                                a_scbf[po:po + 64, j * 64:(j + 1) * 64],
                                a_pq[po:po + 64, j * 256:(j + 1) * 256],
                                start=False, stop=False, skip_group_check=True)
            tT.inc(inst)
        v_inter = tT.v

        @blk.vector
        def _(v, m=m, v_sch=v_sch, v_inter=v_inter):
            v.wait_ge(tT.h, v_sch)
            v.wait_ge(tT.h, v_inter)
            tV.inc(v.tensor_add(a_scum[:, :], a_scum[:, :], p_tf[:, 0:256]))
            v.wait_ge(tV.h, tV.v)
            tV.inc(v.tensor_copy(a_scbf[:, :], a_scum[:, :]))
            v.wait_ge(tV.h, tV.v)
            for j in range(4):
                inst = v.tensor_copy(a_hist[:, (j * NCH + m) * 64:(j * NCH + m) * 64 + 64],
                                     a_scum[:, j * 64:(j + 1) * 64])
            tV.inc(inst)
        v_scup = tV.v

        v_at_ev = [None, None]
        horder = [3, 2, 1, 0, 7, 6, 5, 4]

        def emit_atm(h):
            po, j = hsl(h)
            reg = h % 2

            @blk.tensor
            def _(t, h=h, po=po, j=j, reg=reg, v_den_ev=v_den_ev):
                if v_at_ev[reg] is None:
                    t.wait_ge(tV.h, v_den_ev)
                else:
                    t.wait_ge(tV.h, v_at_ev[reg])
                for ds in range(2):
                    inst = t.matmul(p_b[:, reg * 512 + ds * 256: reg * 512 + (ds + 1) * 256],
                                    a_pk[po:po + 64, j * 256 + ds * 128: j * 256 + ds * 128 + 128],
                                    a_pq[po:po + 64, j * 256:(j + 1) * 256],
                                    start=True, stop=True)
                tT.inc(inst)
            v_atm = tT.v

            @blk.vector
            def _(v, reg=reg, v_atm=v_atm):
                v.wait_ge(tT.h, v_atm)
                tV.inc(v.tensor_mul(a_atm[:, reg * 512:(reg + 1) * 512],
                                    p_b[:, reg * 512:(reg + 1) * 512],
                                    c_trik[:, 0:512]))
            v_at_ev[reg] = tV.v

        def emit_vct(h):
            po, j = hsl(h)
            reg = h % 2

            @blk.tensor
            def _(t, h=h, po=po, j=j, reg=reg, vae=v_at_ev[reg]):
                t.wait_ge(tV.h, vae)
                for ds in range(2):
                    inst = t.matmul(p_a[po:po + 64, j * 256:(j + 1) * 256],
                                    a_vcT[:, ds * 512 + j * 128 + po: ds * 512 + j * 128 + po + 64],
                                    a_atm[:, reg * 512 + ds * 256: reg * 512 + (ds + 1) * 256],
                                    start=False, stop=(ds == 1), skip_group_check=True)
                tT.inc(inst)

        emit_atm(horder[0])
        for i in range(1, 8):
            emit_atm(horder[i])
            emit_vct(horder[i - 1])
        emit_vct(horder[7])
        v_intra = tT.v

        @blk.vector
        def _(v, v_den_ev=v_den_ev):
            v.wait_ge(tV.h, v_den_ev)
            tV.inc(v.reciprocal(a_den2[:, 0:512], a_den[:, 0:512]))
            tV.inc(v.reciprocal(a_den2[:, 512:1024], a_den[:, 512:1024]))
        v_den1, v_den = tV.v - 1, tV.v

        @blk.tensor
        def _(t, v_den1=v_den1, v_den=v_den, v_scup=v_scup):
            t.wait_ge(tV.h, v_den1)
            t.wait_ge(tV.h, v_scup)
            for h in (0, 1, 2, 3):
                po, j = hsl(h)
                inst = t.matmul(p_tf[po:po + 64, (j % 2) * 256:(j % 2) * 256 + 256],
                                ones_sq[po:po + 1, 0:64],
                                a_den2[po:po + 1, j * 256:(j + 1) * 256],
                                start=True, stop=True)
            t.wait_ge(tV.h, v_den)
            for h in (4, 5, 6, 7):
                po, j = hsl(h)
                inst = t.matmul(p_c[po:po + 64, (j % 2) * 256:(j % 2) * 256 + 256],
                                ones_sq[po:po + 1, 0:64],
                                a_den2[po:po + 1, j * 256:(j + 1) * 256],
                                start=True, stop=True)
            tT.inc(inst)
        v_bcm = tT.v

        @blk.vector
        def _(v, v_bcm=v_bcm):
            v.wait_ge(tT.h, v_bcm)
            for h in range(8):
                po, j = hsl(h)
                pp, co = (p_tf, (j % 2) * 256) if h < 4 else (p_c, (j % 2) * 256)
                inst = v.tensor_copy(a_bc[po:po + 64, j * 256:(j + 1) * 256],
                                     pp[po:po + 64, co:co + 256])
            tV.inc(inst)
        v_bc = tV.v

        @blk.vector
        def _(v, m=m, v_intra=v_intra, v_bc=v_bc):
            v.wait_ge(tT.h, v_intra)
            v.wait_ge(tV.h, v_bc)
            if m >= 1:
                v.wait_ge(dS.h, v_ch[m - 1][3])
            if "num_d" in dbg_out:
                tV.inc(v.tensor_copy(a_den[:, :], p_a[:, :]))
            tV.inc(v.tensor_mul(a_pre[:, :], p_a[:, :], a_bc[:, :]))
        v_prem = tV.v
        if "num_d" in dbg_out:
            @blk.gpsimd
            def _(g, m=m, v_prem=v_prem):
                g.wait_ge(tV.h, v_prem)
                dG.inc16(g.dma_start(out=dbg_out["num_d"][:, m * 1024:(m + 1) * 1024],
                                     in_=a_den[:, :]))

        @blk.tensor
        def _(t, va0=v_at_ev[0], va1=v_at_ev[1], v_axn=v_axn, xb=xb):
            t.wait_ge(tV.h, va0)
            t.wait_ge(tV.h, va1)
            for fs in range(4):
                for dt in range(8):
                    inst = t.matmul(p_b[:, fs * 256:(fs + 1) * 256],
                                    w["wu"][:, dt * 512 + fs * 128: dt * 512 + fs * 128 + 128],
                                    a_xn[:, xb + dt * 256: xb + (dt + 1) * 256],
                                    start=(dt == 0), stop=(dt == 7))
            tT.inc(inst)
        v_umm = tT.v

        @blk.vector
        def _(v, m=m, v_umm=v_umm, xb=xb):
            v.wait_ge(tT.h, v_umm)
            tV.inc(v.tensor_copy(a_u[:, :], p_b[:, :]))
            for j in range(4):
                inst = v.reduce_sum(a_ms[:, j * 32 + m: j * 32 + m + 1],
                                    a_xn[:, xb + j * 256: xb + (j + 1) * 256], axis=AX)
            tV.inc(inst)
        v_u = tV.v

        @blk.tensor
        def _(t, m=m, v_u=v_u, v_bc=v_bc, xb=xb):
            t.wait_ge(tV.h, v_u)
            t.wait_ge(tV.h, v_bc)
            if m > 0:
                for cs in range(2):
                    for fs in range(4):
                        inst = t.matmul(p_tf[0:128, cs * 32: cs * 32 + m],
                                        a_u[:, fs * 256 + cs * 128: fs * 256 + cs * 128 + 128],
                                        a_ms[:, fs * 32: fs * 32 + m],
                                        start=(fs == 0), stop=(fs == 3))
                tT.inc(inst)
            for cs in range(2):
                for fs in range(4):
                    inst = t.matmul(p_d[:, cs * 256:(cs + 1) * 256],
                                    a_u[:, fs * 256 + cs * 128: fs * 256 + cs * 128 + 128],
                                    a_xn[:, xb + fs * 256: xb + (fs + 1) * 256],
                                    start=(fs == 0), stop=(fs == 3))
            tT.inc(inst)
        v_wmm = tT.v

        @blk.vector
        def _(v, m=m, v_wmm=v_wmm):
            if m >= 1:
                v.wait_ge(dS.h, v_ch[m - 1][4])
            v.memset(a_scst[:, :], NEG_PAD)
            v.wait_ge(tT.h, v_wmm)
            if m > 0:
                for cs in range(2):
                    inst = v.tensor_scalar(a_scst[:, cs * 33: cs * 33 + m],
                                           p_tf[:, cs * 32: cs * 32 + m],
                                           1.0 / 8192.0, None, OP.mult)
                tV.inc(inst)
            tV.inc(v.tensor_mul(a_wm[:, :], p_d[:, :], c_triw[:, 0:512]))
            v.wait_ge(tV.h, tV.v)
            for cs in range(2):
                inst = v.reduce_sum(a_stat[:, 4 + cs:5 + cs],
                                    a_wm[:, cs * 256:(cs + 1) * 256], axis=AX)
            tV.inc(inst)
            v.wait_ge(tV.h, tV.v)
            for cs in range(2):
                inst = v.tensor_scalar(a_scst[:, cs * 33 + 32: cs * 33 + 33],
                                       a_stat[:, 4 + cs:5 + cs],
                                       c_cnt[:, cs:cs + 1], 1.0 / 32.0, OP.mult, OP.mult)
            tV.inc(inst)
        v_sc = tV.v

        @blk.sync
        def _(sy, m=m, v_prem=v_prem, v_pq=v_pq):
            sy.wait_ge(tV.h, v_prem)
            dS.inc16(sy.dma_start(
                out=pre_in.ap()[:, m * 256:(m + 1) * 256].rearrange("(j p) c -> p j c", p=128),
                in_=a_pre[:, :].rearrange("p (j c) -> p j c", j=4)))
            sy.wait_ge(tV.h, v_pq)
            dS.inc16(sy.dma_start(
                out=pqc.ap()[m * 4 * 128:(m * 4 + 4) * 128, :].rearrange("(j p) c -> p j c", p=128),
                in_=a_pq[:, :].rearrange("p (j c) -> p j c", j=4)))
        v_predma = dS.v

        @blk.sync
        def _(sy, m=m, v_sc=v_sc):
            sy.wait_ge(tV.h, v_sc)
            dS.inc16(sy.dma_start(
                out=sc_in.ap()[m * 256:(m + 1) * 256, :].rearrange("(cs p) n -> p cs n", p=128),
                in_=a_scst[:, :].rearrange("p (cs n) -> p cs n", cs=2)))
        v_scdma = dS.v

        if "pre_d" in dbg_out:
            @blk.gpsimd
            def _(g, m=m, v_prem=v_prem):
                g.wait_ge(tV.h, v_prem)
                dG.inc16(g.dma_start(
                    out=dbg_out["pre_d"][:, m * 256:(m + 1) * 256].rearrange("(j p) c -> p j c", p=128),
                    in_=a_pre[:, :].rearrange("p (j c) -> p j c", j=4)))
        if "bc_d" in dbg_out:
            @blk.gpsimd
            def _(g, m=m, v_bc=v_bc):
                g.wait_ge(tV.h, v_bc)
                dG.inc16(g.dma_start(out=dbg_out["bc_d"][:, m * 1024:(m + 1) * 1024],
                                     in_=a_bc[:, :]))
        if "den_d" in dbg_out:
            @blk.gpsimd
            def _(g, m=m, v_den=v_den, v_prem=v_prem):
                g.wait_ge(tV.h, v_den)
                dG.inc16(g.dma_start(out=dbg_out["den_d"][:, m * 1024:(m + 1) * 1024],
                                     in_=a_den2[:, :]))
        if "zz_d" in dbg_out:
            @blk.gpsimd
            def _(g, m=m, v_zup=v_zup):
                g.wait_ge(tV.h, v_zup)
                dG.inc16(g.dma_start(out=dbg_out["zz_d"][:, m * 1024:(m + 1) * 1024],
                                     in_=a_zbf[:, :]))
        if "atm_d" in dbg_out:
            @blk.gpsimd
            def _(g, m=m, v_intra=v_intra):
                g.wait_ge(tT.h, v_intra)
                dG.inc16(g.dma_start(out=dbg_out["atm_d"][:, m * 1024:(m + 1) * 1024],
                                     in_=a_atm[:, :]))

        v_prem_hist[0] = v_prem
        v_u_hist[0] = v_u
        v_axn_cons[m] = (v_wmm, v_u)
        v_ch[m] = (tT.v, tA.v, tV.v, v_predma, v_scdma)

    if "hist_d" in dbg_out:
        @blk.gpsimd
        def _(g):
            g.wait_ge(tV.h, tV.v)
            dG.inc16(g.dma_start(out=dbg_out["hist_d"], in_=a_hist[:, :]))
    if "pq_d" in dbg_out:
        @blk.gpsimd
        def _(g):
            g.wait_ge(dS.h, dS.v)
            dG.inc16(g.dma_start(out=dbg_out["pq_d"], in_=pqc.ap()[:, :]))
    if "sc_d" in dbg_out:
        @blk.gpsimd
        def _(g):
            g.wait_ge(dS.h, dS.v)
            dG.inc16(g.dma_start(out=dbg_out["sc_d"], in_=sc_in.ap()[:, :]))

    bB = barrier()
    ctxB.close()

    @blk.gpsimd
    def _(g):
        g.wait_ge(dS.h, bB[3])
        cc.inc(g.collective_compute(
            "AllGather", OP.bypass,
            replica_groups=[[0, 1], [2, 3], [4, 5], [6, 7]],
            ins=[pre_in.ap().opt()], outs=[pre_out.ap().opt()]))
        cc.inc(g.collective_compute(
            "AllReduce", OP.add,
            replica_groups=[[0, 1], [2, 3], [4, 5], [6, 7]],
            ins=[sc_in.ap().opt()], outs=[sc_out.ap().opt()]))
    v_cc2 = cc.v

    ctxC = contextlib.ExitStack()

    def sbC(name, shape, dtype):
        return ctxC.enter_context(nc.sbuf_tensor(name, shape, dtype))

    f_k = sbC("f_k", [128, T], BF16)
    f_v = sbC("f_v", [128, 64 * 130], BF16)
    f_q = sbC("f_q", [128, NTC * 512], BF16)
    f_nq = sbC("f_nq", [1, 16], F32)
    f_b2 = sbC("f_b2", [1, 16], BF16)
    f_bcall = sbC("f_bcall", [128, 16], F32)
    f_q2 = sbC("f_q2", [128, 512], BF16)
    f_nk = sbC("f_nk", [1, 16], F32)
    f_rb = sbC("f_rb", [65, 2048], BF16)
    f_dbc = sbC("f_dbc", [64, 512], BF16)
    f_ctx = sbC("f_ctx", [128, 512], BF16)
    f_cx = sbC("f_cx", [128, 2048], F32)
    f_pt = sbC("f_pt", [128, 2 * 1024], BF16)

    qk_v, exp_v, pv_v = {}, {}, {}
    ki = 0
    v_ctxev = None
    v_ctxdma = [0]
    v_B_ctd = [0]
    prev_B = None
    pair_done = None

    for pr in range(4):
        @blk.sync
        def _(sy, pr=pr, pd=pair_done):
            if pr == 0:
                wait_barrier(sy, bA)
                wait_barrier(sy, bB)
            else:
                sy.wait_ge(tT.h, pd[0])
                sy.wait_ge(tV.h, pd[1])
            dS.inc16(sy.dma_start(
                out=f_k[:, :].rearrange("p (tc c) -> p tc c", tc=NTC),
                in_=kf.ap()[pr * NTC * 128:(pr + 1) * NTC * 128, :]
                    .rearrange("(tc p) c -> p tc c", p=128)))
            dS.inc16(sy.dma_start(
                out=f_v[:, :].rearrange("p (kt s) -> p kt s", s=130)[:, :, 0:64],
                in_=vtok.ap()[pr * 64 * 128:(pr + 1) * 64 * 128, :]
                    .rearrange("(kt p) f -> p kt f", p=128)[:, :, 0:64]))
            dS.inc16(sy.dma_start(
                out=f_v[:, :].rearrange("p (kt s) -> p kt s", s=130)[:, :, 65:129],
                in_=vtok.ap()[pr * 64 * 128:(pr + 1) * 64 * 128, :]
                    .rearrange("(kt p) f -> p kt f", p=128)[:, :, 64:128]))
            dS.inc16(sy.dma_start(
                out=f_q[:, :].rearrange("p (tc c) -> p tc c", tc=NTC),
                in_=qf.ap()[pr * NTC * 128:(pr + 1) * NTC * 128, :]
                    .rearrange("(tc p) c -> p tc c", p=128)))
        v_kv = dS.v

        @blk.vector
        def _(v, pr=pr, v_kv=v_kv):
            if pr == 0:
                wait_barrier(v, bB)
            v.wait_ge(dS.h, v_kv)
            v.memset(f_v[:, :].rearrange("p (kt s) -> p kt s", s=130)[:, :, 64:65], 1.0)
            tV.inc(v.memset(f_v[:, :].rearrange("p (kt s) -> p kt s", s=130)[:, :, 129:130], 1.0))
        v_vones = tV.v

        v_nkmm = [None]
        v_nkred = [None]
        for tc in range(NTC):
            @blk.vector
            def _(v, pr=pr, tc=tc, v_kv=v_kv, pd=pair_done):
                v.wait_ge(dS.h, v_kv)
                if tc >= 1:
                    v.wait_ge(tT.h, v_nkmm[0])
                elif pd is not None:
                    v.wait_ge(tT.h, pd[0])
                tV.inc(v.tensor_mul(f_q2[:, :], f_k[:, tc * 512:(tc + 1) * 512],
                                    f_k[:, tc * 512:(tc + 1) * 512]))
            v_k2 = tV.v

            @blk.tensor
            def _(t, pr=pr, tc=tc, v_k2=v_k2, pd=pair_done):
                if pr == 0 and tc == 0:
                    wait_barrier(t, bB)
                t.wait_ge(tV.h, v_k2)
                if tc >= 1:
                    t.wait_ge(tV.h, v_nkred[0])
                elif pr > 0:
                    t.wait_ge(tV.h, pd[1])
                tT.inc(t.matmul(p_tf[0:1, 0:512], ones_col[:, :], f_q2[:, :],
                                start=True, stop=True))
            v_nkmm[0] = tT.v

            @blk.vector
            def _(v, tc=tc, vm=v_nkmm[0]):
                v.wait_ge(tT.h, vm)
                tV.inc(v.reduce_max(f_nk[0:1, tc:tc + 1], p_tf[0:1, 0:512], axis=AX))
            v_nkred[0] = tV.v

        @blk.vector
        def _(v):
            v.wait_ge(tV.h, tV.v)
            tV.inc(v.reduce_max(a_stat[0:1, 4:5], f_nk[0:1, 0:16], axis=AX))
        v_bk2 = tV.v

        v_nqs = {}
        for gi2 in range(NTC):
            @blk.vector
            def _(v, gi2=gi2, v_kv=v_kv):
                v.wait_ge(dS.h, v_kv)
                if gi2 >= 1:
                    v.wait_ge(tT.h, v_nqs[gi2 - 1])
                tV.inc(v.tensor_mul(f_q2[:, :], f_q[:, gi2 * 512:(gi2 + 1) * 512],
                                    f_q[:, gi2 * 512:(gi2 + 1) * 512]))
            v_q2m = tV.v

            @blk.tensor
            def _(t, gi2=gi2, v_q2m=v_q2m):
                t.wait_ge(tV.h, v_q2m)
                if gi2 >= 1:
                    t.wait_ge(tV.h, v_nqr[0])
                tT.inc(t.matmul(p_tf[0:1, 0:512], ones_col[:, :], f_q2[:, :],
                                start=True, stop=True))
            v_nqs[gi2] = tT.v

            @blk.vector
            def _(v, gi2=gi2, vm=v_nqs[gi2]):
                v.wait_ge(tT.h, vm)
                tV.inc(v.reduce_max(f_nq[0:1, gi2:gi2 + 1], p_tf[0:1, 0:512], axis=AX))
            v_nqr = [tV.v]

        @blk.vector
        def _(v, v_bk2=v_bk2):
            v.wait_ge(tV.h, tV.v)
            v.wait_ge(tV.h, v_bk2)
            tV.inc(v.tensor_scalar(a_stat[0:1, 5:6], a_stat[0:1, 4:5], -1.0 / 16.0,
                                   None, OP.mult))
            v.wait_ge(tV.h, tV.v)
            tV.inc(v.tensor_scalar(f_b2[0:1, :], f_nq[0:1, :], -4.0,
                                   a_stat[0:1, 5:6], OP.mult, OP.add))
        v_brow = tV.v

        @blk.tensor
        def _(t, v_brow=v_brow):
            t.wait_ge(tV.h, v_brow)
            tT.inc(t.matmul(p_tf[:, 256:272], ones_row[:, :], f_b2[0:1, :],
                            start=True, stop=True))
        v_bcm = tT.v

        @blk.vector
        def _(v, v_bcm=v_bcm, va=tA.v):
            v.wait_ge(tT.h, v_bcm)
            if va:
                v.wait_ge(tA.h, va)
            tV.inc(v.tensor_copy(f_bcall[:, :], p_tf[:, 256:272]))
        v_bcall = tV.v

        v_bcf = [None]
        exp_base_pair = tA.v
        ki_pair0 = ki
        for kt2 in range(NTC * 64):
            exp_v[ki_pair0 + kt2] = exp_base_pair + kt2 + 1
        for gi in range(NTC):
            ga = gi

            ki0 = ki

            @blk.tensor
            def _(t, ga=ga, ki0=ki0, vb=v_bcall, vce=v_ctxev, vvo=v_vones):
                for kt in range(64):
                    ki_ = ki0 + kt
                    pi = ki_ % 2
                    P = p_a if pi == 0 else p_b
                    if kt == 0:
                        t.wait_ge(tV.h, vb)
                    if ki_ >= 2:
                        t.wait_ge(tA.h, exp_v[ki_ - 2])
                    t.matmul(P[:, 0:512], f_k[0:64, kt * 128:(kt + 1) * 128],
                             f_q[0:64, ga * 512:(ga + 1) * 512], start=True, stop=True,
                             tile_position=(0, 0))
                    tT.inc(t.matmul(P[:, 512:1024], f_k[64:128, kt * 128:(kt + 1) * 128],
                                    f_q[64:128, ga * 512:(ga + 1) * 512], start=True,
                                    stop=True, tile_position=(64, 0)))
                    qk_v[ki_] = tT.v
                    if kt >= 1:
                        kp = ki0 + kt - 1
                        pp = kp % 2
                        t.wait_ge(tA.h, exp_v[kp])
                        if kt == 1:
                            t.wait_ge(tV.h, vvo)
                            if vce is not None:
                                t.wait_ge(tV.h, vce)
                        t.matmul(p_c[0:65, :], f_v[:, (kt - 1) * 130: (kt - 1) * 130 + 65],
                                 f_pt[:, pp * 1024: pp * 1024 + 512],
                                 start=(kt == 1), stop=False)
                        tT.inc(t.matmul(p_d[0:65, :],
                                        f_v[:, (kt - 1) * 130 + 65: (kt - 1) * 130 + 130],
                                        f_pt[:, pp * 1024 + 512:(pp + 1) * 1024],
                                        start=(kt == 1), stop=False))
                        pv_v[kp] = tT.v
                kp = ki0 + 63
                pp = kp % 2
                t.wait_ge(tA.h, exp_v[kp])
                t.matmul(p_c[0:65, :], f_v[:, 63 * 130: 63 * 130 + 65],
                         f_pt[:, pp * 1024: pp * 1024 + 512], start=False, stop=True)
                tT.inc(t.matmul(p_d[0:65, :], f_v[:, 63 * 130 + 65: 63 * 130 + 130],
                                f_pt[:, pp * 1024 + 512:(pp + 1) * 1024],
                                start=False, stop=True))
                pv_v[kp] = tT.v

            ki += 64

            pg = gi % 2
            @blk.vector
            def _(v, kk=ki, pg=pg):
                v.wait_ge(tT.h, pv_v[kk - 1])
                v.tensor_copy(f_cx[0:65, pg * 1024: pg * 1024 + 512], p_c[0:65, :])
                tV.inc(v.tensor_copy(f_cx[0:65, pg * 1024 + 512:(pg + 1) * 1024],
                                     p_d[0:65, :]))
            v_ctxev = tV.v

            @blk.vector
            def _(v, pg=pg, ve=v_ctxev):
                v.wait_ge(tV.h, ve)
                tV.inc(v.reciprocal(f_rb[0:1, pg * 1024: pg * 1024 + 512],
                                    f_cx[64:65, pg * 1024: pg * 1024 + 512]))
                tV.inc(v.reciprocal(f_rb[64:65, pg * 1024 + 512:(pg + 1) * 1024],
                                    f_cx[64:65, pg * 1024 + 512:(pg + 1) * 1024]))
            v_rp1, v_rp2 = tV.v - 1, tV.v

            if prev_B is not None:
                prev_B()

            def mk_B(pr=pr, gi=gi, pg=pg, v_rp1=v_rp1, v_rp2=v_rp2):
                def emit():
                    vprev = v_B_ctd[0]

                    @blk.tensor
                    def _(t):
                        t.wait_ge(tV.h, v_rp1)
                        if vprev:
                            t.wait_ge(tV.h, vprev)
                        tT.inc(t.matmul(p_tf[0:64, 0:512], ones_row[0:1, 0:64],
                                        f_rb[0:1, pg * 1024: pg * 1024 + 512],
                                        start=True, stop=True))
                    v_db1 = tT.v

                    @blk.tensor
                    def _(t):
                        t.wait_ge(tV.h, v_rp2)
                        tT.inc(t.matmul(p_tf[64:128, 0:512], ones_r2[64:65, 0:64],
                                        f_rb[64:65, pg * 1024 + 512:(pg + 1) * 1024],
                                        start=True, stop=True))
                    v_db2 = tT.v

                    @blk.vector
                    def _(v, vcd=v_ctxdma[0]):
                        v.wait_ge(tT.h, v_db1)
                        if vcd:
                            v.wait_ge(dS.h, vcd)
                        tV.inc(v.tensor_copy(f_dbc[:, :], p_tf[0:64, 0:512]))
                        v.wait_ge(tV.h, tV.v)
                        tV.inc(v.tensor_mul(f_ctx[0:64, :],
                                            f_cx[0:64, pg * 1024: pg * 1024 + 512],
                                            f_dbc[:, :]))
                        v.wait_ge(tT.h, v_db2)
                        v.wait_ge(tV.h, tV.v)
                        tV.inc(v.tensor_copy(f_dbc[:, :], p_tf[64:128, 0:512]))
                        v.wait_ge(tV.h, tV.v)
                        tV.inc(v.tensor_mul(f_ctx[64:128, :],
                                            f_cx[0:64, pg * 1024 + 512:(pg + 1) * 1024],
                                            f_dbc[:, :]))
                    v_ctd = tV.v
                    v_B_ctd[0] = v_ctd

                    @blk.sync
                    def _(sy):
                        sy.wait_ge(tV.h, v_ctd)
                        dS.inc16(sy.dma_start(
                            out=ctx_in.ap()[pr * 128:(pr + 1) * 128,
                                            gi * 512:(gi + 1) * 512],
                            in_=f_ctx[:, :]))
                    v_ctxdma[0] = dS.v
                return emit
            prev_B = mk_B()
        prev_B()
        prev_B = None
        @blk.scalar
        def _(s, pr=pr, ki_pair0=ki_pair0):
            if pr == 0:
                wait_barrier(s, bB)
            for kt2 in range(NTC * 64):
                ki_ = ki_pair0 + kt2
                gi2 = kt2 // 64
                pi = ki_ % 2
                P = p_a if pi == 0 else p_b
                s.wait_ge(tT.h, qk_v[ki_])
                if ki_ >= 2:
                    assert pv_v[ki_ - 2] <= qk_v[ki_]
                inst = s.activation(f_pt[:, pi * 1024:(pi + 1) * 1024], P[:, :],
                                    AF.Exp, bias=f_bcall[:, gi2:gi2 + 1])
                tA.inc(inst)
                assert tA.v == exp_v[ki_]
        pair_done = (tT.v, tV.v)

    if "ctx_d" in dbg_out:
        @blk.gpsimd
        def _(g):
            g.wait_ge(dS.h, dS.v)
            dG.inc16(g.dma_start(out=dbg_out["ctx_d"], in_=ctx_in.ap()[:, :]))

    bC = barrier()
    ctxC.close()

    @blk.gpsimd
    def _(g):
        g.wait_ge(dS.h, bC[3])
        cc.inc(g.collective_compute(
            "AllGather", OP.bypass,
            replica_groups=[[0, 1], [2, 3], [4, 5], [6, 7]],
            ins=[ctx_in.ap().opt()], outs=[ctx_out.ap().opt()]))
    v_cc3 = cc.v

    ctxD = contextlib.ExitStack()

    def sbD(name, shape, dtype):
        return ctxD.enter_context(nc.sbuf_tensor(name, shape, dtype))

    g_scs = sbD("g_scs", [128, 64 * 33], F32)
    g_exp = sbD("g_exp", [128, 2 * 33], F32)
    g_mx = sbD("g_mx", [128, 64], F32)
    g_gst = sbD("g_gst", [128, 64 * 33], BF16)
    g_rt = sbD("g_rt", [33, 256], BF16)
    c_pre = sbD("c_pre", [128, 2 * 8 * 256], BF16)
    c_xnf = sbD("c_xnf", [128, 2 * 4 * 256], BF16)
    c_pq = sbD("c_pq", [128, 2 * 4 * 256], BF16)
    c_gbc = sbD("c_gbc", [128, 256], BF16)
    c_spq = sbD("c_spq", [128, 2 * 4 * 256], BF16)
    c_xo = sbD("c_xo", [128, 4 * 256], F32)
    c_s = sbD("c_s", [128, 4 * 256], F32)
    c_ctx = sbD("c_ctx", [128, 2 * 8 * 256], BF16)
    c_xvo = sbD("c_xvo", [128, 256], BF16)
    c_g = sbD("c_g", [128, 4 * 256], F32)
    c_xr = sbD("c_xr", [128, 2 * 2 * 512], F32)
    c_out = sbD("c_out", [128, 2 * 512], F32)

    @blk.sync
    def _(sy):
        wait_barrier(sy, bC)
        sy.wait_ge(cc.h, v_cc2)
        dS.inc16(sy.dma_start(out=g_scs[:, :].rearrange("p (a n) -> p a n", a=NT),
                              in_=sc_out.ap().rearrange("(a p) n -> p a n", p=128)))
    v_scg = dS.v

    v_gt_done = {}
    @blk.vector
    def _(v):
        wait_barrier(v, bC)
        v.wait_ge(dS.h, v_scg)
        for a in range(NT):
            inst = v.reduce_max(g_mx[:, a:a + 1], g_scs[:, a * 33:(a + 1) * 33],
                                axis=AX, negate=True)
        tV.inc(inst)
    v_mxall = tV.v
    dve_base = tV.v
    exp_base = tA.v
    gexp_v = {a: exp_base + a + 1 for a in range(NT)}
    gdve_v = {a: dve_base + 3 * (a + 1) for a in range(NT)}

    @blk.scalar
    def _(s):
        wait_barrier(s, bC)
        s.wait_ge(tV.h, v_mxall)
        for a in range(NT):
            if a >= 2:
                s.wait_ge(tV.h, gdve_v[a - 2])
            inst = s.activation(g_exp[:, (a % 2) * 33:(a % 2) * 33 + 33],
                                g_scs[:, a * 33:(a + 1) * 33], AF.Exp,
                                bias=g_mx[:, a:a + 1])
            tA.inc(inst)
            assert tA.v == gexp_v[a]

    @blk.vector
    def _(v):
        for a in range(NT):
            v.wait_ge(tA.h, gexp_v[a])
            sl = (a % 2) * 33
            tV.inc(v.reduce_sum(st_sm[:, 1:2], g_exp[:, sl:sl + 33], axis=AX))
            v.wait_ge(tV.h, tV.v)
            tV.inc(v.reciprocal(st_sm[:, 2:3], st_sm[:, 1:2]))
            v.wait_ge(tV.h, tV.v)
            tV.inc(v.tensor_scalar(g_gst[:, a * 33:(a + 1) * 33], g_exp[:, sl:sl + 33],
                                   st_sm[:, 2:3], None, OP.mult))
            assert tV.v == gdve_v[a]
            v_gt_done[a] = tV.v

    if "gates_d" in dbg_out:
        @blk.gpsimd
        def _(g):
            g.wait_ge(tV.h, tV.v)
            dG.inc16(g.dma_start(out=dbg_out["gates_d"], in_=g_gst[:, :]))

    v_cm = {}
    v_outdma = [0, 0]
    fin = {}
    v_ldm, v_ldcm = {}, {}

    def emit_ph6_loads(mm, first):
        pm = mm % 2

        @blk.sync
        def _(sy, mm=mm, pm=pm, first=first):
            if first:
                wait_barrier(sy, bC)
                sy.wait_ge(cc.h, v_cc2)
            elif mm >= 2:
                sy.wait_ge(tT.h, fin["wo"])
                sy.wait_ge(tV.h, fin["xo"])
                sy.wait_ge(tV.h, fin["spq"])
                sy.wait_ge(tV.h, fin["out"])
            for dt in range(8):
                dS.inc16(sy.dma_start(
                    out=c_pre[:, pm * 2048 + dt * 256: pm * 2048 + (dt + 1) * 256],
                    in_=pre_out.ap()[dt * 128:(dt + 1) * 128, mm * 256:(mm + 1) * 256]))
            for dt in range(4):
                dS.inc16(sy.dma_start(
                    out=c_xnf[:, pm * 1024 + dt * 256: pm * 1024 + (dt + 1) * 256],
                    in_=xn_tm.ap()[mm * 256:(mm + 1) * 256, dt * 128:(dt + 1) * 128],
                    transpose=True))
            dS.inc16(sy.dma_start(
                out=c_pq[:, pm * 1024:(pm + 1) * 1024].rearrange("p (j c) -> p j c", j=4),
                in_=pqc.ap()[mm * 4 * 128:(mm * 4 + 4) * 128, :]
                    .rearrange("(j p) c -> p j c", p=128)))
            for cs in range(2):
                dS.inc16(sy.dma_start(
                    out=c_xr[:, pm * 1024 + cs * 512: pm * 1024 + (cs + 1) * 512],
                    in_=x[mm * 256 + cs * 128: mm * 256 + (cs + 1) * 128, 0:512]))
        v_ldm[mm] = dS.v

        @blk.sync
        def _(sy, mm=mm, pm=pm, first=first):
            sy.wait_ge(cc.h, v_cc3)
            if not first and mm >= 2:
                sy.wait_ge(tT.h, fin["omm"])
            for dt in range(8):
                dS.inc16(sy.dma_start(
                    out=c_ctx[:, pm * 2048 + dt * 256: pm * 2048 + (dt + 1) * 256],
                    in_=ctx_out.ap()[dt * 128:(dt + 1) * 128, mm * 256:(mm + 1) * 256]))
        v_ldcm[mm] = dS.v

    emit_ph6_loads(0, True)
    for m in range(NCH):
        pm = m % 2
        if m + 1 < NCH:
            emit_ph6_loads(m + 1, False)
        v_ld = v_ldm[m]
        v_ldc = v_ldcm[m]

        @blk.tensor
        def _(t, m=m):
            if m == 0:
                wait_barrier(t, bC)
            t.wait_ge(tV.h, v_gt_done[2 * m + 1])
            if m >= 1:
                t.wait_ge(tV.h, fin["grt_tv"])
            for tt in range(2):
                inst = t.transpose(p_t[0:33, tt * 128:(tt + 1) * 128],
                                   g_gst[:, (2 * m + tt) * 33:(2 * m + tt + 1) * 33],
                                   c_identb[:, :])
            tT.inc(inst)
        v_gtm = tT.v

        @blk.vector
        def _(v, m=m, v_gtm=v_gtm):
            if m == 0:
                wait_barrier(v, bC)
            v.wait_ge(tT.h, v_gtm)
            if m >= 1:
                v.wait_ge(tT.h, fin["grt"])
            tV.inc(v.tensor_copy(g_rt[:, :], p_t[0:33, 0:256]))
        v_grt = tV.v
        fin["grt_tv"] = v_grt

        if m > 0:
            @blk.vector
            def _(v, m=m):
                if "g" in fin:
                    v.wait_ge(tA.h, fin["g"])
                tV.inc(v.memset(p_b[:, :], 0.0))
        v_pbz = tV.v

        @blk.tensor
        def _(t, m=m, v_ld=v_ld, pm=pm):
            t.wait_ge(dS.h, v_ld)
            if m >= 1:
                t.wait_ge(tV.h, fin["xo"])
            for fs in range(4):
                for dt in range(8):
                    inst = t.matmul(p_a[:, fs * 256:(fs + 1) * 256],
                                    w["wo"][:, dt * 512 + fs * 128: dt * 512 + fs * 128 + 128],
                                    c_pre[:, pm * 2048 + dt * 256: pm * 2048 + (dt + 1) * 256],
                                    start=(dt == 0), stop=(dt == 7))
            tT.inc(inst)
        v_wo = tT.v
        fin["wo"] = v_wo

        v_hr = None
        v_gbcev = {}
        v_selv = {}
        v_hrv = {}
        if m == 0:
            fin["spq"] = tV.v
        for n in range(m):
            sel = p_c if n % 2 == 0 else p_d
            sb = (n % 2) * 1024

            @blk.tensor
            def _(t, m=m, n=n, sel=sel, v_grt=v_grt, vz=v_pbz):
                t.wait_ge(tV.h, v_grt)
                if n >= 2:
                    t.wait_ge(tV.h, v_gbcev[n - 2])
                tT.inc(t.matmul(sel[:, 0:256], c_e33[:, n * 128:(n + 1) * 128],
                                g_rt[:, :], start=True, stop=True))
                v_selv[n] = tT.v
                if n >= 1:
                    nn = n - 1
                    t.wait_ge(tV.h, v_gbcev[nn])
                    if nn == 0:
                        t.wait_ge(tV.h, vz)
                    for h in range(8):
                        po, j = hsl(h)
                        inst = t.matmul(p_b[po:po + 64, j * 256:(j + 1) * 256],
                                        a_hist[po:po + 64, (j * NCH + nn) * 64:(j * NCH + nn) * 64 + 64],
                                        c_spq[po:po + 64, (nn % 2) * 1024 + j * 256:
                                              (nn % 2) * 1024 + (j + 1) * 256],
                                        start=False, stop=False, skip_group_check=True)
                    tT.inc(inst)
                    v_hrv[nn] = tT.v

            @blk.vector
            def _(v, m=m, n=n, sel=sel, vs=v_selv[n], pm=pm, sb=sb):
                v.wait_ge(tT.h, vs)
                if n >= 2:
                    v.wait_ge(tT.h, v_hrv[n - 2])
                for j in range(4):
                    inst = v.tensor_mul(c_spq[:, sb + j * 256: sb + (j + 1) * 256],
                                        c_pq[:, pm * 1024 + j * 256: pm * 1024 + (j + 1) * 256],
                                        sel[:, 0:256])
                tV.inc(inst)
            v_gbcev[n] = tV.v
            fin["spq"] = tV.v

        if m > 0:
            @blk.tensor
            def _(t, m=m, vg=v_gbcev[m - 1], vz=v_pbz):
                t.wait_ge(tV.h, vg)
                if m == 1:
                    t.wait_ge(tV.h, vz)
                for h in range(8):
                    po, j = hsl(h)
                    inst = t.matmul(p_b[po:po + 64, j * 256:(j + 1) * 256],
                                    a_hist[po:po + 64, ((j * NCH) + m - 1) * 64:((j * NCH) + m - 1) * 64 + 64],
                                    c_spq[po:po + 64, ((m - 1) % 2) * 1024 + j * 256:
                                          ((m - 1) % 2) * 1024 + (j + 1) * 256],
                                    start=False, stop=True, skip_group_check=True)
                tT.inc(inst)
            v_hr = tT.v

        @blk.vector
        def _(v, m=m, v_wo=v_wo, v_ld=v_ld, pm=pm):
            v.wait_ge(tT.h, v_wo)
            v.wait_ge(dS.h, v_ld)
            tV.inc(v.tensor_add(c_xo[:, :], p_a[:, :],
                                c_xnf[:, pm * 1024:(pm + 1) * 1024]))
        v_xo = tV.v
        fin["xo"] = v_xo

        @blk.tensor
        def _(t, m=m, v_grt=v_grt, v_xo=v_xo):
            t.wait_ge(tV.h, v_grt)
            if m >= 2:
                t.wait_ge(tV.h, v_gbcev.get(m - 2, v_xo))
            sel = p_c if m % 2 == 0 else p_d
            tT.inc(t.matmul(sel[:, 0:256], c_e33[:, 32 * 128:33 * 128], g_rt[:, :],
                            start=True, stop=True))
        v_gcm = tT.v
        fin["grt"] = v_gcm

        @blk.vector
        def _(v, m=m, v_gcm=v_gcm, v_xo=v_xo, v_hr=v_hr):
            v.wait_ge(tT.h, v_gcm)
            if m >= 1:
                v.wait_ge(tT.h, fin["sgtr"])
            sel = p_c if m % 2 == 0 else p_d
            tV.inc(v.tensor_copy(c_gbc[:, :], sel[:, 0:256]))
            v.wait_ge(tV.h, tV.v)
            v.wait_ge(tV.h, v_xo)
            for j in range(4):
                inst = v.tensor_mul(c_s[:, j * 256:(j + 1) * 256],
                                    c_xo[:, j * 256:(j + 1) * 256], c_gbc[:, :])
            tV.inc(inst)
            if m > 0:
                v.wait_ge(tT.h, v_hr)
                v.wait_ge(tV.h, tV.v)
                tV.inc(v.tensor_add(c_s[:, :], c_s[:, :], p_b[:, :]))
        v_s = tV.v

        @blk.tensor
        def _(t, m=m, v_ldc=v_ldc, v_s=v_s, pm=pm):
            t.wait_ge(dS.h, v_ldc)
            t.wait_ge(tV.h, v_s)
            sel = p_d if m % 2 == 0 else p_c
            for dt in range(8):
                inst = t.matmul(sel[:, 0:256], w_vc["o"][:, dt * 128:(dt + 1) * 128],
                                c_ctx[:, pm * 2048 + dt * 256: pm * 2048 + (dt + 1) * 256],
                                start=(dt == 0), stop=(dt == 7))
            tT.inc(inst)
        v_xvo = tT.v

        @blk.vector
        def _(v, m=m, v_xvo=v_xvo):
            v.wait_ge(tT.h, v_xvo)
            sel = p_d if m % 2 == 0 else p_c
            tV.inc(v.tensor_scalar(c_xvo[:, :], sel[:, 0:256], scl128["o"][:, :],
                                   None, OP.mult))
        v_xvoe = tV.v

        @blk.tensor
        def _(t, m=m, v_xvoe=v_xvoe, v_s=v_s, pm=pm):
            t.wait_ge(tV.h, v_xvoe)
            t.wait_ge(tV.h, v_s)
            for fs in range(4):
                for dt in range(8):
                    t.matmul(p_b[:, fs * 256:(fs + 1) * 256],
                             w["fo"][:, dt * 512 + fs * 128: dt * 512 + fs * 128 + 128],
                             c_ctx[:, pm * 2048 + dt * 256: pm * 2048 + (dt + 1) * 256],
                             start=(dt == 0), stop=False)
                inst = t.matmul(p_b[:, fs * 256:(fs + 1) * 256],
                                w_uc["o"][:, fs * 128:(fs + 1) * 128], c_xvo[:, :],
                                start=False, stop=True)
            tT.inc(inst)
        v_omm = tT.v
        fin["omm"] = v_omm

        @blk.scalar
        def _(s, m=m, v_omm=v_omm):
            if m == 0:
                wait_barrier(s, bC)
            s.wait_ge(tT.h, v_omm)
            for fs in range(4):
                inst = s.activation(c_g[:, fs * 256:(fs + 1) * 256],
                                    p_b[:, fs * 256:(fs + 1) * 256], AF.Silu,
                                    bias=b_o[:, fs:fs + 1])
            tA.inc(inst)
        v_g = tA.v
        fin["g"] = v_g

        @blk.vector
        def _(v, m=m, v_g=v_g, v_s=v_s):
            v.wait_ge(tA.h, v_g)
            tV.inc(v.tensor_mul(c_s[:, :], c_s[:, :], c_g[:, :]))
        v_sg = tV.v

        if "s_d" in dbg_out or "g_d" in dbg_out:
            @blk.gpsimd
            def _(g, m=m, v_sg=v_sg):
                g.wait_ge(tV.h, v_sg)
                if "s_d" in dbg_out:
                    dG.inc16(g.dma_start(out=dbg_out["s_d"][:, m * 1024:(m + 1) * 1024],
                                         in_=c_s[:, :]))
                if "g_d" in dbg_out:
                    dG.inc16(g.dma_start(out=dbg_out["g_d"][:, m * 1024:(m + 1) * 1024],
                                         in_=c_g[:, :]))

        v_oev = []
        for cs in range(2):
            @blk.tensor
            def _(t, m=m, cs=cs, v_sg=v_sg):
                t.wait_ge(tV.h, v_sg)
                if cs == 1:
                    t.wait_ge(tV.h, v_oev[0])
                for j in range(4):
                    inst = t.transpose(p_tf[:, j * 128:(j + 1) * 128],
                                       c_s[:, j * 256 + cs * 128: j * 256 + cs * 128 + 128],
                                       c_identf[:, :])
                tT.inc(inst)
            v_trm = tT.v
            fin["sgtr"] = v_trm

            @blk.vector
            def _(v, m=m, cs=cs, v_trm=v_trm, v_ld=v_ld, pm=pm):
                v.wait_ge(tT.h, v_trm)
                v.wait_ge(dS.h, v_ld)
                if v_outdma[cs]:
                    v.wait_ge(dS.h, v_outdma[cs])
                tV.inc(v.tensor_add(c_out[:, cs * 512:(cs + 1) * 512], p_tf[:, 0:512],
                                    c_xr[:, pm * 1024 + cs * 512: pm * 1024 + (cs + 1) * 512]))
            v_oev.append(tV.v)

            @blk.sync
            def _(sy, m=m, cs=cs, vo=v_oev[cs]):
                sy.wait_ge(tV.h, vo)
                dS.inc16(sy.dma_start(
                    out=out[m * 256 + cs * 128: m * 256 + (cs + 1) * 128, :],
                    in_=c_out[:, cs * 512:(cs + 1) * 512]))
            v_outdma[cs] = dS.v

        fin["out"] = v_oev[1]
        v_cm[m] = (tT.v, tA.v, tV.v)

    @blk.sync
    def _(sy):
        sy.wait_ge(dS.h, dS.v)
        sy.wait_ge(dG.h, dG.v)
        sy.wait_ge(tT.h, tT.v)
        sy.wait_ge(tA.h, tA.v)
        sy.wait_ge(tV.h, tV.v)
        sy.wait_ge(cc.h, cc.v)

    ctxD.close()
    ctxG.close()
    return nc, dbg_out


def _host_inputs(inputs, c):
    b, hg = c // 2, c % 2
    F = np.arange(512 * hg, 512 * hg + 512)
    notF = np.arange(512 * (1 - hg), 512 * (1 - hg) + 512)
    perm = np.concatenate([F, notF])
    g = lambda n: np.asarray(inputs[n], np.float32)
    d = {}
    d["x"] = np.ascontiguousarray(g("L0_attn_x")[b][:, perm])
    d["wqT"] = np.ascontiguousarray(g("L0_attn_Wq").T[perm][:, F])
    d["wkT"] = np.ascontiguousarray(g("L0_attn_Wk").T[perm][:, F])
    d["wvT"] = np.ascontiguousarray(g("L0_attn_Wv").T[perm][:, F])
    d["wuT"] = np.ascontiguousarray(g("L0_attn_Wu").T[perm][:, F])
    d["woT"] = np.ascontiguousarray(g("L0_attn_Wo").T[:, F])
    for p in "qkv":
        d[f"f{p}T"] = np.ascontiguousarray(g(f"L0_ffn_{p}Wm").T[perm][:, F])
        d[f"vcT_{p}"] = np.ascontiguousarray(g(f"L0_ffn_{p}V").reshape(ER, D).T[perm])
    d["foT"] = np.ascontiguousarray(g("L0_ffn_oWm").T[:, F])
    d["vcT_o"] = np.ascontiguousarray(g("L0_ffn_oV").reshape(ER, D).T)
    for p in "qkvo":
        d[f"ucat_{p}"] = np.ascontiguousarray(
            g(f"L0_ffn_{p}U").transpose(0, 2, 1).reshape(ER, D)[:, F])
        d[f"srow_{p}"] = np.ascontiguousarray(g(f"L0_ffn_{p}S").reshape(1, ER))
    for p in "qko":
        d[f"bcol_{p}"] = np.ascontiguousarray(g(f"L0_ffn_{p}b")[F].reshape(4, 128).T)
    d["vb_row"] = np.ascontiguousarray(g("L0_ffn_vb")[F].reshape(1, 512))
    d["nw_pa"] = np.ascontiguousarray(g("L0_attn_norm_w")[perm].reshape(8, 128).T)
    d["gw1T"] = np.ascontiguousarray(g("L0_ffn_gw1").T[perm])
    d["gb1"] = np.ascontiguousarray(g("L0_ffn_gb1").reshape(1, GH))
    d["gw2T"] = np.ascontiguousarray(g("L0_ffn_gw2").T)
    d["gb2"] = np.ascontiguousarray(g("L0_ffn_gb2").reshape(1, E))
    d["identb"] = np.eye(128, dtype=ml_dtypes.bfloat16)
    d["identf"] = np.eye(128, dtype=np.float32)
    tri = (np.arange(SEG)[:, None] >= np.arange(SEG)[None, :])
    d["triw"] = tri.astype(np.float32)
    d["trik"] = np.ascontiguousarray(tri.T.astype(ml_dtypes.bfloat16))
    d["cntinv"] = np.ascontiguousarray(
        (1.0 / np.arange(1, SEG + 1, dtype=np.float32)).reshape(2, 128).T)
    e33a = np.zeros((33, 33 * 128), ml_dtypes.bfloat16)
    for n in range(33):
        e33a[n, n * 128:(n + 1) * 128] = 1
    d["e33"] = e33a
    e16a = np.zeros((E, 128), ml_dtypes.bfloat16)
    for e in range(E):
        e16a[e, e * 16:(e + 1) * 16] = 1
    d["e16"] = e16a
    return d


_CACHE = {}


def kernel(**inputs):
    key = tuple(DBG)
    if key not in _CACHE:
        _CACHE[key] = build_nc(dbg=DBG)
    nc, dbg_out = _CACHE[key]
    in_maps = [_host_inputs(inputs, c) for c in range(8)]
    res = run_bass_kernel_spmd(nc, in_maps, core_ids=list(range(8)))
    outa = np.empty((B, T, D), np.float32)
    for c in range(8):
        b, hg = c // 2, c % 2
        outa[b, :, 512 * hg:512 * hg + 512] = res.results[c]["out"]
    kernel.last_results = res.results
    return outa


# version 19
# speedup vs baseline: 1.4061x; 1.0003x over previous
import contextlib
import numpy as np
import ml_dtypes
import concourse.bass as bass
import concourse.mybir as mybir
from concourse.bass_utils import run_bass_kernel_spmd
from concourse.alu_op_type import AluOpType as OP

F32 = mybir.dt.float32
BF16 = mybir.dt.bfloat16
AF = mybir.ActivationFunctionType
AX = mybir.AxisListType.X

B, T, D = 4, 8192, 1024
H, DK, SEG, NCH = 16, 64, 256, 32
NT = T // 128
NTC = T // 512
E, R, ER, GH = 8, 16, 128, 256
SCALE = float(DK) ** -0.5
EPS_LN, EPS_RMS, EPS_DEN = 1e-5, 1e-6, 1e-6
NEG_PAD = -5e29

DBG = []


class Ctr:
    def __init__(self, nc, ctx, name):
        self.h = ctx.enter_context(nc.semaphore(name))
        self.v = 0

    def inc(self, inst, k=1):
        inst.then_inc(self.h, k)
        self.v += k
        return self.v

    def inc16(self, inst):
        return self.inc(inst, 16)


def build_nc(dbg=()):
    nc = bass.Bass("TRN2", target_bir_lowering=False, debug=False)

    def din(name, shape, dtype=F32):
        return nc.dram_tensor(name, shape, dtype, kind="ExternalInput").ap()

    x = din("x", [T, D])
    wT = {n: din(n, [D, 512]) for n in ("wqT", "wkT", "wvT", "wuT", "woT",
                                        "fqT", "fkT", "fvT", "foT")}
    vcT = {p: din(f"vcT_{p}", [D, ER]) for p in "qkvo"}
    ucat = {p: din(f"ucat_{p}", [ER, 512]) for p in "qkvo"}
    bcol = {p: din(f"bcol_{p}", [128, 4]) for p in "qko"}
    vb_row = din("vb_row", [1, 512])
    srow = {p: din(f"srow_{p}", [1, ER]) for p in "qkvo"}
    nw_pa = din("nw_pa", [128, 8])
    gw1T, gb1 = din("gw1T", [D, GH]), din("gb1", [1, GH])
    gw2T, gb2 = din("gw2T", [GH, E]), din("gb2", [1, E])
    identb = din("identb", [128, 128], BF16)
    identf = din("identf", [128, 128], F32)
    triw = din("triw", [SEG, SEG])
    trik = din("trik", [SEG, SEG], BF16)
    cntinv = din("cntinv", [128, 2])
    e33 = din("e33", [33, 33 * 128], BF16)
    e16 = din("e16", [E, 128], BF16)

    out = nc.dram_tensor("out", [T, 512], F32, kind="ExternalOutput").ap()

    def dint(name, shape, dtype):
        return nc.dram_tensor(name, shape, dtype)

    xn_tm = dint("xn_tm", [T, D], BF16)
    qf = dint("qf", [4 * NTC * 128, 512], BF16)
    kf = dint("kf", [4 * NTC * 128, 512], BF16)
    vtok = dint("vtok", [4 * 64 * 128, 128], BF16)
    pqc = dint("pqc", [NCH * 4 * 128, 256], BF16)
    pre_in = dint("pre_in", [512, T], BF16)
    pre_out = dint("pre_out", [1024, T], BF16)
    ctx_in = dint("ctx_in", [512, T], BF16)
    ctx_out = dint("ctx_out", [1024, T], BF16)
    sc_in = dint("sc_in", [T, 33], F32)
    sc_out = dint("sc_out", [T, 33], F32)

    dbg_out = {}
    for name, shape in [
        ("xn_d", [T, D]), ("pooled_d", [128, 8]), ("gate_d", [1, 8]),
        ("rms_d", [1, T]), ("qf_d", [4 * NTC * 128, 512]),
        ("kf_d", [4 * NTC * 128, 512]), ("vt_d", [4 * 64 * 128, 128]),
        ("pre_d", [512, T]), ("sc_d", [T, 33]), ("ctx_d", [512, T]),
        ("hist_d", [128, 4 * NCH * 64]), ("gates_d", [128, 64 * 33]),
        ("s_d", [128, NCH * 1024]), ("g_d", [128, NCH * 1024]),
        ("pq_d", [NCH * 4 * 128, 256]), ("den_d", [128, NCH * 1024]),
        ("bc_d", [128, NCH * 1024]), ("num_d", [128, NCH * 1024]),
        ("zz_d", [128, NCH * 1024]), ("atm_d", [128, NCH * 1024]),
    ]:
        if name in dbg:
            dbg_out[name] = nc.dram_tensor(name, shape, F32, kind="ExternalOutput").ap()

    ctxG = contextlib.ExitStack()

    def sbG(name, shape, dtype):
        return ctxG.enter_context(nc.sbuf_tensor(name, shape, dtype))

    def psG(name, shape, dtype=F32):
        return ctxG.enter_context(nc.psum_tensor(name, shape, dtype))

    w = {n: sbG("w_" + n, [128, 8 * 512], BF16) for n in
         ("wq", "wk", "wv", "wu", "wo", "fq", "fk", "fv", "fo")}
    w_vc = {p: sbG(f"w_vc_{p}", [128, 8 * 128], BF16) for p in "qkvo"}
    w_uc = {p: sbG(f"w_uc_{p}", [128, 512], BF16) for p in "qkvo"}
    w_g1 = sbG("w_g1", [128, 8 * 256], BF16)
    w_g2 = sbG("w_g2", [128, 2 * 8], BF16)
    c_identb = sbG("c_identb", [128, 128], BF16)
    c_identf = sbG("c_identf", [128, 128], F32)
    c_triw = sbG("c_triw", [128, 2 * 256], F32)
    c_trik = sbG("c_trik", [128, 2 * 256], BF16)
    c_cnt = sbG("c_cnt", [128, 2], F32)
    c_e33 = sbG("c_e33", [33, 33 * 128], BF16)
    c_e16 = sbG("c_e16", [E, 128], BF16)
    b_q = sbG("b_q", [128, 4], F32)
    b_k = sbG("b_k", [128, 4], F32)
    b_o = sbG("b_o", [128, 4], F32)
    b_v = sbG("b_v", [1, 512], BF16)
    c_nw = sbG("c_nw", [128, 8], F32)
    c_gb1 = sbG("c_gb1", [1, GH], F32)
    c_gb2 = sbG("c_gb2", [1, E], F32)
    c_srow = {p: sbG(f"c_srow_{p}", [1, ER], F32) for p in "qkvo"}
    ones_col = sbG("ones_col", [128, 1], BF16)
    c_eps = sbG("c_eps", [128, 1], F32)
    ones_row = sbG("ones_row", [1, 128], BF16)
    ones_r2 = sbG("ones_r2", [65, 128], BF16)
    ones_sq = sbG("ones_sq", [128, 128], BF16)
    scl128 = {p: sbG(f"scl128_{p}", [128, 1], F32) for p in "qkvo"}
    ge128 = sbG("ge128", [128, 1], F32)
    rms_row = sbG("rms_row", [1, T], BF16)
    a_hist = sbG("a_hist", [128, 4 * NCH * 64], BF16)
    a_scum = sbG("a_scum", [128, 4 * 64], F32)
    a_scbf = sbG("a_scbf", [128, 4 * 64], BF16)
    a_z = sbG("a_z", [128, 4], F32)
    a_stat = sbG("a_stat", [128, 8], F32)
    st_sm = sbG("st_sm", [128, 12], F32)

    p_a = psG("p_a", [128, 1024])
    p_b = psG("p_b", [128, 1024])
    p_c = psG("p_c", [128, 512])
    p_d = psG("p_d", [128, 512])
    p_t = psG("p_t", [128, 512], BF16)
    p_tf = psG("p_tf", [128, 512])

    sems = {n: Ctr(nc, ctxG, n) for n in ("dS", "dG", "tT", "tA", "tV", "cc")}
    dS, dG, tT, tA, tV, cc = (sems[n] for n in ("dS", "dG", "tT", "tA", "tV", "cc"))

    ctxG.enter_context(nc.allow_low_precision(
        reason="bf16 matmul operand stream with fp32 accumulation by design; "
               "tolerance bar measured for bf16 matmuls"))
    blk = ctxG.enter_context(nc.Block())

    def barrier():
        return (tT.v, tA.v, tV.v, dS.v, dG.v)

    def wait_barrier(eng, bv):
        vt, va, vv, vs, vg = bv
        if vt: eng.wait_ge(tT.h, vt)
        if va: eng.wait_ge(tA.h, va)
        if vv: eng.wait_ge(tV.h, vv)
        if vs: eng.wait_ge(dS.h, vs)
        if vg: eng.wait_ge(dG.h, vg)

    def hsl(h):
        return 64 * (h % 2), h // 2

    @blk.gpsimd
    def _(g):
        names = {"wq": "wqT", "wk": "wkT", "wv": "wvT", "wu": "wuT",
                 "wo": "woT", "fq": "fqT", "fk": "fkT", "fv": "fvT", "fo": "foT"}
        for n, sn in names.items():
            dG.inc16(g.dma_start(out=w[n][:, :].rearrange("p (a f) -> p a f", a=8),
                                 in_=wT[sn].rearrange("(a p) f -> p a f", p=128)))
        for p in "qkvo":
            dG.inc16(g.dma_start(out=w_vc[p][:, :].rearrange("p (a f) -> p a f", a=8),
                                 in_=vcT[p].rearrange("(a p) f -> p a f", p=128)))
            dG.inc16(g.dma_start(out=w_uc[p][:, :], in_=ucat[p]))
            dG.inc16(g.dma_start(out=c_srow[p][:, :], in_=srow[p]))
        dG.inc16(g.dma_start(out=w_g1[:, :].rearrange("p (a f) -> p a f", a=8),
                             in_=gw1T.rearrange("(a p) f -> p a f", p=128)))
        dG.inc16(g.dma_start(out=w_g2[:, :].rearrange("p (a f) -> p a f", a=2),
                             in_=gw2T.rearrange("(a p) f -> p a f", p=128)))
        dG.inc16(g.dma_start(out=c_identb[:, :], in_=identb))
        dG.inc16(g.dma_start(out=c_identf[:, :], in_=identf))
        dG.inc16(g.dma_start(out=c_triw[:, :].rearrange("p (a c) -> p a c", a=2),
                             in_=triw.rearrange("(a p) c -> p a c", p=128)))
        dG.inc16(g.dma_start(out=c_trik[:, :].rearrange("p (a c) -> p a c", a=2),
                             in_=trik.rearrange("(a p) c -> p a c", p=128)))
        dG.inc16(g.dma_start(out=c_cnt[:, :], in_=cntinv))
        dG.inc16(g.dma_start(out=c_e33[:, :], in_=e33))
        dG.inc16(g.dma_start(out=c_e16[:, :], in_=e16))
        for dst, src in [(b_q, bcol["q"]), (b_k, bcol["k"]), (b_o, bcol["o"])]:
            dG.inc16(g.dma_start(out=dst[:, :], in_=src))
        dG.inc16(g.dma_start(out=b_v[:, :], in_=vb_row))
        dG.inc16(g.dma_start(out=c_nw[:, :], in_=nw_pa))
        dG.inc16(g.dma_start(out=c_gb1[:, :], in_=gb1))
        dG.inc16(g.dma_start(out=c_gb2[:, :], in_=gb2))
    v_w = dG.v

    @blk.vector
    def _(v):
        v.memset(ones_col[:, :], 1.0)
        v.memset(ones_row[:, :], 1.0)
        v.memset(ones_r2[:, :], 1.0)
        v.memset(ones_sq[:, :], 1.0)
        v.memset(a_scum[:, :], 0.0)
        v.memset(a_scbf[:, :], 0.0)
        v.memset(a_z[:, :], 0.0)
        v.memset(c_eps[:, :], EPS_LN)
        tV.inc(v.memset(p_c[:, 0:8], 0.0))
    v_ones = tV.v

    ctxA = contextlib.ExitStack()

    def sbA(name, shape, dtype):
        return ctxA.enter_context(nc.sbuf_tensor(name, shape, dtype))

    x_t = sbA("x_t", [128, 4 * 1024], F32)
    sq_scr = sbA("sq_scr", [128, 1024], BF16)
    xn_t = sbA("xn_t", [128, 4 * 1024], BF16)
    pool_bf = sbA("pool_bf", [128, 8], BF16)
    g_sc = sbA("g_sc", [1, 64], F32)
    g_h = sbA("g_h", [1, GH], F32)
    g_hb = sbA("g_hb", [128, 3], BF16)
    xf = sbA("xf", [128, 2 * 8 * 512], BF16)
    x2 = sbA("x2", [128, 8 * 512], BF16)
    xvs = sbA("xvs", [128, 512], BF16)
    prj = sbA("prj", [128, 4 * 512], BF16)
    prj2 = sbA("prj2", [128, 4 * 512], BF16)
    rms_scr = sbA("rms_scr", [1, 512], F32)
    rms_sq = sbA("rms_sq", [1, 512], F32)

    NB = 4
    v_xnw, v_xndma, v_pool = {}, {}, {}
    v_statdone = {}
    for it in range(NT // NB):
        @blk.sync
        def _(sy, it=it):
            if it >= 1:
                sy.wait_ge(tV.h, v_xnw[it - 1])
            for ti in range(NB):
                i = it * NB + ti
                dS.inc16(sy.dma_start(out=x_t[:, ti * 1024:(ti + 1) * 1024],
                                      in_=x[i * 128:(i + 1) * 128, :]))
        v_x = dS.v

        @blk.scalar
        def _(s, it=it, v_x=v_x):
            s.wait_ge(dS.h, v_x)
            if it == 0:
                s.wait_ge(tV.h, v_ones)
            if it >= 1:
                s.wait_ge(tV.h, v_statdone[it - 1])
            for ti in range(NB):
                inst = s.activation(sq_scr[:, :], x_t[:, ti * 1024:(ti + 1) * 1024],
                                    AF.Square, accum_out=st_sm[:, 4 + ti:5 + ti])
            tA.inc(inst)
        v_ssq = tA.v

        @blk.vector
        def _(v, it=it, v_x=v_x, v_ssq=v_ssq):
            v.wait_ge(dS.h, v_x)
            if it >= 1:
                v.wait_ge(dS.h, v_xndma[it - 1])
                v.wait_ge(tT.h, v_pool[it - 1])
            for ti in range(NB):
                inst = v.reduce_sum(st_sm[:, ti:ti + 1],
                                    x_t[:, ti * 1024:(ti + 1) * 1024], axis=AX)
            tV.inc(inst)
            v.wait_ge(tV.h, tV.v)
            tV.inc(v.tensor_scalar(st_sm[:, 0:4], st_sm[:, 0:4], 1.0 / D, None, OP.mult))
            v.wait_ge(tV.h, tV.v)
            tV.inc(v.tensor_mul(st_sm[:, 8:12], st_sm[:, 0:4], st_sm[:, 0:4]))
            v.wait_ge(tA.h, v_ssq)
            v.wait_ge(tV.h, tV.v)
            tV.inc(v.scalar_tensor_tensor(st_sm[:, 4:8], st_sm[:, 4:8], 1.0 / D,
                                          st_sm[:, 8:12], OP.mult, OP.subtract))
        v_var = tV.v

        @blk.scalar
        def _(s, v_var=v_var):
            s.wait_ge(tV.h, v_var)
            tA.inc(s.activation(st_sm[:, 4:8], st_sm[:, 4:8], AF.Sqrt, bias=c_eps[:, :]))
        v_sd = tA.v

        @blk.vector
        def _(v, it=it, v_sd=v_sd):
            v.wait_ge(tA.h, v_sd)
            tV.inc(v.reciprocal(st_sm[:, 8:12], st_sm[:, 4:8]))
            v.wait_ge(tV.h, tV.v)
            for ti in range(NB):
                inst = v.tensor_scalar(xn_t[:, ti * 1024:(ti + 1) * 1024],
                                       x_t[:, ti * 1024:(ti + 1) * 1024],
                                       st_sm[:, ti:ti + 1], st_sm[:, 8 + ti:9 + ti],
                                       OP.subtract, OP.mult)
            tV.inc(inst)
        v_xnw[it] = tV.v
        v_statdone[it] = tV.v

        @blk.tensor
        def _(t, it=it, vx=v_xnw[it]):
            t.wait_ge(tV.h, vx)
            for ti in range(NB):
                for ds in range(8):
                    inst = t.matmul(p_c[:, ds:ds + 1],
                                    xn_t[:, ti * 1024 + ds * 128: ti * 1024 + (ds + 1) * 128],
                                    ones_col[:, :], start=False,
                                    stop=(it == NT // NB - 1 and ti == NB - 1))
            tT.inc(inst)
        v_pool[it] = tT.v

        @blk.sync
        def _(sy, it=it, vx=v_xnw[it]):
            sy.wait_ge(tV.h, vx)
            dS.inc16(sy.dma_start(
                out=xn_tm.ap()[it * NB * 128:(it + 1) * NB * 128, :]
                    .rearrange("(ti p) c -> p ti c", p=128),
                in_=xn_t[:, :].rearrange("p (ti c) -> p ti c", ti=NB)))
        v_xndma[it] = dS.v
    v_xn_all = dS.v
    if "xn_d" in dbg_out:
        @blk.gpsimd
        def _(g):
            g.wait_ge(dS.h, v_xn_all)
            dG.inc16(g.dma_start(out=dbg_out["xn_d"], in_=xn_tm.ap()[:, :]))

    @blk.vector
    def _(v):
        v.wait_ge(tT.h, v_pool[NT // 4 - 1])
        v.wait_ge(dG.h, v_w)
        tV.inc(v.tensor_scalar(pool_bf[:, :], p_c[:, 0:8], 1.0 / T, None, OP.mult))
    v_poolbf = tV.v

    @blk.tensor
    def _(t):
        t.wait_ge(tV.h, v_poolbf)
        for ds in range(8):
            inst = t.matmul(p_d[0:1, 0:GH], pool_bf[:, ds:ds + 1],
                            w_g1[:, ds * 256:(ds + 1) * 256], start=(ds == 0), stop=(ds == 7))
        tT.inc(inst)
    v_g1 = tT.v

    @blk.vector
    def _(v):
        v.wait_ge(tT.h, v_g1)
        tV.inc(v.tensor_add(g_h[:, :], p_d[0:1, 0:GH], c_gb1[:, :]))
        v.wait_ge(tV.h, tV.v)
        tV.inc(v.tensor_relu(g_h[:, :], g_h[:, :]))
    v_h = tV.v

    @blk.tensor
    def _(t):
        t.wait_ge(tV.h, v_h)
        t.transpose(p_tf[:, 0:1], g_h[0:1, 0:128], c_identf[0:1, 0:1])
        tT.inc(t.transpose(p_tf[:, 1:2], g_h[0:1, 128:256], c_identf[0:1, 0:1]))
    v_ht = tT.v

    @blk.vector
    def _(v):
        v.wait_ge(tT.h, v_ht)
        tV.inc(v.tensor_copy(g_hb[:, 0:2], p_tf[:, 0:2]))
    v_hb = tV.v

    @blk.tensor
    def _(t):
        t.wait_ge(tV.h, v_hb)
        t.matmul(p_d[0:1, 256:256 + E], g_hb[:, 0:1], w_g2[:, 0:8], start=True, stop=False)
        tT.inc(t.matmul(p_d[0:1, 256:256 + E], g_hb[:, 1:2], w_g2[:, 8:16],
                        start=False, stop=True))
    v_g2 = tT.v

    @blk.vector
    def _(v):
        v.wait_ge(tT.h, v_g2)
        tV.inc(v.tensor_add(g_sc[:, 0:8], p_d[0:1, 256:256 + E], c_gb2[:, :]))
        v.wait_ge(tV.h, tV.v)
        tV.inc(v.reduce_max(g_sc[:, 8:9], g_sc[:, 0:8], axis=AX))
        v.wait_ge(tV.h, tV.v)
        tV.inc(v.tensor_scalar(g_sc[:, 16:24], g_sc[:, 0:8], g_sc[:, 8:9], None, OP.is_ge))
        v.wait_ge(tV.h, tV.v)
        tV.inc(v.scalar_tensor_tensor(g_sc[:, 24:32], g_sc[:, 16:24], -1e30,
                                      g_sc[:, 0:8], OP.mult, OP.add))
        v.wait_ge(tV.h, tV.v)
        tV.inc(v.reduce_max(g_sc[:, 9:10], g_sc[:, 24:32], axis=AX))
        v.wait_ge(tV.h, tV.v)
        tV.inc(v.tensor_scalar(g_sc[:, 24:32], g_sc[:, 24:32], g_sc[:, 9:10], None, OP.is_ge))
        tV.inc(v.tensor_sub(g_sc[:, 32:33], g_sc[:, 8:9], g_sc[:, 9:10]))
    v_top = tV.v

    @blk.scalar
    def _(s):
        s.wait_ge(tV.h, v_top)
        tA.inc(s.activation(g_sc[:, 32:33], g_sc[:, 32:33], AF.Sigmoid))
    v_sig = tA.v

    @blk.vector
    def _(v):
        v.wait_ge(tA.h, v_sig)
        tV.inc(v.tensor_scalar(g_sc[:, 40:48], g_sc[:, 16:24], g_sc[:, 32:33], None, OP.mult))
        tV.inc(v.scalar_tensor_tensor(g_sc[:, 48:56], g_sc[:, 24:32], g_sc[:, 32:33],
                                      g_sc[:, 24:32], OP.mult, OP.subtract))
        v.wait_ge(tV.h, tV.v)
        tV.inc(v.tensor_sub(g_sc[:, 40:48], g_sc[:, 40:48], g_sc[:, 48:56]))
    v_gate = tV.v

    @blk.tensor
    def _(t):
        t.wait_ge(tV.h, v_gate)
        tT.inc(t.transpose(p_tf[0:8, 2:3], g_sc[0:1, 40:48], c_identf[0:1, 0:1]))
        for ji, p in enumerate("qkvo"):
            tT.inc(t.transpose(p_tf[:, 3 + ji:4 + ji], c_srow[p][0:1, :],
                               c_identf[0:1, 0:1]))
    v_gt = tT.v

    @blk.vector
    def _(v):
        v.wait_ge(tT.h, v_gt)
        tV.inc(v.tensor_copy(g_hb[0:8, 2:3], p_tf[0:8, 2:3]))
    v_gtb = tV.v

    @blk.tensor
    def _(t):
        t.wait_ge(tV.h, v_gtb)
        tT.inc(t.matmul(p_tf[:, 7:8], c_e16[:, :], g_hb[0:E, 2:3], start=True, stop=True))
    v_ge = tT.v

    @blk.vector
    def _(v):
        v.wait_ge(tT.h, v_ge)
        tV.inc(v.tensor_copy(ge128[:, :], p_tf[:, 7:8]))
        v.wait_ge(tV.h, tV.v)
        for ji, p in enumerate("qkvo"):
            inst = v.tensor_mul(scl128[p][:, :], ge128[:, :], p_tf[:, 3 + ji:4 + ji])
        tV.inc(inst)
        if "gate_d" in dbg_out:
            tV.inc(v.tensor_copy(g_h[0:1, 0:8], g_sc[0:1, 40:48]))
    v_scl = tV.v

    if "pooled_d" in dbg_out or "gate_d" in dbg_out:
        @blk.gpsimd
        def _(g):
            if "pooled_d" in dbg_out:
                g.wait_ge(tV.h, v_poolbf)
                dG.inc16(g.dma_start(out=dbg_out["pooled_d"], in_=pool_bf[:, :]))
            if "gate_d" in dbg_out:
                g.wait_ge(tV.h, v_scl)
                dG.inc16(g.dma_start(out=dbg_out["gate_d"], in_=g_h[0:1, 0:8]))

    v_tc_pe, v_tc_ve, v_prjdma, v_prj2dma, v_rms = {}, {}, {}, {}, {}
    last_pd_read = v_top
    last_xvs = [v_scl]
    last_pab_evac = [v_scl]
    for tc in range(NTC):
        fa = tc % 2
        xo = fa * 4096

        @blk.sync
        def _(sy, tc=tc, xo=xo):
            if tc >= 2:
                sy.wait_ge(tT.h, v_tc_pe[tc - 2])
                sy.wait_ge(tV.h, v_tc_ve[tc - 2])
            sy.wait_ge(dS.h, v_xn_all)
            for dt in range(8):
                inst = sy.dma_start(
                    out=xf[:, xo + dt * 512: xo + (dt + 1) * 512],
                    in_=xn_tm.ap()[tc * 512:(tc + 1) * 512, dt * 128:(dt + 1) * 128],
                    transpose=True)
                dS.inc16(inst)
        v_xf = dS.v

        @blk.vector
        def _(v, tc=tc, xo=xo, v_xf=v_xf):
            v.wait_ge(dS.h, v_xf)
            if tc >= 1:
                v.wait_ge(tT.h, v_rms[tc - 1])
            tV.inc(v.tensor_mul(x2[:, :], xf[:, xo:xo + 4096], xf[:, xo:xo + 4096]))
        v_x2 = tV.v

        @blk.tensor
        def _(t, tc=tc, v_x2=v_x2, lpd=last_pd_read):
            t.wait_ge(tV.h, v_x2)
            t.wait_ge(tV.h, lpd)
            for dt in range(8):
                inst = t.matmul(p_d[0:1, 0:512], ones_col[:, :],
                                x2[:, dt * 512:(dt + 1) * 512],
                                start=(dt == 0), stop=(dt == 7))
            tT.inc(inst)
        v_rms[tc] = tT.v

        @blk.vector
        def _(v, vr=v_rms[tc]):
            v.wait_ge(tT.h, vr)
            tV.inc(v.tensor_scalar(rms_scr[:, :], p_d[0:1, 0:512], 1.0 / D, EPS_RMS,
                                   OP.mult, OP.add))
        v_rsc = tV.v
        last_pd_read = v_rsc

        @blk.scalar
        def _(s, v_rsc=v_rsc):
            s.wait_ge(tV.h, v_rsc)
            tA.inc(s.activation(rms_sq[:, :], rms_scr[:, :], AF.Sqrt))
        v_rq = tA.v

        @blk.vector
        def _(v, tc=tc, v_rq=v_rq):
            v.wait_ge(tA.h, v_rq)
            tV.inc(v.reciprocal(rms_row[0:1, tc * 512:(tc + 1) * 512], rms_sq[:, :]))

        for pi, p in enumerate("qkv"):
            wm = w["f" + p]

            @blk.tensor
            def _(t, tc=tc, xo=xo, p=p, pi=pi, lxv=last_xvs[0]):
                if tc == 0 and pi == 0:
                    t.wait_ge(tV.h, v_scl)
                    t.wait_ge(tV.h, v_poolbf)
                else:
                    t.wait_ge(tV.h, lxv)
                for dt in range(8):
                    inst = t.matmul(p_c[:, 0:512], w_vc[p][:, dt * 128:(dt + 1) * 128],
                                    xf[:, xo + dt * 512: xo + (dt + 1) * 512],
                                    start=(dt == 0), stop=(dt == 7))
                tT.inc(inst)
            v_xv = tT.v

            @blk.vector
            def _(v, p=p, v_xv=v_xv):
                v.wait_ge(tT.h, v_xv)
                tV.inc(v.tensor_scalar(xvs[:, :], p_c[:, 0:512], scl128[p][:, :],
                                       None, OP.mult))
            last_xvs = [tV.v]
            v_xvs = tV.v

            if p in "qk":
                stg = prj if p == "q" else prj2

                @blk.tensor
                def _(t, tc=tc, xo=xo, p=p, wm=wm, v_xvs=v_xvs, lpe=last_pab_evac[0]):
                    t.wait_ge(tV.h, v_xvs)
                    t.wait_ge(tV.h, lpe)
                    for fs in range(4):
                        pp = p_a if fs < 2 else p_b
                        po = (fs % 2) * 512
                        for dt in range(8):
                            t.matmul(pp[:, po:po + 512],
                                     wm[:, dt * 512 + fs * 128: dt * 512 + fs * 128 + 128],
                                     xf[:, xo + dt * 512: xo + (dt + 1) * 512],
                                     start=(dt == 0), stop=False)
                        inst = t.matmul(pp[:, po:po + 512],
                                        w_uc[p][:, fs * 128:(fs + 1) * 128], xvs[:, :],
                                        start=False, stop=True)
                    tT.inc(inst)
                v_mm = tT.v

                @blk.vector
                def _(v, tc=tc, p=p, stg=stg, v_mm=v_mm):
                    v.wait_ge(tT.h, v_mm)
                    if tc >= 1:
                        v.wait_ge(dS.h, (v_prjdma if p == "q" else v_prj2dma)[tc - 1])
                    for fs in range(4):
                        pp = p_a if fs < 2 else p_b
                        po = (fs % 2) * 512
                        if p == "q":
                            inst = v.tensor_scalar(stg[:, fs * 512:(fs + 1) * 512],
                                                   pp[:, po:po + 512], b_q[:, fs:fs + 1],
                                                   SCALE, OP.add, OP.mult)
                        else:
                            inst = v.tensor_scalar(stg[:, fs * 512:(fs + 1) * 512],
                                                   pp[:, po:po + 512], b_k[:, fs:fs + 1],
                                                   None, OP.add)
                    tV.inc(inst)
                v_stg = tV.v
                last_pab_evac = [v_stg]

                @blk.sync
                def _(sy, tc=tc, p=p, stg=stg, v_stg=v_stg):
                    sy.wait_ge(tV.h, v_stg)
                    dst = qf if p == "q" else kf
                    for fs in range(4):
                        inst = sy.dma_start(
                            out=dst.ap()[(fs * NTC + tc) * 128:(fs * NTC + tc) * 128 + 128, :],
                            in_=stg[:, fs * 512:(fs + 1) * 512])
                        dS.inc16(inst)
                if p == "q":
                    v_prjdma[tc] = dS.v
                else:
                    v_prj2dma[tc] = dS.v

                if ("qf_d" in dbg_out and p == "q") or ("kf_d" in dbg_out and p == "k"):
                    @blk.gpsimd
                    def _(g, tc=tc, p=p, stg=stg, v_stg=v_stg):
                        g.wait_ge(tV.h, v_stg)
                        dd = dbg_out["qf_d" if p == "q" else "kf_d"]
                        for fs in range(4):
                            dG.inc16(g.dma_start(
                                out=dd[(fs * NTC + tc) * 128:(fs * NTC + tc) * 128 + 128, :],
                                in_=stg[:, fs * 512:(fs + 1) * 512]))
            else:
                @blk.tensor
                def _(t, tc=tc, xo=xo, wm=wm, v_xvs=v_xvs, lpe=last_pab_evac[0]):
                    t.wait_ge(tV.h, v_xvs)
                    t.wait_ge(tV.h, lpe)
                    t.wait_ge(tV.h, v_ones)
                    for mt in range(4):
                        pp = p_a if mt < 2 else p_b
                        po = (mt % 2) * 512
                        for dt in range(8):
                            t.matmul(pp[:, po:po + 512],
                                     xf[:, xo + dt * 512 + mt * 128: xo + dt * 512 + mt * 128 + 128],
                                     wm[:, dt * 512:(dt + 1) * 512],
                                     start=(dt == 0), stop=False)
                        t.matmul(pp[:, po:po + 512], xvs[:, mt * 128:(mt + 1) * 128],
                                 w_uc["v"][:, :], start=False, stop=False)
                        inst = t.matmul(pp[:, po:po + 512], ones_row[:, :], b_v[:, :],
                                        start=False, stop=True)
                    tT.inc(inst)
                v_vmm = tT.v

                @blk.vector
                def _(v, tc=tc, v_vmm=v_vmm):
                    v.wait_ge(tT.h, v_vmm)
                    v.wait_ge(dS.h, v_prjdma[tc])
                    for mt in range(4):
                        pp = p_a if mt < 2 else p_b
                        po = (mt % 2) * 512
                        inst = v.tensor_copy(prj[:, mt * 512:(mt + 1) * 512], pp[:, po:po + 512])
                    tV.inc(inst)
                v_vst = tV.v
                last_pab_evac = [v_vst]

                @blk.sync
                def _(sy, tc=tc, v_vst=v_vst):
                    sy.wait_ge(tV.h, v_vst)
                    for pr in range(4):
                        inst = sy.dma_start(
                            out=vtok.ap()[(pr * 64 + tc * 4) * 128:(pr * 64 + tc * 4 + 4) * 128, :]
                                .rearrange("(mt p) f -> p mt f", p=128),
                            in_=prj[:, :].rearrange("p (mt f) -> p mt f", mt=4)[
                                :, :, pr * 128:(pr + 1) * 128])
                        dS.inc16(inst)

                if "vt_d" in dbg_out:
                    @blk.gpsimd
                    def _(g, tc=tc, v_vst=v_vst):
                        g.wait_ge(tV.h, v_vst)
                        for pr in range(4):
                            dG.inc16(g.dma_start(
                                out=dbg_out["vt_d"][(pr * 64 + tc * 4) * 128:(pr * 64 + tc * 4 + 4) * 128, :]
                                    .rearrange("(mt p) f -> p mt f", p=128),
                                in_=prj[:, :].rearrange("p (mt f) -> p mt f", mt=4)[
                                    :, :, pr * 128:(pr + 1) * 128]))
        v_tc_pe[tc] = tT.v
        v_tc_ve[tc] = tV.v

    if "rms_d" in dbg_out:
        @blk.gpsimd
        def _(g):
            g.wait_ge(tV.h, tV.v)
            dG.inc16(g.dma_start(out=dbg_out["rms_d"], in_=rms_row[:, :]))

    bA = barrier()
    ctxA.close()

    ctxB = contextlib.ExitStack()

    def sbB(name, shape, dtype):
        return ctxB.enter_context(nc.sbuf_tensor(name, shape, dtype))

    a_xn = sbB("a_xn", [128, 2 * 8 * 256], BF16)
    a_xn2 = sbB("a_xn2", [128, 8 * 256], BF16)
    a_rbc = sbB("a_rbc", [128, 256], BF16)
    a_pq = sbB("a_pq", [128, 4 * 256], BF16)
    a_pk = sbB("a_pk", [128, 4 * 256], BF16)
    a_vc = sbB("a_vc", [128, 4 * 256], BF16)
    a_el = sbB("a_el", [128, 4 * 256], BF16)
    a_pkT = sbB("a_pkT", [128, 2 * 512], BF16)
    a_vcT = sbB("a_vcT", [128, 2 * 512], BF16)
    a_atm = sbB("a_atm", [128, 2 * 512], BF16)
    a_zbf = sbB("a_zbf", [128, 4 * 256], BF16)
    a_pt = sbB("a_pt", [128, 4 * 256], BF16)
    a_u = sbB("a_u", [128, 4 * 256], BF16)
    a_den = sbB("a_den", [128, 4 * 256], F32)
    a_den2 = sbB("a_den2", [128, 4 * 256], BF16)
    a_den3 = sbB("a_den3", [128, 4 * 256], F32)
    a_bc = sbB("a_bc", [128, 4 * 256], BF16)
    a_pre = sbB("a_pre", [128, 4 * 256], BF16)
    a_ms = sbB("a_ms", [128, 4 * 32], BF16)
    a_scst = sbB("a_scst", [128, 2 * 33], F32)
    a_wm = sbB("a_wm", [128, 2 * 256], F32)

    v_ch = {}
    v_axn_d = {}
    v_axn_cons = {}
    v_rbcd, v_xn2d = {}, {}
    v_prem_hist = [0]
    v_u_hist = [0]

    def emit_head(mm, v_pc_free):
        @blk.tensor
        def _(t, mm=mm, v_pc_free=v_pc_free):
            if mm == 0:
                wait_barrier(t, bA)
            elif v_pc_free:
                t.wait_ge(tV.h, v_pc_free)
            tT.inc(t.matmul(p_c[:, 0:256], ones_row[:, :],
                            rms_row[0:1, mm * 256:(mm + 1) * 256],
                            start=True, stop=True))
        v_rbcm = tT.v
        v_rbcd[mm] = v_rbcm

        @blk.vector
        def _(v, mm=mm, v_rbcm=v_rbcm):
            if mm == 0:
                wait_barrier(v, bA)
            v.wait_ge(tT.h, v_rbcm)
            tV.inc(v.tensor_copy(a_rbc[:, :], p_c[:, 0:256]))
            v.wait_ge(dS.h, v_axn_d[mm])
            v.wait_ge(tV.h, tV.v)
            if mm >= 1:
                v.wait_ge(tT.h, v_vmm_hist[0])
            xb_ = (mm % 2) * 2048
            for dt in range(8):
                inst = v.scalar_tensor_tensor(a_xn2[:, dt * 256:(dt + 1) * 256],
                                              a_xn[:, xb_ + dt * 256: xb_ + (dt + 1) * 256],
                                              c_nw[:, dt:dt + 1], a_rbc[:, :],
                                              OP.mult, OP.mult)
            tV.inc(inst)
        v_xn2d[mm] = tV.v
    v_vmm_hist = [0]

    @blk.sync
    def _(sy):
        wait_barrier(sy, bA)
        for dt in range(8):
            inst = sy.dma_start(
                out=a_xn[:, dt * 256:(dt + 1) * 256],
                in_=xn_tm.ap()[0:256, dt * 128:(dt + 1) * 128],
                transpose=True)
            dS.inc16(inst)
    v_axn_d[0] = dS.v

    for m in range(NCH):
        xb = (m % 2) * 2048

        @blk.sync
        def _(sy, m=m):
            if m + 1 < NCH:
                xb2 = ((m + 1) % 2) * 2048
                if m >= 1:
                    ca, cb = v_axn_cons[m - 1]
                    sy.wait_ge(tT.h, ca)
                    sy.wait_ge(tV.h, cb)
                for dt in range(8):
                    inst = sy.dma_start(
                        out=a_xn[:, xb2 + dt * 256: xb2 + (dt + 1) * 256],
                        in_=xn_tm.ap()[(m + 1) * 256:(m + 2) * 256, dt * 128:(dt + 1) * 128],
                        transpose=True)
                    dS.inc16(inst)
                v_axn_d[m + 1] = dS.v
        v_axn = v_axn_d[m]

        if m == 0:
            emit_head(0, None)
        v_rbc = v_rbcd[m]
        v_xn2 = v_xn2d[m]

        @blk.tensor
        def _(t, v_xn2=v_xn2, vp=v_prem_hist[0]):
            t.wait_ge(tV.h, v_xn2)
            if vp:
                t.wait_ge(tV.h, vp)
            for fs in range(4):
                for dt in range(8):
                    inst = t.matmul(p_a[:, fs * 256:(fs + 1) * 256],
                                    w["wq"][:, dt * 512 + fs * 128: dt * 512 + fs * 128 + 128],
                                    a_xn2[:, dt * 256:(dt + 1) * 256],
                                    start=(dt == 0), stop=(dt == 7))
            tT.inc(inst)
        v_qmm = tT.v

        @blk.vector
        def _(v, v_qmm=v_qmm):
            v.wait_ge(tT.h, v_qmm)
            tV.inc(v.tensor_scalar(a_el[:, :], p_a[:, :], 0.0, None, OP.min))
        v_min = tV.v

        @blk.scalar
        def _(s, m=m, v_min=v_min):
            if m == 0:
                wait_barrier(s, bA)
            s.wait_ge(tV.h, v_min)
            tA.inc(s.activation(a_el[:, :], a_el[:, :], AF.Exp))
        v_exp = tA.v

        @blk.vector
        def _(v, m=m, v_exp=v_exp):
            v.wait_ge(tA.h, v_exp)
            if m >= 1:
                v.wait_ge(dS.h, v_ch[m - 1][3])
            tV.inc(v.scalar_tensor_tensor(a_pq[:, :], p_a[:, :], 0.0, a_el[:, :],
                                          OP.max, OP.add))
        v_pq = tV.v

        @blk.tensor
        def _(t, v_xn2=v_xn2, vu=v_u_hist[0]):
            if vu:
                t.wait_ge(tV.h, vu)
            for fs in range(4):
                for dt in range(8):
                    inst = t.matmul(p_b[:, fs * 256:(fs + 1) * 256],
                                    w["wk"][:, dt * 512 + fs * 128: dt * 512 + fs * 128 + 128],
                                    a_xn2[:, dt * 256:(dt + 1) * 256],
                                    start=(dt == 0), stop=(dt == 7))
            tT.inc(inst)
        v_kmm = tT.v

        @blk.vector
        def _(v, v_kmm=v_kmm, v_pq=v_pq):
            v.wait_ge(tT.h, v_kmm)
            v.wait_ge(tV.h, v_pq)
            tV.inc(v.tensor_scalar(a_el[:, :], p_b[:, :], 0.0, None, OP.min))
        v_mink = tV.v

        @blk.scalar
        def _(s, v_mink=v_mink):
            s.wait_ge(tV.h, v_mink)
            tA.inc(s.activation(a_el[:, :], a_el[:, :], AF.Exp))
        v_expk = tA.v

        @blk.vector
        def _(v, v_expk=v_expk):
            v.wait_ge(tA.h, v_expk)
            tV.inc(v.scalar_tensor_tensor(a_pk[:, :], p_b[:, :], 0.0, a_el[:, :],
                                          OP.max, OP.add))
        v_pk = tV.v

        @blk.tensor
        def _(t, v_pq=v_pq, v_xn2=v_xn2):
            t.wait_ge(tV.h, v_pq)
            for fs in range(4):
                for dt in range(8):
                    inst = t.matmul(p_a[:, fs * 256:(fs + 1) * 256],
                                    w["wv"][:, dt * 512 + fs * 128: dt * 512 + fs * 128 + 128],
                                    a_xn2[:, dt * 256:(dt + 1) * 256],
                                    start=(dt == 0), stop=(dt == 7))
            tT.inc(inst)
        v_vmm = tT.v

        @blk.vector
        def _(v, v_vmm=v_vmm):
            v.wait_ge(tT.h, v_vmm)
            tV.inc(v.tensor_copy(a_vc[:, :], p_a[:, :]))
        v_vc = tV.v
        v_vmm_hist[0] = v_vmm
        if m + 1 < NCH:
            emit_head(m + 1, v_vc)

        tr_jobs = [("pk", fs, ct) for ct in range(2) for fs in range(4)] + \
                  [("vc", fs, ct) for ct in range(2) for fs in range(4)]
        v_trev = []
        v_zmm_l = [None]
        for r in range(4):
            @blk.tensor
            def _(t, r=r, v_pk=v_pk, v_vc=v_vc):
                t.wait_ge(tV.h, v_pk if r < 2 else v_vc)
                if r == 2:
                    t.wait_ge(tV.h, v_vc)
                if r >= 1:
                    t.wait_ge(tV.h, v_trev[r - 1])
                for ji in range(4):
                    which, fs, ct = tr_jobs[r * 4 + ji]
                    src = a_pk if which == "pk" else a_vc
                    inst = t.transpose(p_t[:, ji * 128:(ji + 1) * 128],
                                       src[:, fs * 256 + ct * 128: fs * 256 + ct * 128 + 128],
                                       c_identb[:, :])
                tT.inc(inst)
            v_trm = tT.v

            @blk.vector
            def _(v, r=r, v_trm=v_trm):
                v.wait_ge(tT.h, v_trm)
                for ji in range(4):
                    which, fs, ct = tr_jobs[r * 4 + ji]
                    dst = a_pkT if which == "pk" else a_vcT
                    inst = v.tensor_copy(dst[:, ct * 512 + fs * 128: ct * 512 + fs * 128 + 128],
                                         p_t[:, ji * 128:(ji + 1) * 128])
                tV.inc(inst)
            v_trev.append(tV.v)
            if r == 1:
                @blk.tensor
                def _(t, v_vc=v_vc, v_tr=v_trev[-1]):
                    t.wait_ge(tV.h, v_vc)
                    t.wait_ge(tV.h, v_tr)
                    for h in range(8):
                        po, j = hsl(h)
                        for ct in range(2):
                            inst = t.matmul(p_a[po:po + 64, j * 256:(j + 1) * 256],
                                            a_pkT[:, ct * 512 + j * 128 + po: ct * 512 + j * 128 + po + 64],
                                            c_trik[:, ct * 256:(ct + 1) * 256],
                                            start=(ct == 0), stop=(ct == 1))
                    tT.inc(inst)
                v_zmm_l[0] = tT.v
        v_tr = v_trev[-1]
        v_zmm = v_zmm_l[0]

        @blk.vector
        def _(v, v_zmm=v_zmm):
            v.wait_ge(tT.h, v_zmm)
            for j in range(4):
                inst = v.tensor_scalar(a_zbf[:, j * 256:(j + 1) * 256],
                                       p_a[:, j * 256:(j + 1) * 256],
                                       a_z[:, j:j + 1], None, OP.add)
            tV.inc(inst)
            v.wait_ge(tV.h, tV.v)
            tV.inc(v.memset(p_a[:, :], 0.0))
            v.wait_ge(tV.h, tV.v)
            tV.inc(v.tensor_mul(a_pt[:, :], a_pq[:, :], a_zbf[:, :]))
            v.wait_ge(tV.h, tV.v)
            for j in range(4):
                inst = v.reduce_sum(a_stat[:, j:j + 1], a_pk[:, j * 256:(j + 1) * 256], axis=AX)
            tV.inc(inst)
            v.wait_ge(tV.h, tV.v)
            tV.inc(v.tensor_add(a_z[:, 0:4], a_z[:, 0:4], a_stat[:, 0:4]))
        v_zup = tV.v

        @blk.tensor
        def _(t, v_tr=v_tr):
            t.wait_ge(tV.h, v_tr)
            for h in range(8):
                po, j = hsl(h)
                for ct in range(2):
                    inst = t.matmul(p_tf[po:po + 64, j * 64:(j + 1) * 64],
                                    a_pkT[:, ct * 512 + j * 128 + po: ct * 512 + j * 128 + po + 64],
                                    a_vcT[:, ct * 512 + j * 128 + po: ct * 512 + j * 128 + po + 64],
                                    start=(ct == 0), stop=(ct == 1))
            tT.inc(inst)
        v_sch = tT.v

        @blk.tensor
        def _(t, v_zup=v_zup):
            t.wait_ge(tV.h, v_zup)
            for h in range(8):
                po, j = hsl(h)
                inst = t.matmul(p_b[po:po + 1, j * 256:(j + 1) * 256],
                                ones_col[po:po + 64, 0:1],
                                a_pt[po:po + 64, j * 256:(j + 1) * 256],
                                start=True, stop=True,
                                tile_position=(po, po))
            tT.inc(inst)
        v_den_mm = tT.v

        @blk.vector
        def _(v, v_den_mm=v_den_mm):
            v.wait_ge(tT.h, v_den_mm)
            tV.inc(v.tensor_scalar(a_den[:, :], p_b[:, :], EPS_DEN, None, OP.add))
        v_den_ev = tV.v

        @blk.tensor
        def _(t, v_zup=v_zup):
            t.wait_ge(tV.h, v_zup)
            for h in range(8):
                po, j = hsl(h)
                inst = t.matmul(p_a[po:po + 64, j * 256:(j + 1) * 256],
                                a_scbf[po:po + 64, j * 64:(j + 1) * 64],
                                a_pq[po:po + 64, j * 256:(j + 1) * 256],
                                start=False, stop=False, skip_group_check=True)
            tT.inc(inst)
        v_inter = tT.v

        @blk.vector
        def _(v, m=m, v_sch=v_sch, v_inter=v_inter):
            v.wait_ge(tT.h, v_sch)
            v.wait_ge(tT.h, v_inter)
            tV.inc(v.tensor_add(a_scum[:, :], a_scum[:, :], p_tf[:, 0:256]))
            v.wait_ge(tV.h, tV.v)
            tV.inc(v.tensor_copy(a_scbf[:, :], a_scum[:, :]))
            v.wait_ge(tV.h, tV.v)
            for j in range(4):
                inst = v.tensor_copy(a_hist[:, (j * NCH + m) * 64:(j * NCH + m) * 64 + 64],
                                     a_scum[:, j * 64:(j + 1) * 64])
            tV.inc(inst)
        v_scup = tV.v

        v_at_ev = [None, None]
        horder = [3, 2, 1, 0, 7, 6, 5, 4]

        def emit_atm(h):
            po, j = hsl(h)
            reg = h % 2

            @blk.tensor
            def _(t, h=h, po=po, j=j, reg=reg, v_den_ev=v_den_ev):
                if v_at_ev[reg] is None:
                    t.wait_ge(tV.h, v_den_ev)
                else:
                    t.wait_ge(tV.h, v_at_ev[reg])
                for ds in range(2):
                    inst = t.matmul(p_b[:, reg * 512 + ds * 256: reg * 512 + (ds + 1) * 256],
                                    a_pk[po:po + 64, j * 256 + ds * 128: j * 256 + ds * 128 + 128],
                                    a_pq[po:po + 64, j * 256:(j + 1) * 256],
                                    start=True, stop=True)
                tT.inc(inst)
            v_atm = tT.v

            @blk.vector
            def _(v, reg=reg, v_atm=v_atm):
                v.wait_ge(tT.h, v_atm)
                tV.inc(v.tensor_mul(a_atm[:, reg * 512:(reg + 1) * 512],
                                    p_b[:, reg * 512:(reg + 1) * 512],
                                    c_trik[:, 0:512]))
            v_at_ev[reg] = tV.v

        def emit_vct(h):
            po, j = hsl(h)
            reg = h % 2

            @blk.tensor
            def _(t, h=h, po=po, j=j, reg=reg, vae=v_at_ev[reg]):
                t.wait_ge(tV.h, vae)
                for ds in range(2):
                    inst = t.matmul(p_a[po:po + 64, j * 256:(j + 1) * 256],
                                    a_vcT[:, ds * 512 + j * 128 + po: ds * 512 + j * 128 + po + 64],
                                    a_atm[:, reg * 512 + ds * 256: reg * 512 + (ds + 1) * 256],
                                    start=False, stop=(ds == 1), skip_group_check=True)
                tT.inc(inst)

        emit_atm(horder[0])
        for i in range(1, 8):
            emit_atm(horder[i])
            emit_vct(horder[i - 1])
        emit_vct(horder[7])
        v_intra = tT.v

        @blk.vector
        def _(v, v_den_ev=v_den_ev):
            v.wait_ge(tV.h, v_den_ev)
            tV.inc(v.reciprocal(a_den2[:, 0:512], a_den[:, 0:512]))
            tV.inc(v.reciprocal(a_den2[:, 512:1024], a_den[:, 512:1024]))
        v_den1, v_den = tV.v - 1, tV.v

        @blk.tensor
        def _(t, v_den1=v_den1, v_den=v_den, v_scup=v_scup):
            t.wait_ge(tV.h, v_den1)
            t.wait_ge(tV.h, v_scup)
            for h in (0, 1, 2, 3):
                po, j = hsl(h)
                inst = t.matmul(p_tf[po:po + 64, (j % 2) * 256:(j % 2) * 256 + 256],
                                ones_sq[po:po + 1, 0:64],
                                a_den2[po:po + 1, j * 256:(j + 1) * 256],
                                start=True, stop=True)
            t.wait_ge(tV.h, v_den)
            for h in (4, 5, 6, 7):
                po, j = hsl(h)
                inst = t.matmul(p_c[po:po + 64, (j % 2) * 256:(j % 2) * 256 + 256],
                                ones_sq[po:po + 1, 0:64],
                                a_den2[po:po + 1, j * 256:(j + 1) * 256],
                                start=True, stop=True)
            tT.inc(inst)
        v_bcm = tT.v

        @blk.vector
        def _(v, v_bcm=v_bcm):
            v.wait_ge(tT.h, v_bcm)
            for h in range(8):
                po, j = hsl(h)
                pp, co = (p_tf, (j % 2) * 256) if h < 4 else (p_c, (j % 2) * 256)
                inst = v.tensor_copy(a_bc[po:po + 64, j * 256:(j + 1) * 256],
                                     pp[po:po + 64, co:co + 256])
            tV.inc(inst)
        v_bc = tV.v

        @blk.vector
        def _(v, m=m, v_intra=v_intra, v_bc=v_bc):
            v.wait_ge(tT.h, v_intra)
            v.wait_ge(tV.h, v_bc)
            if m >= 1:
                v.wait_ge(dS.h, v_ch[m - 1][3])
            if "num_d" in dbg_out:
                tV.inc(v.tensor_copy(a_den[:, :], p_a[:, :]))
            tV.inc(v.tensor_mul(a_pre[:, :], p_a[:, :], a_bc[:, :]))
        v_prem = tV.v
        if "num_d" in dbg_out:
            @blk.gpsimd
            def _(g, m=m, v_prem=v_prem):
                g.wait_ge(tV.h, v_prem)
                dG.inc16(g.dma_start(out=dbg_out["num_d"][:, m * 1024:(m + 1) * 1024],
                                     in_=a_den[:, :]))

        @blk.tensor
        def _(t, va0=v_at_ev[0], va1=v_at_ev[1], v_axn=v_axn, xb=xb):
            t.wait_ge(tV.h, va0)
            t.wait_ge(tV.h, va1)
            for fs in range(4):
                for dt in range(8):
                    inst = t.matmul(p_b[:, fs * 256:(fs + 1) * 256],
                                    w["wu"][:, dt * 512 + fs * 128: dt * 512 + fs * 128 + 128],
                                    a_xn[:, xb + dt * 256: xb + (dt + 1) * 256],
                                    start=(dt == 0), stop=(dt == 7))
            tT.inc(inst)
        v_umm = tT.v

        @blk.vector
        def _(v, m=m, v_umm=v_umm, xb=xb):
            v.wait_ge(tT.h, v_umm)
            tV.inc(v.tensor_copy(a_u[:, :], p_b[:, :]))
            for j in range(4):
                inst = v.reduce_sum(a_ms[:, j * 32 + m: j * 32 + m + 1],
                                    a_xn[:, xb + j * 256: xb + (j + 1) * 256], axis=AX)
            tV.inc(inst)
        v_u = tV.v

        @blk.tensor
        def _(t, m=m, v_u=v_u, v_bc=v_bc, xb=xb):
            t.wait_ge(tV.h, v_u)
            t.wait_ge(tV.h, v_bc)
            if m > 0:
                for cs in range(2):
                    for fs in range(4):
                        inst = t.matmul(p_tf[0:128, cs * 32: cs * 32 + m],
                                        a_u[:, fs * 256 + cs * 128: fs * 256 + cs * 128 + 128],
                                        a_ms[:, fs * 32: fs * 32 + m],
                                        start=(fs == 0), stop=(fs == 3))
                tT.inc(inst)
            for cs in range(2):
                for fs in range(4):
                    inst = t.matmul(p_d[:, cs * 256:(cs + 1) * 256],
                                    a_u[:, fs * 256 + cs * 128: fs * 256 + cs * 128 + 128],
                                    a_xn[:, xb + fs * 256: xb + (fs + 1) * 256],
                                    start=(fs == 0), stop=(fs == 3))
            tT.inc(inst)
        v_wmm = tT.v

        @blk.vector
        def _(v, m=m, v_wmm=v_wmm):
            if m >= 1:
                v.wait_ge(dS.h, v_ch[m - 1][4])
            v.memset(a_scst[:, :], NEG_PAD)
            v.wait_ge(tT.h, v_wmm)
            if m > 0:
                for cs in range(2):
                    inst = v.tensor_scalar(a_scst[:, cs * 33: cs * 33 + m],
                                           p_tf[:, cs * 32: cs * 32 + m],
                                           1.0 / 8192.0, None, OP.mult)
                tV.inc(inst)
            tV.inc(v.tensor_mul(a_wm[:, :], p_d[:, :], c_triw[:, 0:512]))
            v.wait_ge(tV.h, tV.v)
            for cs in range(2):
                inst = v.reduce_sum(a_stat[:, 4 + cs:5 + cs],
                                    a_wm[:, cs * 256:(cs + 1) * 256], axis=AX)
            tV.inc(inst)
            v.wait_ge(tV.h, tV.v)
            for cs in range(2):
                inst = v.tensor_scalar(a_scst[:, cs * 33 + 32: cs * 33 + 33],
                                       a_stat[:, 4 + cs:5 + cs],
                                       c_cnt[:, cs:cs + 1], 1.0 / 32.0, OP.mult, OP.mult)
            tV.inc(inst)
        v_sc = tV.v

        @blk.sync
        def _(sy, m=m, v_prem=v_prem, v_pq=v_pq):
            sy.wait_ge(tV.h, v_prem)
            dS.inc16(sy.dma_start(
                out=pre_in.ap()[:, m * 256:(m + 1) * 256].rearrange("(j p) c -> p j c", p=128),
                in_=a_pre[:, :].rearrange("p (j c) -> p j c", j=4)))
            sy.wait_ge(tV.h, v_pq)
            dS.inc16(sy.dma_start(
                out=pqc.ap()[m * 4 * 128:(m * 4 + 4) * 128, :].rearrange("(j p) c -> p j c", p=128),
                in_=a_pq[:, :].rearrange("p (j c) -> p j c", j=4)))
        v_predma = dS.v

        @blk.sync
        def _(sy, m=m, v_sc=v_sc):
            sy.wait_ge(tV.h, v_sc)
            dS.inc16(sy.dma_start(
                out=sc_in.ap()[m * 256:(m + 1) * 256, :].rearrange("(cs p) n -> p cs n", p=128),
                in_=a_scst[:, :].rearrange("p (cs n) -> p cs n", cs=2)))
        v_scdma = dS.v

        if "pre_d" in dbg_out:
            @blk.gpsimd
            def _(g, m=m, v_prem=v_prem):
                g.wait_ge(tV.h, v_prem)
                dG.inc16(g.dma_start(
                    out=dbg_out["pre_d"][:, m * 256:(m + 1) * 256].rearrange("(j p) c -> p j c", p=128),
                    in_=a_pre[:, :].rearrange("p (j c) -> p j c", j=4)))
        if "bc_d" in dbg_out:
            @blk.gpsimd
            def _(g, m=m, v_bc=v_bc):
                g.wait_ge(tV.h, v_bc)
                dG.inc16(g.dma_start(out=dbg_out["bc_d"][:, m * 1024:(m + 1) * 1024],
                                     in_=a_bc[:, :]))
        if "den_d" in dbg_out:
            @blk.gpsimd
            def _(g, m=m, v_den=v_den, v_prem=v_prem):
                g.wait_ge(tV.h, v_den)
                dG.inc16(g.dma_start(out=dbg_out["den_d"][:, m * 1024:(m + 1) * 1024],
                                     in_=a_den2[:, :]))
        if "zz_d" in dbg_out:
            @blk.gpsimd
            def _(g, m=m, v_zup=v_zup):
                g.wait_ge(tV.h, v_zup)
                dG.inc16(g.dma_start(out=dbg_out["zz_d"][:, m * 1024:(m + 1) * 1024],
                                     in_=a_zbf[:, :]))
        if "atm_d" in dbg_out:
            @blk.gpsimd
            def _(g, m=m, v_intra=v_intra):
                g.wait_ge(tT.h, v_intra)
                dG.inc16(g.dma_start(out=dbg_out["atm_d"][:, m * 1024:(m + 1) * 1024],
                                     in_=a_atm[:, :]))

        v_prem_hist[0] = v_prem
        v_u_hist[0] = v_u
        v_axn_cons[m] = (v_wmm, v_u)
        v_ch[m] = (tT.v, tA.v, tV.v, v_predma, v_scdma)

    if "hist_d" in dbg_out:
        @blk.gpsimd
        def _(g):
            g.wait_ge(tV.h, tV.v)
            dG.inc16(g.dma_start(out=dbg_out["hist_d"], in_=a_hist[:, :]))
    if "pq_d" in dbg_out:
        @blk.gpsimd
        def _(g):
            g.wait_ge(dS.h, dS.v)
            dG.inc16(g.dma_start(out=dbg_out["pq_d"], in_=pqc.ap()[:, :]))
    if "sc_d" in dbg_out:
        @blk.gpsimd
        def _(g):
            g.wait_ge(dS.h, dS.v)
            dG.inc16(g.dma_start(out=dbg_out["sc_d"], in_=sc_in.ap()[:, :]))

    bB = barrier()
    ctxB.close()

    @blk.gpsimd
    def _(g):
        g.wait_ge(dS.h, bB[3])
        cc.inc(g.collective_compute(
            "AllGather", OP.bypass,
            replica_groups=[[0, 1], [2, 3], [4, 5], [6, 7]],
            ins=[pre_in.ap().opt()], outs=[pre_out.ap().opt()]))
        cc.inc(g.collective_compute(
            "AllReduce", OP.add,
            replica_groups=[[0, 1], [2, 3], [4, 5], [6, 7]],
            ins=[sc_in.ap().opt()], outs=[sc_out.ap().opt()]))
    v_cc2 = cc.v

    ctxC = contextlib.ExitStack()

    def sbC(name, shape, dtype):
        return ctxC.enter_context(nc.sbuf_tensor(name, shape, dtype))

    f_k = sbC("f_k", [128, T], BF16)
    f_v = sbC("f_v", [128, 64 * 130], BF16)
    f_q = sbC("f_q", [128, NTC * 512], BF16)
    f_nq = sbC("f_nq", [1, 16], F32)
    f_b2 = sbC("f_b2", [1, 16], BF16)
    f_bcall = sbC("f_bcall", [128, 16], F32)
    f_q2 = sbC("f_q2", [128, 2 * 512], BF16)
    f_nk = sbC("f_nk", [1, 16], F32)
    f_rb = sbC("f_rb", [65, 2048], BF16)
    f_dbc = sbC("f_dbc", [64, 512], BF16)
    f_ctx = sbC("f_ctx", [128, 512], BF16)
    f_cx = sbC("f_cx", [128, 2048], F32)
    f_pt = sbC("f_pt", [128, 2 * 1024], BF16)

    qk_v, exp_v, pv_v = {}, {}, {}
    ki = 0
    v_ctxev = None
    v_ctxdma = [0]
    v_B_ctd = [0]
    prev_B = None
    pair_done = None

    for pr in range(4):
        @blk.sync
        def _(sy, pr=pr, pd=pair_done):
            if pr == 0:
                wait_barrier(sy, bA)
                wait_barrier(sy, bB)
            else:
                sy.wait_ge(tT.h, pd[0])
                sy.wait_ge(tV.h, pd[1])
            dS.inc16(sy.dma_start(
                out=f_k[:, :].rearrange("p (tc c) -> p tc c", tc=NTC),
                in_=kf.ap()[pr * NTC * 128:(pr + 1) * NTC * 128, :]
                    .rearrange("(tc p) c -> p tc c", p=128)))
            dS.inc16(sy.dma_start(
                out=f_v[:, :].rearrange("p (kt s) -> p kt s", s=130)[:, :, 0:64],
                in_=vtok.ap()[pr * 64 * 128:(pr + 1) * 64 * 128, :]
                    .rearrange("(kt p) f -> p kt f", p=128)[:, :, 0:64]))
            dS.inc16(sy.dma_start(
                out=f_v[:, :].rearrange("p (kt s) -> p kt s", s=130)[:, :, 65:129],
                in_=vtok.ap()[pr * 64 * 128:(pr + 1) * 64 * 128, :]
                    .rearrange("(kt p) f -> p kt f", p=128)[:, :, 64:128]))
            dS.inc16(sy.dma_start(
                out=f_q[:, :].rearrange("p (tc c) -> p tc c", tc=NTC),
                in_=qf.ap()[pr * NTC * 128:(pr + 1) * NTC * 128, :]
                    .rearrange("(tc p) c -> p tc c", p=128)))
        v_kv = dS.v

        @blk.vector
        def _(v, pr=pr, v_kv=v_kv):
            if pr == 0:
                wait_barrier(v, bB)
            v.wait_ge(dS.h, v_kv)
            v.memset(f_v[:, :].rearrange("p (kt s) -> p kt s", s=130)[:, :, 64:65], 1.0)
            tV.inc(v.memset(f_v[:, :].rearrange("p (kt s) -> p kt s", s=130)[:, :, 129:130], 1.0))
        v_vones = tV.v

        v_nkmm_h = {}
        v_nkred_h = {}
        for tc in range(NTC):
            qb = (tc % 2) * 512
            pp = p_tf if tc % 2 == 0 else p_c

            @blk.vector
            def _(v, pr=pr, tc=tc, v_kv=v_kv, pd=pair_done, qb=qb):
                v.wait_ge(dS.h, v_kv)
                if tc >= 2:
                    v.wait_ge(tT.h, v_nkmm_h[tc - 2])
                elif pd is not None:
                    v.wait_ge(tT.h, pd[0])
                tV.inc(v.tensor_mul(f_q2[:, qb:qb + 512],
                                    f_k[:, tc * 512:(tc + 1) * 512],
                                    f_k[:, tc * 512:(tc + 1) * 512]))
            v_k2 = tV.v

            @blk.tensor
            def _(t, pr=pr, tc=tc, v_k2=v_k2, pd=pair_done, qb=qb, pp=pp):
                if pr == 0 and tc == 0:
                    wait_barrier(t, bB)
                t.wait_ge(tV.h, v_k2)
                if tc >= 2:
                    t.wait_ge(tV.h, v_nkred_h[tc - 2])
                elif pr > 0:
                    t.wait_ge(tV.h, pd[1])
                tT.inc(t.matmul(pp[0:1, 0:512], ones_col[:, :], f_q2[:, qb:qb + 512],
                                start=True, stop=True))
            v_nkmm_h[tc] = tT.v

            @blk.vector
            def _(v, tc=tc, vm=v_nkmm_h[tc], pp=pp):
                v.wait_ge(tT.h, vm)
                tV.inc(v.reduce_max(f_nk[0:1, tc:tc + 1], pp[0:1, 0:512], axis=AX))
            v_nkred_h[tc] = tV.v
        v_nkmm = [v_nkmm_h[NTC - 1]]
        v_nkred = [v_nkred_h[NTC - 1]]

        @blk.vector
        def _(v):
            v.wait_ge(tV.h, tV.v)
            tV.inc(v.reduce_max(a_stat[0:1, 4:5], f_nk[0:1, 0:16], axis=AX))
        v_bk2 = tV.v

        v_nqs = {}
        v_nqr_h = {}
        for gi2 in range(NTC):
            qb = (gi2 % 2) * 512
            pp = p_tf if gi2 % 2 == 0 else p_c

            @blk.vector
            def _(v, gi2=gi2, v_kv=v_kv, qb=qb, vkm=v_nkmm[0]):
                v.wait_ge(dS.h, v_kv)
                if gi2 >= 2:
                    v.wait_ge(tT.h, v_nqs[gi2 - 2])
                else:
                    v.wait_ge(tT.h, vkm)
                tV.inc(v.tensor_mul(f_q2[:, qb:qb + 512],
                                    f_q[:, gi2 * 512:(gi2 + 1) * 512],
                                    f_q[:, gi2 * 512:(gi2 + 1) * 512]))
            v_q2m = tV.v

            @blk.tensor
            def _(t, gi2=gi2, v_q2m=v_q2m, qb=qb, pp=pp, vkr=v_nkred[0]):
                t.wait_ge(tV.h, v_q2m)
                if gi2 >= 2:
                    t.wait_ge(tV.h, v_nqr_h[gi2 - 2])
                else:
                    t.wait_ge(tV.h, vkr)
                tT.inc(t.matmul(pp[0:1, 0:512], ones_col[:, :], f_q2[:, qb:qb + 512],
                                start=True, stop=True))
            v_nqs[gi2] = tT.v

            @blk.vector
            def _(v, gi2=gi2, vm=v_nqs[gi2], pp=pp):
                v.wait_ge(tT.h, vm)
                tV.inc(v.reduce_max(f_nq[0:1, gi2:gi2 + 1], pp[0:1, 0:512], axis=AX))
            v_nqr_h[gi2] = tV.v
        v_nqr = [v_nqr_h[NTC - 1]]

        @blk.vector
        def _(v, v_bk2=v_bk2):
            v.wait_ge(tV.h, tV.v)
            v.wait_ge(tV.h, v_bk2)
            tV.inc(v.tensor_scalar(a_stat[0:1, 5:6], a_stat[0:1, 4:5], -1.0 / 16.0,
                                   None, OP.mult))
            v.wait_ge(tV.h, tV.v)
            tV.inc(v.tensor_scalar(f_b2[0:1, :], f_nq[0:1, :], -4.0,
                                   a_stat[0:1, 5:6], OP.mult, OP.add))
        v_brow = tV.v

        @blk.tensor
        def _(t, v_brow=v_brow):
            t.wait_ge(tV.h, v_brow)
            tT.inc(t.matmul(p_tf[:, 256:272], ones_row[:, :], f_b2[0:1, :],
                            start=True, stop=True))
        v_bcm = tT.v

        @blk.vector
        def _(v, v_bcm=v_bcm, va=tA.v):
            v.wait_ge(tT.h, v_bcm)
            if va:
                v.wait_ge(tA.h, va)
            tV.inc(v.tensor_copy(f_bcall[:, :], p_tf[:, 256:272]))
        v_bcall = tV.v

        v_bcf = [None]
        exp_base_pair = tA.v
        ki_pair0 = ki
        for kt2 in range(NTC * 64):
            exp_v[ki_pair0 + kt2] = exp_base_pair + kt2 + 1
        for gi in range(NTC):
            ga = gi

            ki0 = ki

            @blk.tensor
            def _(t, ga=ga, ki0=ki0, vb=v_bcall, vce=v_ctxev, vvo=v_vones):
                for kt in range(64):
                    ki_ = ki0 + kt
                    pi = ki_ % 2
                    P = p_a if pi == 0 else p_b
                    if kt == 0:
                        t.wait_ge(tV.h, vb)
                    if ki_ >= 2:
                        t.wait_ge(tA.h, exp_v[ki_ - 2])
                    t.matmul(P[:, 0:512], f_k[0:64, kt * 128:(kt + 1) * 128],
                             f_q[0:64, ga * 512:(ga + 1) * 512], start=True, stop=True,
                             tile_position=(0, 0))
                    tT.inc(t.matmul(P[:, 512:1024], f_k[64:128, kt * 128:(kt + 1) * 128],
                                    f_q[64:128, ga * 512:(ga + 1) * 512], start=True,
                                    stop=True, tile_position=(64, 0)))
                    qk_v[ki_] = tT.v
                    if kt >= 1:
                        kp = ki0 + kt - 1
                        pp = kp % 2
                        t.wait_ge(tA.h, exp_v[kp])
                        if kt == 1:
                            t.wait_ge(tV.h, vvo)
                            if vce is not None:
                                t.wait_ge(tV.h, vce)
                        t.matmul(p_c[0:65, :], f_v[:, (kt - 1) * 130: (kt - 1) * 130 + 65],
                                 f_pt[:, pp * 1024: pp * 1024 + 512],
                                 start=(kt == 1), stop=False)
                        tT.inc(t.matmul(p_d[0:65, :],
                                        f_v[:, (kt - 1) * 130 + 65: (kt - 1) * 130 + 130],
                                        f_pt[:, pp * 1024 + 512:(pp + 1) * 1024],
                                        start=(kt == 1), stop=False))
                        pv_v[kp] = tT.v
                kp = ki0 + 63
                pp = kp % 2
                t.wait_ge(tA.h, exp_v[kp])
                t.matmul(p_c[0:65, :], f_v[:, 63 * 130: 63 * 130 + 65],
                         f_pt[:, pp * 1024: pp * 1024 + 512], start=False, stop=True)
                tT.inc(t.matmul(p_d[0:65, :], f_v[:, 63 * 130 + 65: 63 * 130 + 130],
                                f_pt[:, pp * 1024 + 512:(pp + 1) * 1024],
                                start=False, stop=True))
                pv_v[kp] = tT.v

            ki += 64

            pg = gi % 2
            @blk.vector
            def _(v, kk=ki, pg=pg):
                v.wait_ge(tT.h, pv_v[kk - 1])
                v.tensor_copy(f_cx[0:65, pg * 1024: pg * 1024 + 512], p_c[0:65, :])
                tV.inc(v.tensor_copy(f_cx[0:65, pg * 1024 + 512:(pg + 1) * 1024],
                                     p_d[0:65, :]))
            v_ctxev = tV.v

            @blk.vector
            def _(v, pg=pg, ve=v_ctxev):
                v.wait_ge(tV.h, ve)
                tV.inc(v.reciprocal(f_rb[0:1, pg * 1024: pg * 1024 + 512],
                                    f_cx[64:65, pg * 1024: pg * 1024 + 512]))
                tV.inc(v.reciprocal(f_rb[64:65, pg * 1024 + 512:(pg + 1) * 1024],
                                    f_cx[64:65, pg * 1024 + 512:(pg + 1) * 1024]))
            v_rp1, v_rp2 = tV.v - 1, tV.v

            if prev_B is not None:
                prev_B()

            def mk_B(pr=pr, gi=gi, pg=pg, v_rp1=v_rp1, v_rp2=v_rp2):
                def emit():
                    vprev = v_B_ctd[0]

                    @blk.tensor
                    def _(t):
                        t.wait_ge(tV.h, v_rp1)
                        if vprev:
                            t.wait_ge(tV.h, vprev)
                        tT.inc(t.matmul(p_tf[0:64, 0:512], ones_row[0:1, 0:64],
                                        f_rb[0:1, pg * 1024: pg * 1024 + 512],
                                        start=True, stop=True))
                    v_db1 = tT.v

                    @blk.tensor
                    def _(t):
                        t.wait_ge(tV.h, v_rp2)
                        tT.inc(t.matmul(p_tf[64:128, 0:512], ones_r2[64:65, 0:64],
                                        f_rb[64:65, pg * 1024 + 512:(pg + 1) * 1024],
                                        start=True, stop=True))
                    v_db2 = tT.v

                    @blk.vector
                    def _(v, vcd=v_ctxdma[0]):
                        v.wait_ge(tT.h, v_db1)
                        if vcd:
                            v.wait_ge(dS.h, vcd)
                        tV.inc(v.tensor_copy(f_dbc[:, :], p_tf[0:64, 0:512]))
                        v.wait_ge(tV.h, tV.v)
                        tV.inc(v.tensor_mul(f_ctx[0:64, :],
                                            f_cx[0:64, pg * 1024: pg * 1024 + 512],
                                            f_dbc[:, :]))
                        v.wait_ge(tT.h, v_db2)
                        v.wait_ge(tV.h, tV.v)
                        tV.inc(v.tensor_copy(f_dbc[:, :], p_tf[64:128, 0:512]))
                        v.wait_ge(tV.h, tV.v)
                        tV.inc(v.tensor_mul(f_ctx[64:128, :],
                                            f_cx[0:64, pg * 1024 + 512:(pg + 1) * 1024],
                                            f_dbc[:, :]))
                    v_ctd = tV.v
                    v_B_ctd[0] = v_ctd

                    @blk.sync
                    def _(sy):
                        sy.wait_ge(tV.h, v_ctd)
                        dS.inc16(sy.dma_start(
                            out=ctx_in.ap()[pr * 128:(pr + 1) * 128,
                                            gi * 512:(gi + 1) * 512],
                            in_=f_ctx[:, :]))
                    v_ctxdma[0] = dS.v
                return emit
            prev_B = mk_B()
        prev_B()
        prev_B = None
        @blk.scalar
        def _(s, pr=pr, ki_pair0=ki_pair0):
            if pr == 0:
                wait_barrier(s, bB)
            for kt2 in range(NTC * 64):
                ki_ = ki_pair0 + kt2
                gi2 = kt2 // 64
                pi = ki_ % 2
                P = p_a if pi == 0 else p_b
                s.wait_ge(tT.h, qk_v[ki_])
                if ki_ >= 2:
                    assert pv_v[ki_ - 2] <= qk_v[ki_]
                inst = s.activation(f_pt[:, pi * 1024:(pi + 1) * 1024], P[:, :],
                                    AF.Exp, bias=f_bcall[:, gi2:gi2 + 1])
                tA.inc(inst)
                assert tA.v == exp_v[ki_]
        pair_done = (tT.v, tV.v)

    if "ctx_d" in dbg_out:
        @blk.gpsimd
        def _(g):
            g.wait_ge(dS.h, dS.v)
            dG.inc16(g.dma_start(out=dbg_out["ctx_d"], in_=ctx_in.ap()[:, :]))

    bC = barrier()
    ctxC.close()

    @blk.gpsimd
    def _(g):
        g.wait_ge(dS.h, bC[3])
        cc.inc(g.collective_compute(
            "AllGather", OP.bypass,
            replica_groups=[[0, 1], [2, 3], [4, 5], [6, 7]],
            ins=[ctx_in.ap().opt()], outs=[ctx_out.ap().opt()]))
    v_cc3 = cc.v

    ctxD = contextlib.ExitStack()

    def sbD(name, shape, dtype):
        return ctxD.enter_context(nc.sbuf_tensor(name, shape, dtype))

    g_scs = sbD("g_scs", [128, 64 * 33], F32)
    g_exp = sbD("g_exp", [128, 2 * 33], F32)
    g_mx = sbD("g_mx", [128, 64], F32)
    g_gst = sbD("g_gst", [128, 64 * 33], BF16)
    g_rt = sbD("g_rt", [33, 256], BF16)
    c_pre = sbD("c_pre", [128, 2 * 8 * 256], BF16)
    c_xnf = sbD("c_xnf", [128, 2 * 4 * 256], BF16)
    c_pq = sbD("c_pq", [128, 2 * 4 * 256], BF16)
    c_gbc = sbD("c_gbc", [128, 256], BF16)
    c_spq = sbD("c_spq", [128, 2 * 4 * 256], BF16)
    c_xo = sbD("c_xo", [128, 4 * 256], F32)
    c_s = sbD("c_s", [128, 4 * 256], F32)
    c_ctx = sbD("c_ctx", [128, 2 * 8 * 256], BF16)
    c_xvo = sbD("c_xvo", [128, 256], BF16)
    c_g = sbD("c_g", [128, 4 * 256], F32)
    c_xr = sbD("c_xr", [128, 2 * 2 * 512], F32)
    c_out = sbD("c_out", [128, 2 * 512], F32)

    @blk.sync
    def _(sy):
        wait_barrier(sy, bC)
        sy.wait_ge(cc.h, v_cc2)
        dS.inc16(sy.dma_start(out=g_scs[:, :].rearrange("p (a n) -> p a n", a=NT),
                              in_=sc_out.ap().rearrange("(a p) n -> p a n", p=128)))
    v_scg = dS.v

    v_gt_done = {}
    @blk.vector
    def _(v):
        wait_barrier(v, bC)
        v.wait_ge(dS.h, v_scg)
        for a in range(NT):
            inst = v.reduce_max(g_mx[:, a:a + 1], g_scs[:, a * 33:(a + 1) * 33],
                                axis=AX, negate=True)
        tV.inc(inst)
    v_mxall = tV.v
    dve_base = tV.v
    exp_base = tA.v
    gexp_v = {a: exp_base + a + 1 for a in range(NT)}
    gdve_v = {a: dve_base + 3 * (a + 1) for a in range(NT)}

    @blk.scalar
    def _(s):
        wait_barrier(s, bC)
        s.wait_ge(tV.h, v_mxall)
        for a in range(NT):
            if a >= 2:
                s.wait_ge(tV.h, gdve_v[a - 2])
            inst = s.activation(g_exp[:, (a % 2) * 33:(a % 2) * 33 + 33],
                                g_scs[:, a * 33:(a + 1) * 33], AF.Exp,
                                bias=g_mx[:, a:a + 1])
            tA.inc(inst)
            assert tA.v == gexp_v[a]

    @blk.vector
    def _(v):
        for a in range(NT):
            v.wait_ge(tA.h, gexp_v[a])
            sl = (a % 2) * 33
            tV.inc(v.reduce_sum(st_sm[:, 1:2], g_exp[:, sl:sl + 33], axis=AX))
            v.wait_ge(tV.h, tV.v)
            tV.inc(v.reciprocal(st_sm[:, 2:3], st_sm[:, 1:2]))
            v.wait_ge(tV.h, tV.v)
            tV.inc(v.tensor_scalar(g_gst[:, a * 33:(a + 1) * 33], g_exp[:, sl:sl + 33],
                                   st_sm[:, 2:3], None, OP.mult))
            assert tV.v == gdve_v[a]
            v_gt_done[a] = tV.v

    if "gates_d" in dbg_out:
        @blk.gpsimd
        def _(g):
            g.wait_ge(tV.h, tV.v)
            dG.inc16(g.dma_start(out=dbg_out["gates_d"], in_=g_gst[:, :]))

    v_cm = {}
    v_outdma = [0, 0]
    fin = {}
    v_ldm, v_ldcm = {}, {}

    def emit_ph6_loads(mm, first):
        pm = mm % 2

        @blk.sync
        def _(sy, mm=mm, pm=pm, first=first):
            if first:
                wait_barrier(sy, bC)
                sy.wait_ge(cc.h, v_cc2)
            elif mm >= 2:
                sy.wait_ge(tT.h, fin["wo"])
                sy.wait_ge(tV.h, fin["xo"])
                sy.wait_ge(tV.h, fin["spq"])
                sy.wait_ge(tV.h, fin["out"])
            for dt in range(8):
                dS.inc16(sy.dma_start(
                    out=c_pre[:, pm * 2048 + dt * 256: pm * 2048 + (dt + 1) * 256],
                    in_=pre_out.ap()[dt * 128:(dt + 1) * 128, mm * 256:(mm + 1) * 256]))
            for dt in range(4):
                dS.inc16(sy.dma_start(
                    out=c_xnf[:, pm * 1024 + dt * 256: pm * 1024 + (dt + 1) * 256],
                    in_=xn_tm.ap()[mm * 256:(mm + 1) * 256, dt * 128:(dt + 1) * 128],
                    transpose=True))
            dS.inc16(sy.dma_start(
                out=c_pq[:, pm * 1024:(pm + 1) * 1024].rearrange("p (j c) -> p j c", j=4),
                in_=pqc.ap()[mm * 4 * 128:(mm * 4 + 4) * 128, :]
                    .rearrange("(j p) c -> p j c", p=128)))
            for cs in range(2):
                dS.inc16(sy.dma_start(
                    out=c_xr[:, pm * 1024 + cs * 512: pm * 1024 + (cs + 1) * 512],
                    in_=x[mm * 256 + cs * 128: mm * 256 + (cs + 1) * 128, 0:512]))
        v_ldm[mm] = dS.v

        @blk.sync
        def _(sy, mm=mm, pm=pm, first=first):
            sy.wait_ge(cc.h, v_cc3)
            if not first and mm >= 2:
                sy.wait_ge(tT.h, fin["omm"])
            for dt in range(8):
                dS.inc16(sy.dma_start(
                    out=c_ctx[:, pm * 2048 + dt * 256: pm * 2048 + (dt + 1) * 256],
                    in_=ctx_out.ap()[dt * 128:(dt + 1) * 128, mm * 256:(mm + 1) * 256]))
        v_ldcm[mm] = dS.v

    emit_ph6_loads(0, True)
    for m in range(NCH):
        pm = m % 2
        if m + 1 < NCH:
            emit_ph6_loads(m + 1, False)
        v_ld = v_ldm[m]
        v_ldc = v_ldcm[m]

        @blk.tensor
        def _(t, m=m):
            if m == 0:
                wait_barrier(t, bC)
            t.wait_ge(tV.h, v_gt_done[2 * m + 1])
            if m >= 1:
                t.wait_ge(tV.h, fin["grt_tv"])
            for tt in range(2):
                inst = t.transpose(p_t[0:33, tt * 128:(tt + 1) * 128],
                                   g_gst[:, (2 * m + tt) * 33:(2 * m + tt + 1) * 33],
                                   c_identb[:, :])
            tT.inc(inst)
        v_gtm = tT.v

        @blk.vector
        def _(v, m=m, v_gtm=v_gtm):
            if m == 0:
                wait_barrier(v, bC)
            v.wait_ge(tT.h, v_gtm)
            if m >= 1:
                v.wait_ge(tT.h, fin["grt"])
            tV.inc(v.tensor_copy(g_rt[:, :], p_t[0:33, 0:256]))
        v_grt = tV.v
        fin["grt_tv"] = v_grt

        if m > 0:
            @blk.vector
            def _(v, m=m):
                if "g" in fin:
                    v.wait_ge(tA.h, fin["g"])
                tV.inc(v.memset(p_b[:, :], 0.0))
        v_pbz = tV.v

        @blk.tensor
        def _(t, m=m, v_ld=v_ld, pm=pm):
            t.wait_ge(dS.h, v_ld)
            if m >= 1:
                t.wait_ge(tV.h, fin["xo"])
            for fs in range(4):
                for dt in range(8):
                    inst = t.matmul(p_a[:, fs * 256:(fs + 1) * 256],
                                    w["wo"][:, dt * 512 + fs * 128: dt * 512 + fs * 128 + 128],
                                    c_pre[:, pm * 2048 + dt * 256: pm * 2048 + (dt + 1) * 256],
                                    start=(dt == 0), stop=(dt == 7))
            tT.inc(inst)
        v_wo = tT.v
        fin["wo"] = v_wo

        v_hr = None
        v_gbcev = {}
        v_selv = {}
        v_hrv = {}
        if m == 0:
            fin["spq"] = tV.v
        for n in range(m):
            sel = p_c if n % 2 == 0 else p_d
            sb = (n % 2) * 1024

            @blk.tensor
            def _(t, m=m, n=n, sel=sel, v_grt=v_grt, vz=v_pbz):
                t.wait_ge(tV.h, v_grt)
                if n >= 2:
                    t.wait_ge(tV.h, v_gbcev[n - 2])
                tT.inc(t.matmul(sel[:, 0:256], c_e33[:, n * 128:(n + 1) * 128],
                                g_rt[:, :], start=True, stop=True))
                v_selv[n] = tT.v
                if n >= 1:
                    nn = n - 1
                    t.wait_ge(tV.h, v_gbcev[nn])
                    if nn == 0:
                        t.wait_ge(tV.h, vz)
                    for h in range(8):
                        po, j = hsl(h)
                        inst = t.matmul(p_b[po:po + 64, j * 256:(j + 1) * 256],
                                        a_hist[po:po + 64, (j * NCH + nn) * 64:(j * NCH + nn) * 64 + 64],
                                        c_spq[po:po + 64, (nn % 2) * 1024 + j * 256:
                                              (nn % 2) * 1024 + (j + 1) * 256],
                                        start=False, stop=False, skip_group_check=True)
                    tT.inc(inst)
                    v_hrv[nn] = tT.v

            @blk.vector
            def _(v, m=m, n=n, sel=sel, vs=v_selv[n], pm=pm, sb=sb):
                v.wait_ge(tT.h, vs)
                if n >= 2:
                    v.wait_ge(tT.h, v_hrv[n - 2])
                for j in range(4):
                    inst = v.tensor_mul(c_spq[:, sb + j * 256: sb + (j + 1) * 256],
                                        c_pq[:, pm * 1024 + j * 256: pm * 1024 + (j + 1) * 256],
                                        sel[:, 0:256])
                tV.inc(inst)
            v_gbcev[n] = tV.v
            fin["spq"] = tV.v

        if m > 0:
            @blk.tensor
            def _(t, m=m, vg=v_gbcev[m - 1], vz=v_pbz):
                t.wait_ge(tV.h, vg)
                if m == 1:
                    t.wait_ge(tV.h, vz)
                for h in range(8):
                    po, j = hsl(h)
                    inst = t.matmul(p_b[po:po + 64, j * 256:(j + 1) * 256],
                                    a_hist[po:po + 64, ((j * NCH) + m - 1) * 64:((j * NCH) + m - 1) * 64 + 64],
                                    c_spq[po:po + 64, ((m - 1) % 2) * 1024 + j * 256:
                                          ((m - 1) % 2) * 1024 + (j + 1) * 256],
                                    start=False, stop=True, skip_group_check=True)
                tT.inc(inst)
            v_hr = tT.v

        @blk.vector
        def _(v, m=m, v_wo=v_wo, v_ld=v_ld, pm=pm):
            v.wait_ge(tT.h, v_wo)
            v.wait_ge(dS.h, v_ld)
            tV.inc(v.tensor_add(c_xo[:, :], p_a[:, :],
                                c_xnf[:, pm * 1024:(pm + 1) * 1024]))
        v_xo = tV.v
        fin["xo"] = v_xo

        @blk.tensor
        def _(t, m=m, v_grt=v_grt, v_xo=v_xo):
            t.wait_ge(tV.h, v_grt)
            if m >= 2:
                t.wait_ge(tV.h, v_gbcev.get(m - 2, v_xo))
            sel = p_c if m % 2 == 0 else p_d
            tT.inc(t.matmul(sel[:, 0:256], c_e33[:, 32 * 128:33 * 128], g_rt[:, :],
                            start=True, stop=True))
        v_gcm = tT.v
        fin["grt"] = v_gcm

        @blk.vector
        def _(v, m=m, v_gcm=v_gcm, v_xo=v_xo, v_hr=v_hr):
            v.wait_ge(tT.h, v_gcm)
            if m >= 1:
                v.wait_ge(tT.h, fin["sgtr"])
            sel = p_c if m % 2 == 0 else p_d
            tV.inc(v.tensor_copy(c_gbc[:, :], sel[:, 0:256]))
            v.wait_ge(tV.h, tV.v)
            v.wait_ge(tV.h, v_xo)
            for j in range(4):
                inst = v.tensor_mul(c_s[:, j * 256:(j + 1) * 256],
                                    c_xo[:, j * 256:(j + 1) * 256], c_gbc[:, :])
            tV.inc(inst)
            if m > 0:
                v.wait_ge(tT.h, v_hr)
                v.wait_ge(tV.h, tV.v)
                tV.inc(v.tensor_add(c_s[:, :], c_s[:, :], p_b[:, :]))
        v_s = tV.v

        @blk.tensor
        def _(t, m=m, v_ldc=v_ldc, v_s=v_s, pm=pm):
            t.wait_ge(dS.h, v_ldc)
            t.wait_ge(tV.h, v_s)
            sel = p_d if m % 2 == 0 else p_c
            for dt in range(8):
                inst = t.matmul(sel[:, 0:256], w_vc["o"][:, dt * 128:(dt + 1) * 128],
                                c_ctx[:, pm * 2048 + dt * 256: pm * 2048 + (dt + 1) * 256],
                                start=(dt == 0), stop=(dt == 7))
            tT.inc(inst)
        v_xvo = tT.v

        @blk.vector
        def _(v, m=m, v_xvo=v_xvo):
            v.wait_ge(tT.h, v_xvo)
            sel = p_d if m % 2 == 0 else p_c
            tV.inc(v.tensor_scalar(c_xvo[:, :], sel[:, 0:256], scl128["o"][:, :],
                                   None, OP.mult))
        v_xvoe = tV.v

        @blk.tensor
        def _(t, m=m, v_xvoe=v_xvoe, v_s=v_s, pm=pm):
            t.wait_ge(tV.h, v_xvoe)
            t.wait_ge(tV.h, v_s)
            for fs in range(4):
                for dt in range(8):
                    t.matmul(p_b[:, fs * 256:(fs + 1) * 256],
                             w["fo"][:, dt * 512 + fs * 128: dt * 512 + fs * 128 + 128],
                             c_ctx[:, pm * 2048 + dt * 256: pm * 2048 + (dt + 1) * 256],
                             start=(dt == 0), stop=False)
                inst = t.matmul(p_b[:, fs * 256:(fs + 1) * 256],
                                w_uc["o"][:, fs * 128:(fs + 1) * 128], c_xvo[:, :],
                                start=False, stop=True)
            tT.inc(inst)
        v_omm = tT.v
        fin["omm"] = v_omm

        @blk.scalar
        def _(s, m=m, v_omm=v_omm):
            if m == 0:
                wait_barrier(s, bC)
            s.wait_ge(tT.h, v_omm)
            for fs in range(4):
                inst = s.activation(c_g[:, fs * 256:(fs + 1) * 256],
                                    p_b[:, fs * 256:(fs + 1) * 256], AF.Silu,
                                    bias=b_o[:, fs:fs + 1])
            tA.inc(inst)
        v_g = tA.v
        fin["g"] = v_g

        @blk.vector
        def _(v, m=m, v_g=v_g, v_s=v_s):
            v.wait_ge(tA.h, v_g)
            tV.inc(v.tensor_mul(c_s[:, :], c_s[:, :], c_g[:, :]))
        v_sg = tV.v

        if "s_d" in dbg_out or "g_d" in dbg_out:
            @blk.gpsimd
            def _(g, m=m, v_sg=v_sg):
                g.wait_ge(tV.h, v_sg)
                if "s_d" in dbg_out:
                    dG.inc16(g.dma_start(out=dbg_out["s_d"][:, m * 1024:(m + 1) * 1024],
                                         in_=c_s[:, :]))
                if "g_d" in dbg_out:
                    dG.inc16(g.dma_start(out=dbg_out["g_d"][:, m * 1024:(m + 1) * 1024],
                                         in_=c_g[:, :]))

        v_oev = []
        for cs in range(2):
            @blk.tensor
            def _(t, m=m, cs=cs, v_sg=v_sg):
                t.wait_ge(tV.h, v_sg)
                if cs == 1:
                    t.wait_ge(tV.h, v_oev[0])
                for j in range(4):
                    inst = t.transpose(p_tf[:, j * 128:(j + 1) * 128],
                                       c_s[:, j * 256 + cs * 128: j * 256 + cs * 128 + 128],
                                       c_identf[:, :])
                tT.inc(inst)
            v_trm = tT.v
            fin["sgtr"] = v_trm

            @blk.vector
            def _(v, m=m, cs=cs, v_trm=v_trm, v_ld=v_ld, pm=pm):
                v.wait_ge(tT.h, v_trm)
                v.wait_ge(dS.h, v_ld)
                if v_outdma[cs]:
                    v.wait_ge(dS.h, v_outdma[cs])
                tV.inc(v.tensor_add(c_out[:, cs * 512:(cs + 1) * 512], p_tf[:, 0:512],
                                    c_xr[:, pm * 1024 + cs * 512: pm * 1024 + (cs + 1) * 512]))
            v_oev.append(tV.v)

            @blk.sync
            def _(sy, m=m, cs=cs, vo=v_oev[cs]):
                sy.wait_ge(tV.h, vo)
                dS.inc16(sy.dma_start(
                    out=out[m * 256 + cs * 128: m * 256 + (cs + 1) * 128, :],
                    in_=c_out[:, cs * 512:(cs + 1) * 512]))
            v_outdma[cs] = dS.v

        fin["out"] = v_oev[1]
        v_cm[m] = (tT.v, tA.v, tV.v)

    @blk.sync
    def _(sy):
        sy.wait_ge(dS.h, dS.v)
        sy.wait_ge(dG.h, dG.v)
        sy.wait_ge(tT.h, tT.v)
        sy.wait_ge(tA.h, tA.v)
        sy.wait_ge(tV.h, tV.v)
        sy.wait_ge(cc.h, cc.v)

    ctxD.close()
    ctxG.close()
    return nc, dbg_out


def _host_inputs(inputs, c):
    b, hg = c // 2, c % 2
    F = np.arange(512 * hg, 512 * hg + 512)
    notF = np.arange(512 * (1 - hg), 512 * (1 - hg) + 512)
    perm = np.concatenate([F, notF])
    g = lambda n: np.asarray(inputs[n], np.float32)
    d = {}
    d["x"] = np.ascontiguousarray(g("L0_attn_x")[b][:, perm])
    d["wqT"] = np.ascontiguousarray(g("L0_attn_Wq").T[perm][:, F])
    d["wkT"] = np.ascontiguousarray(g("L0_attn_Wk").T[perm][:, F])
    d["wvT"] = np.ascontiguousarray(g("L0_attn_Wv").T[perm][:, F])
    d["wuT"] = np.ascontiguousarray(g("L0_attn_Wu").T[perm][:, F])
    d["woT"] = np.ascontiguousarray(g("L0_attn_Wo").T[:, F])
    for p in "qkv":
        d[f"f{p}T"] = np.ascontiguousarray(g(f"L0_ffn_{p}Wm").T[perm][:, F])
        d[f"vcT_{p}"] = np.ascontiguousarray(g(f"L0_ffn_{p}V").reshape(ER, D).T[perm])
    d["foT"] = np.ascontiguousarray(g("L0_ffn_oWm").T[:, F])
    d["vcT_o"] = np.ascontiguousarray(g("L0_ffn_oV").reshape(ER, D).T)
    for p in "qkvo":
        d[f"ucat_{p}"] = np.ascontiguousarray(
            g(f"L0_ffn_{p}U").transpose(0, 2, 1).reshape(ER, D)[:, F])
        d[f"srow_{p}"] = np.ascontiguousarray(g(f"L0_ffn_{p}S").reshape(1, ER))
    for p in "qko":
        d[f"bcol_{p}"] = np.ascontiguousarray(g(f"L0_ffn_{p}b")[F].reshape(4, 128).T)
    d["vb_row"] = np.ascontiguousarray(g("L0_ffn_vb")[F].reshape(1, 512))
    d["nw_pa"] = np.ascontiguousarray(g("L0_attn_norm_w")[perm].reshape(8, 128).T)
    d["gw1T"] = np.ascontiguousarray(g("L0_ffn_gw1").T[perm])
    d["gb1"] = np.ascontiguousarray(g("L0_ffn_gb1").reshape(1, GH))
    d["gw2T"] = np.ascontiguousarray(g("L0_ffn_gw2").T)
    d["gb2"] = np.ascontiguousarray(g("L0_ffn_gb2").reshape(1, E))
    d["identb"] = np.eye(128, dtype=ml_dtypes.bfloat16)
    d["identf"] = np.eye(128, dtype=np.float32)
    tri = (np.arange(SEG)[:, None] >= np.arange(SEG)[None, :])
    d["triw"] = tri.astype(np.float32)
    d["trik"] = np.ascontiguousarray(tri.T.astype(ml_dtypes.bfloat16))
    d["cntinv"] = np.ascontiguousarray(
        (1.0 / np.arange(1, SEG + 1, dtype=np.float32)).reshape(2, 128).T)
    e33a = np.zeros((33, 33 * 128), ml_dtypes.bfloat16)
    for n in range(33):
        e33a[n, n * 128:(n + 1) * 128] = 1
    d["e33"] = e33a
    e16a = np.zeros((E, 128), ml_dtypes.bfloat16)
    for e in range(E):
        e16a[e, e * 16:(e + 1) * 16] = 1
    d["e16"] = e16a
    return d


_CACHE = {}


def kernel(**inputs):
    key = tuple(DBG)
    if key not in _CACHE:
        _CACHE[key] = build_nc(dbg=DBG)
    nc, dbg_out = _CACHE[key]
    in_maps = [_host_inputs(inputs, c) for c in range(8)]
    res = run_bass_kernel_spmd(nc, in_maps, core_ids=list(range(8)))
    outa = np.empty((B, T, D), np.float32)
    for c in range(8):
        b, hg = c // 2, c % 2
        outa[b, :, 512 * hg:512 * hg + 512] = res.results[c]["out"]
    kernel.last_results = res.results
    return outa


# version 20
# speedup vs baseline: 1.4087x; 1.0019x over previous
import contextlib
import numpy as np
import ml_dtypes
import concourse.bass as bass
import concourse.mybir as mybir
from concourse.bass_utils import run_bass_kernel_spmd
from concourse.alu_op_type import AluOpType as OP

F32 = mybir.dt.float32
BF16 = mybir.dt.bfloat16
AF = mybir.ActivationFunctionType
AX = mybir.AxisListType.X

B, T, D = 4, 8192, 1024
H, DK, SEG, NCH = 16, 64, 256, 32
NT = T // 128
NTC = T // 512
E, R, ER, GH = 8, 16, 128, 256
SCALE = float(DK) ** -0.5
EPS_LN, EPS_RMS, EPS_DEN = 1e-5, 1e-6, 1e-6
NEG_PAD = -5e29

DBG = []


class Ctr:
    def __init__(self, nc, ctx, name):
        self.h = ctx.enter_context(nc.semaphore(name))
        self.v = 0

    def inc(self, inst, k=1):
        inst.then_inc(self.h, k)
        self.v += k
        return self.v

    def inc16(self, inst):
        return self.inc(inst, 16)


def build_nc(dbg=()):
    nc = bass.Bass("TRN2", target_bir_lowering=False, debug=False)

    def din(name, shape, dtype=F32):
        return nc.dram_tensor(name, shape, dtype, kind="ExternalInput").ap()

    x = din("x", [T, D])
    wT = {n: din(n, [D, 512]) for n in ("wqT", "wkT", "wvT", "wuT", "woT",
                                        "fqT", "fkT", "fvT", "foT")}
    vcT = {p: din(f"vcT_{p}", [D, ER]) for p in "qkvo"}
    ucat = {p: din(f"ucat_{p}", [ER, 512]) for p in "qkvo"}
    bcol = {p: din(f"bcol_{p}", [128, 4]) for p in "qko"}
    vb_row = din("vb_row", [1, 512])
    srow = {p: din(f"srow_{p}", [1, ER]) for p in "qkvo"}
    nw_pa = din("nw_pa", [128, 8])
    gw1T, gb1 = din("gw1T", [D, GH]), din("gb1", [1, GH])
    gw2T, gb2 = din("gw2T", [GH, E]), din("gb2", [1, E])
    identb = din("identb", [128, 128], BF16)
    identf = din("identf", [128, 128], F32)
    triw = din("triw", [SEG, SEG])
    trik = din("trik", [SEG, SEG], BF16)
    cntinv = din("cntinv", [128, 2])
    e33 = din("e33", [33, 33 * 128], BF16)
    e16 = din("e16", [E, 128], BF16)

    out = nc.dram_tensor("out", [T, 512], F32, kind="ExternalOutput").ap()

    def dint(name, shape, dtype):
        return nc.dram_tensor(name, shape, dtype)

    xn_tm = dint("xn_tm", [T, D], BF16)
    qf = dint("qf", [4 * NTC * 128, 512], BF16)
    kf = dint("kf", [4 * NTC * 128, 512], BF16)
    vtok = dint("vtok", [4 * 64 * 128, 128], BF16)
    pqc = dint("pqc", [NCH * 4 * 128, 256], BF16)
    pre_in = dint("pre_in", [512, T], BF16)
    pre_out = dint("pre_out", [1024, T], BF16)
    ctx_in = dint("ctx_in", [512, T], BF16)
    ctx_out = dint("ctx_out", [1024, T], BF16)
    sc_in = dint("sc_in", [T, 33], F32)
    sc_out = dint("sc_out", [T, 33], F32)

    dbg_out = {}
    for name, shape in [
        ("xn_d", [T, D]), ("pooled_d", [128, 8]), ("gate_d", [1, 8]),
        ("rms_d", [1, T]), ("qf_d", [4 * NTC * 128, 512]),
        ("kf_d", [4 * NTC * 128, 512]), ("vt_d", [4 * 64 * 128, 128]),
        ("pre_d", [512, T]), ("sc_d", [T, 33]), ("ctx_d", [512, T]),
        ("hist_d", [128, 4 * NCH * 64]), ("gates_d", [128, 64 * 33]),
        ("s_d", [128, NCH * 1024]), ("g_d", [128, NCH * 1024]),
        ("pq_d", [NCH * 4 * 128, 256]), ("den_d", [128, NCH * 1024]),
        ("bc_d", [128, NCH * 1024]), ("num_d", [128, NCH * 1024]),
        ("zz_d", [128, NCH * 1024]), ("atm_d", [128, NCH * 1024]),
    ]:
        if name in dbg:
            dbg_out[name] = nc.dram_tensor(name, shape, F32, kind="ExternalOutput").ap()

    ctxG = contextlib.ExitStack()

    def sbG(name, shape, dtype):
        return ctxG.enter_context(nc.sbuf_tensor(name, shape, dtype))

    def psG(name, shape, dtype=F32):
        return ctxG.enter_context(nc.psum_tensor(name, shape, dtype))

    w = {n: sbG("w_" + n, [128, 8 * 512], BF16) for n in
         ("wq", "wk", "wv", "wu", "wo", "fq", "fk", "fv", "fo")}
    w_vc = {p: sbG(f"w_vc_{p}", [128, 8 * 128], BF16) for p in "qkvo"}
    w_uc = {p: sbG(f"w_uc_{p}", [128, 512], BF16) for p in "qkvo"}
    w_g1 = sbG("w_g1", [128, 8 * 256], BF16)
    w_g2 = sbG("w_g2", [128, 2 * 8], BF16)
    c_identb = sbG("c_identb", [128, 128], BF16)
    c_identf = sbG("c_identf", [128, 128], F32)
    c_triw = sbG("c_triw", [128, 2 * 256], F32)
    c_trik = sbG("c_trik", [128, 2 * 256], BF16)
    c_cnt = sbG("c_cnt", [128, 2], F32)
    c_e33 = sbG("c_e33", [33, 33 * 128], BF16)
    c_e16 = sbG("c_e16", [E, 128], BF16)
    b_q = sbG("b_q", [128, 4], F32)
    b_k = sbG("b_k", [128, 4], F32)
    b_o = sbG("b_o", [128, 4], F32)
    b_v = sbG("b_v", [1, 512], BF16)
    c_nw = sbG("c_nw", [128, 8], F32)
    c_gb1 = sbG("c_gb1", [1, GH], F32)
    c_gb2 = sbG("c_gb2", [1, E], F32)
    c_srow = {p: sbG(f"c_srow_{p}", [1, ER], F32) for p in "qkvo"}
    ones_col = sbG("ones_col", [128, 1], BF16)
    c_eps = sbG("c_eps", [128, 1], F32)
    ones_row = sbG("ones_row", [1, 128], BF16)
    ones_r2 = sbG("ones_r2", [65, 128], BF16)
    ones_sq = sbG("ones_sq", [128, 128], BF16)
    scl128 = {p: sbG(f"scl128_{p}", [128, 1], F32) for p in "qkvo"}
    ge128 = sbG("ge128", [128, 1], F32)
    rms_row = sbG("rms_row", [1, T], BF16)
    a_hist = sbG("a_hist", [128, 4 * NCH * 64], BF16)
    a_scum = sbG("a_scum", [128, 4 * 64], F32)
    a_scbf = sbG("a_scbf", [128, 4 * 64], BF16)
    a_z = sbG("a_z", [128, 4], F32)
    a_stat = sbG("a_stat", [128, 8], F32)
    st_sm = sbG("st_sm", [128, 12], F32)

    p_a = psG("p_a", [128, 1024])
    p_b = psG("p_b", [128, 1024])
    p_c = psG("p_c", [128, 512])
    p_d = psG("p_d", [128, 512])
    p_t = psG("p_t", [128, 512], BF16)
    p_tf = psG("p_tf", [128, 512])

    sems = {n: Ctr(nc, ctxG, n) for n in ("dS", "dG", "tT", "tA", "tV", "cc")}
    dS, dG, tT, tA, tV, cc = (sems[n] for n in ("dS", "dG", "tT", "tA", "tV", "cc"))

    ctxG.enter_context(nc.allow_low_precision(
        reason="bf16 matmul operand stream with fp32 accumulation by design; "
               "tolerance bar measured for bf16 matmuls"))
    blk = ctxG.enter_context(nc.Block())

    def barrier():
        return (tT.v, tA.v, tV.v, dS.v, dG.v)

    def wait_barrier(eng, bv):
        vt, va, vv, vs, vg = bv
        if vt: eng.wait_ge(tT.h, vt)
        if va: eng.wait_ge(tA.h, va)
        if vv: eng.wait_ge(tV.h, vv)
        if vs: eng.wait_ge(dS.h, vs)
        if vg: eng.wait_ge(dG.h, vg)

    def hsl(h):
        return 64 * (h % 2), h // 2

    @blk.gpsimd
    def _(g):
        names = {"wq": "wqT", "wk": "wkT", "wv": "wvT", "wu": "wuT",
                 "wo": "woT", "fq": "fqT", "fk": "fkT", "fv": "fvT", "fo": "foT"}
        for n, sn in names.items():
            dG.inc16(g.dma_start(out=w[n][:, :].rearrange("p (a f) -> p a f", a=8),
                                 in_=wT[sn].rearrange("(a p) f -> p a f", p=128)))
        for p in "qkvo":
            dG.inc16(g.dma_start(out=w_vc[p][:, :].rearrange("p (a f) -> p a f", a=8),
                                 in_=vcT[p].rearrange("(a p) f -> p a f", p=128)))
            dG.inc16(g.dma_start(out=w_uc[p][:, :], in_=ucat[p]))
            dG.inc16(g.dma_start(out=c_srow[p][:, :], in_=srow[p]))
        dG.inc16(g.dma_start(out=w_g1[:, :].rearrange("p (a f) -> p a f", a=8),
                             in_=gw1T.rearrange("(a p) f -> p a f", p=128)))
        dG.inc16(g.dma_start(out=w_g2[:, :].rearrange("p (a f) -> p a f", a=2),
                             in_=gw2T.rearrange("(a p) f -> p a f", p=128)))
        dG.inc16(g.dma_start(out=c_identb[:, :], in_=identb))
        dG.inc16(g.dma_start(out=c_identf[:, :], in_=identf))
        dG.inc16(g.dma_start(out=c_triw[:, :].rearrange("p (a c) -> p a c", a=2),
                             in_=triw.rearrange("(a p) c -> p a c", p=128)))
        dG.inc16(g.dma_start(out=c_trik[:, :].rearrange("p (a c) -> p a c", a=2),
                             in_=trik.rearrange("(a p) c -> p a c", p=128)))
        dG.inc16(g.dma_start(out=c_cnt[:, :], in_=cntinv))
        dG.inc16(g.dma_start(out=c_e33[:, :], in_=e33))
        dG.inc16(g.dma_start(out=c_e16[:, :], in_=e16))
        for dst, src in [(b_q, bcol["q"]), (b_k, bcol["k"]), (b_o, bcol["o"])]:
            dG.inc16(g.dma_start(out=dst[:, :], in_=src))
        dG.inc16(g.dma_start(out=b_v[:, :], in_=vb_row))
        dG.inc16(g.dma_start(out=c_nw[:, :], in_=nw_pa))
        dG.inc16(g.dma_start(out=c_gb1[:, :], in_=gb1))
        dG.inc16(g.dma_start(out=c_gb2[:, :], in_=gb2))
    v_w = dG.v

    @blk.vector
    def _(v):
        v.memset(ones_col[:, :], 1.0)
        v.memset(ones_row[:, :], 1.0)
        v.memset(ones_r2[:, :], 1.0)
        v.memset(ones_sq[:, :], 1.0)
        v.memset(a_scum[:, :], 0.0)
        v.memset(a_scbf[:, :], 0.0)
        v.memset(a_z[:, :], 0.0)
        v.memset(c_eps[:, :], EPS_LN)
        tV.inc(v.memset(p_c[:, 0:8], 0.0))
    v_ones = tV.v

    ctxA = contextlib.ExitStack()

    def sbA(name, shape, dtype):
        return ctxA.enter_context(nc.sbuf_tensor(name, shape, dtype))

    x_t = sbA("x_t", [128, 4 * 1024], F32)
    sq_scr = sbA("sq_scr", [128, 1024], BF16)
    xn_t = sbA("xn_t", [128, 4 * 1024], BF16)
    pool_bf = sbA("pool_bf", [128, 8], BF16)
    g_sc = sbA("g_sc", [1, 64], F32)
    g_h = sbA("g_h", [1, GH], F32)
    g_hb = sbA("g_hb", [128, 3], BF16)
    xf = sbA("xf", [128, 2 * 8 * 512], BF16)
    x2 = sbA("x2", [128, 8 * 512], BF16)
    xvs = sbA("xvs", [128, 512], BF16)
    prj = sbA("prj", [128, 4 * 512], BF16)
    prj2 = sbA("prj2", [128, 4 * 512], BF16)
    rms_scr = sbA("rms_scr", [1, 512], F32)
    rms_sq = sbA("rms_sq", [1, 512], F32)

    NB = 4
    v_xnw, v_xndma, v_pool = {}, {}, {}
    v_statdone = {}
    for it in range(NT // NB):
        @blk.sync
        def _(sy, it=it):
            if it >= 1:
                sy.wait_ge(tV.h, v_xnw[it - 1])
            for ti in range(NB):
                i = it * NB + ti
                dS.inc16(sy.dma_start(out=x_t[:, ti * 1024:(ti + 1) * 1024],
                                      in_=x[i * 128:(i + 1) * 128, :]))
        v_x = dS.v

        @blk.scalar
        def _(s, it=it, v_x=v_x):
            s.wait_ge(dS.h, v_x)
            if it == 0:
                s.wait_ge(tV.h, v_ones)
            if it >= 1:
                s.wait_ge(tV.h, v_statdone[it - 1])
            for ti in range(NB):
                inst = s.activation(sq_scr[:, :], x_t[:, ti * 1024:(ti + 1) * 1024],
                                    AF.Square, accum_out=st_sm[:, 4 + ti:5 + ti])
            tA.inc(inst)
        v_ssq = tA.v

        @blk.vector
        def _(v, it=it, v_x=v_x, v_ssq=v_ssq):
            v.wait_ge(dS.h, v_x)
            if it >= 1:
                v.wait_ge(dS.h, v_xndma[it - 1])
                v.wait_ge(tT.h, v_pool[it - 1])
            for ti in range(NB):
                inst = v.reduce_sum(st_sm[:, ti:ti + 1],
                                    x_t[:, ti * 1024:(ti + 1) * 1024], axis=AX)
            tV.inc(inst)
            v.wait_ge(tV.h, tV.v)
            tV.inc(v.tensor_scalar(st_sm[:, 0:4], st_sm[:, 0:4], 1.0 / D, None, OP.mult))
            v.wait_ge(tV.h, tV.v)
            tV.inc(v.tensor_mul(st_sm[:, 8:12], st_sm[:, 0:4], st_sm[:, 0:4]))
            v.wait_ge(tA.h, v_ssq)
            v.wait_ge(tV.h, tV.v)
            tV.inc(v.scalar_tensor_tensor(st_sm[:, 4:8], st_sm[:, 4:8], 1.0 / D,
                                          st_sm[:, 8:12], OP.mult, OP.subtract))
        v_var = tV.v

        @blk.scalar
        def _(s, v_var=v_var):
            s.wait_ge(tV.h, v_var)
            tA.inc(s.activation(st_sm[:, 4:8], st_sm[:, 4:8], AF.Sqrt, bias=c_eps[:, :]))
        v_sd = tA.v

        @blk.vector
        def _(v, it=it, v_sd=v_sd):
            v.wait_ge(tA.h, v_sd)
            tV.inc(v.reciprocal(st_sm[:, 8:12], st_sm[:, 4:8]))
            v.wait_ge(tV.h, tV.v)
            for ti in range(NB):
                inst = v.tensor_scalar(xn_t[:, ti * 1024:(ti + 1) * 1024],
                                       x_t[:, ti * 1024:(ti + 1) * 1024],
                                       st_sm[:, ti:ti + 1], st_sm[:, 8 + ti:9 + ti],
                                       OP.subtract, OP.mult)
            tV.inc(inst)
        v_xnw[it] = tV.v
        v_statdone[it] = tV.v

        @blk.tensor
        def _(t, it=it, vx=v_xnw[it]):
            t.wait_ge(tV.h, vx)
            for ti in range(NB):
                for ds in range(8):
                    inst = t.matmul(p_c[:, ds:ds + 1],
                                    xn_t[:, ti * 1024 + ds * 128: ti * 1024 + (ds + 1) * 128],
                                    ones_col[:, :], start=False,
                                    stop=(it == NT // NB - 1 and ti == NB - 1))
            tT.inc(inst)
        v_pool[it] = tT.v

        @blk.sync
        def _(sy, it=it, vx=v_xnw[it]):
            sy.wait_ge(tV.h, vx)
            dS.inc16(sy.dma_start(
                out=xn_tm.ap()[it * NB * 128:(it + 1) * NB * 128, :]
                    .rearrange("(ti p) c -> p ti c", p=128),
                in_=xn_t[:, :].rearrange("p (ti c) -> p ti c", ti=NB)))
        v_xndma[it] = dS.v
    v_xn_all = dS.v
    if "xn_d" in dbg_out:
        @blk.gpsimd
        def _(g):
            g.wait_ge(dS.h, v_xn_all)
            dG.inc16(g.dma_start(out=dbg_out["xn_d"], in_=xn_tm.ap()[:, :]))

    @blk.vector
    def _(v):
        v.wait_ge(tT.h, v_pool[NT // 4 - 1])
        v.wait_ge(dG.h, v_w)
        tV.inc(v.tensor_scalar(pool_bf[:, :], p_c[:, 0:8], 1.0 / T, None, OP.mult))
    v_poolbf = tV.v

    @blk.tensor
    def _(t):
        t.wait_ge(tV.h, v_poolbf)
        for ds in range(8):
            inst = t.matmul(p_d[0:1, 0:GH], pool_bf[:, ds:ds + 1],
                            w_g1[:, ds * 256:(ds + 1) * 256], start=(ds == 0), stop=(ds == 7))
        tT.inc(inst)
    v_g1 = tT.v

    @blk.vector
    def _(v):
        v.wait_ge(tT.h, v_g1)
        tV.inc(v.tensor_add(g_h[:, :], p_d[0:1, 0:GH], c_gb1[:, :]))
        v.wait_ge(tV.h, tV.v)
        tV.inc(v.tensor_relu(g_h[:, :], g_h[:, :]))
    v_h = tV.v

    @blk.tensor
    def _(t):
        t.wait_ge(tV.h, v_h)
        t.transpose(p_tf[:, 0:1], g_h[0:1, 0:128], c_identf[0:1, 0:1])
        tT.inc(t.transpose(p_tf[:, 1:2], g_h[0:1, 128:256], c_identf[0:1, 0:1]))
    v_ht = tT.v

    @blk.vector
    def _(v):
        v.wait_ge(tT.h, v_ht)
        tV.inc(v.tensor_copy(g_hb[:, 0:2], p_tf[:, 0:2]))
    v_hb = tV.v

    @blk.tensor
    def _(t):
        t.wait_ge(tV.h, v_hb)
        t.matmul(p_d[0:1, 256:256 + E], g_hb[:, 0:1], w_g2[:, 0:8], start=True, stop=False)
        tT.inc(t.matmul(p_d[0:1, 256:256 + E], g_hb[:, 1:2], w_g2[:, 8:16],
                        start=False, stop=True))
    v_g2 = tT.v

    @blk.vector
    def _(v):
        v.wait_ge(tT.h, v_g2)
        tV.inc(v.tensor_add(g_sc[:, 0:8], p_d[0:1, 256:256 + E], c_gb2[:, :]))
        v.wait_ge(tV.h, tV.v)
        tV.inc(v.reduce_max(g_sc[:, 8:9], g_sc[:, 0:8], axis=AX))
        v.wait_ge(tV.h, tV.v)
        tV.inc(v.tensor_scalar(g_sc[:, 16:24], g_sc[:, 0:8], g_sc[:, 8:9], None, OP.is_ge))
        v.wait_ge(tV.h, tV.v)
        tV.inc(v.scalar_tensor_tensor(g_sc[:, 24:32], g_sc[:, 16:24], -1e30,
                                      g_sc[:, 0:8], OP.mult, OP.add))
        v.wait_ge(tV.h, tV.v)
        tV.inc(v.reduce_max(g_sc[:, 9:10], g_sc[:, 24:32], axis=AX))
        v.wait_ge(tV.h, tV.v)
        tV.inc(v.tensor_scalar(g_sc[:, 24:32], g_sc[:, 24:32], g_sc[:, 9:10], None, OP.is_ge))
        tV.inc(v.tensor_sub(g_sc[:, 32:33], g_sc[:, 8:9], g_sc[:, 9:10]))
    v_top = tV.v

    @blk.scalar
    def _(s):
        s.wait_ge(tV.h, v_top)
        tA.inc(s.activation(g_sc[:, 32:33], g_sc[:, 32:33], AF.Sigmoid))
    v_sig = tA.v

    @blk.vector
    def _(v):
        v.wait_ge(tA.h, v_sig)
        tV.inc(v.tensor_scalar(g_sc[:, 40:48], g_sc[:, 16:24], g_sc[:, 32:33], None, OP.mult))
        tV.inc(v.scalar_tensor_tensor(g_sc[:, 48:56], g_sc[:, 24:32], g_sc[:, 32:33],
                                      g_sc[:, 24:32], OP.mult, OP.subtract))
        v.wait_ge(tV.h, tV.v)
        tV.inc(v.tensor_sub(g_sc[:, 40:48], g_sc[:, 40:48], g_sc[:, 48:56]))
    v_gate = tV.v

    @blk.tensor
    def _(t):
        t.wait_ge(tV.h, v_gate)
        tT.inc(t.transpose(p_tf[0:8, 2:3], g_sc[0:1, 40:48], c_identf[0:1, 0:1]))
        for ji, p in enumerate("qkvo"):
            tT.inc(t.transpose(p_tf[:, 3 + ji:4 + ji], c_srow[p][0:1, :],
                               c_identf[0:1, 0:1]))
    v_gt = tT.v

    @blk.vector
    def _(v):
        v.wait_ge(tT.h, v_gt)
        tV.inc(v.tensor_copy(g_hb[0:8, 2:3], p_tf[0:8, 2:3]))
    v_gtb = tV.v

    @blk.tensor
    def _(t):
        t.wait_ge(tV.h, v_gtb)
        tT.inc(t.matmul(p_tf[:, 7:8], c_e16[:, :], g_hb[0:E, 2:3], start=True, stop=True))
    v_ge = tT.v

    @blk.vector
    def _(v):
        v.wait_ge(tT.h, v_ge)
        tV.inc(v.tensor_copy(ge128[:, :], p_tf[:, 7:8]))
        v.wait_ge(tV.h, tV.v)
        for ji, p in enumerate("qkvo"):
            inst = v.tensor_mul(scl128[p][:, :], ge128[:, :], p_tf[:, 3 + ji:4 + ji])
        tV.inc(inst)
        if "gate_d" in dbg_out:
            tV.inc(v.tensor_copy(g_h[0:1, 0:8], g_sc[0:1, 40:48]))
    v_scl = tV.v

    if "pooled_d" in dbg_out or "gate_d" in dbg_out:
        @blk.gpsimd
        def _(g):
            if "pooled_d" in dbg_out:
                g.wait_ge(tV.h, v_poolbf)
                dG.inc16(g.dma_start(out=dbg_out["pooled_d"], in_=pool_bf[:, :]))
            if "gate_d" in dbg_out:
                g.wait_ge(tV.h, v_scl)
                dG.inc16(g.dma_start(out=dbg_out["gate_d"], in_=g_h[0:1, 0:8]))

    v_tc_pe, v_tc_ve, v_prjdma, v_prj2dma, v_rms = {}, {}, {}, {}, {}
    last_pd_read = v_top
    last_xvs = [v_scl]
    last_pab_evac = [v_scl]
    for tc in range(NTC):
        fa = tc % 2
        xo = fa * 4096

        @blk.sync
        def _(sy, tc=tc, xo=xo):
            if tc >= 2:
                sy.wait_ge(tT.h, v_tc_pe[tc - 2])
                sy.wait_ge(tV.h, v_tc_ve[tc - 2])
            sy.wait_ge(dS.h, v_xn_all)
            for dt in range(8):
                inst = sy.dma_start(
                    out=xf[:, xo + dt * 512: xo + (dt + 1) * 512],
                    in_=xn_tm.ap()[tc * 512:(tc + 1) * 512, dt * 128:(dt + 1) * 128],
                    transpose=True)
                dS.inc16(inst)
        v_xf = dS.v

        @blk.vector
        def _(v, tc=tc, xo=xo, v_xf=v_xf):
            v.wait_ge(dS.h, v_xf)
            if tc >= 1:
                v.wait_ge(tT.h, v_rms[tc - 1])
            tV.inc(v.tensor_mul(x2[:, :], xf[:, xo:xo + 4096], xf[:, xo:xo + 4096]))
        v_x2 = tV.v

        @blk.tensor
        def _(t, tc=tc, v_x2=v_x2, lpd=last_pd_read):
            t.wait_ge(tV.h, v_x2)
            t.wait_ge(tV.h, lpd)
            for dt in range(8):
                inst = t.matmul(p_d[0:1, 0:512], ones_col[:, :],
                                x2[:, dt * 512:(dt + 1) * 512],
                                start=(dt == 0), stop=(dt == 7))
            tT.inc(inst)
        v_rms[tc] = tT.v

        @blk.vector
        def _(v, vr=v_rms[tc]):
            v.wait_ge(tT.h, vr)
            tV.inc(v.tensor_scalar(rms_scr[:, :], p_d[0:1, 0:512], 1.0 / D, EPS_RMS,
                                   OP.mult, OP.add))
        v_rsc = tV.v
        last_pd_read = v_rsc

        @blk.scalar
        def _(s, v_rsc=v_rsc):
            s.wait_ge(tV.h, v_rsc)
            tA.inc(s.activation(rms_sq[:, :], rms_scr[:, :], AF.Sqrt))
        v_rq = tA.v

        @blk.vector
        def _(v, tc=tc, v_rq=v_rq):
            v.wait_ge(tA.h, v_rq)
            tV.inc(v.reciprocal(rms_row[0:1, tc * 512:(tc + 1) * 512], rms_sq[:, :]))

        for pi, p in enumerate("qkv"):
            wm = w["f" + p]

            @blk.tensor
            def _(t, tc=tc, xo=xo, p=p, pi=pi, lxv=last_xvs[0]):
                if tc == 0 and pi == 0:
                    t.wait_ge(tV.h, v_scl)
                    t.wait_ge(tV.h, v_poolbf)
                else:
                    t.wait_ge(tV.h, lxv)
                for dt in range(8):
                    inst = t.matmul(p_c[:, 0:512], w_vc[p][:, dt * 128:(dt + 1) * 128],
                                    xf[:, xo + dt * 512: xo + (dt + 1) * 512],
                                    start=(dt == 0), stop=(dt == 7))
                tT.inc(inst)
            v_xv = tT.v

            @blk.vector
            def _(v, p=p, v_xv=v_xv):
                v.wait_ge(tT.h, v_xv)
                tV.inc(v.tensor_scalar(xvs[:, :], p_c[:, 0:512], scl128[p][:, :],
                                       None, OP.mult))
            last_xvs = [tV.v]
            v_xvs = tV.v

            if p in "qk":
                stg = prj if p == "q" else prj2

                @blk.tensor
                def _(t, tc=tc, xo=xo, p=p, wm=wm, v_xvs=v_xvs, lpe=last_pab_evac[0]):
                    t.wait_ge(tV.h, v_xvs)
                    t.wait_ge(tV.h, lpe)
                    for fs in range(4):
                        pp = p_a if fs < 2 else p_b
                        po = (fs % 2) * 512
                        for dt in range(8):
                            t.matmul(pp[:, po:po + 512],
                                     wm[:, dt * 512 + fs * 128: dt * 512 + fs * 128 + 128],
                                     xf[:, xo + dt * 512: xo + (dt + 1) * 512],
                                     start=(dt == 0), stop=False)
                        inst = t.matmul(pp[:, po:po + 512],
                                        w_uc[p][:, fs * 128:(fs + 1) * 128], xvs[:, :],
                                        start=False, stop=True)
                    tT.inc(inst)
                v_mm = tT.v

                @blk.vector
                def _(v, tc=tc, p=p, stg=stg, v_mm=v_mm):
                    v.wait_ge(tT.h, v_mm)
                    if tc >= 1:
                        v.wait_ge(dS.h, (v_prjdma if p == "q" else v_prj2dma)[tc - 1])
                    for fs in range(4):
                        pp = p_a if fs < 2 else p_b
                        po = (fs % 2) * 512
                        if p == "q":
                            inst = v.tensor_scalar(stg[:, fs * 512:(fs + 1) * 512],
                                                   pp[:, po:po + 512], b_q[:, fs:fs + 1],
                                                   SCALE, OP.add, OP.mult)
                        else:
                            inst = v.tensor_scalar(stg[:, fs * 512:(fs + 1) * 512],
                                                   pp[:, po:po + 512], b_k[:, fs:fs + 1],
                                                   None, OP.add)
                    tV.inc(inst)
                v_stg = tV.v
                last_pab_evac = [v_stg]

                @blk.sync
                def _(sy, tc=tc, p=p, stg=stg, v_stg=v_stg):
                    sy.wait_ge(tV.h, v_stg)
                    dst = qf if p == "q" else kf
                    for fs in range(4):
                        inst = sy.dma_start(
                            out=dst.ap()[(fs * NTC + tc) * 128:(fs * NTC + tc) * 128 + 128, :],
                            in_=stg[:, fs * 512:(fs + 1) * 512])
                        dS.inc16(inst)
                if p == "q":
                    v_prjdma[tc] = dS.v
                else:
                    v_prj2dma[tc] = dS.v

                if ("qf_d" in dbg_out and p == "q") or ("kf_d" in dbg_out and p == "k"):
                    @blk.gpsimd
                    def _(g, tc=tc, p=p, stg=stg, v_stg=v_stg):
                        g.wait_ge(tV.h, v_stg)
                        dd = dbg_out["qf_d" if p == "q" else "kf_d"]
                        for fs in range(4):
                            dG.inc16(g.dma_start(
                                out=dd[(fs * NTC + tc) * 128:(fs * NTC + tc) * 128 + 128, :],
                                in_=stg[:, fs * 512:(fs + 1) * 512]))
            else:
                @blk.tensor
                def _(t, tc=tc, xo=xo, wm=wm, v_xvs=v_xvs, lpe=last_pab_evac[0]):
                    t.wait_ge(tV.h, v_xvs)
                    t.wait_ge(tV.h, lpe)
                    t.wait_ge(tV.h, v_ones)
                    for mt in range(4):
                        pp = p_a if mt < 2 else p_b
                        po = (mt % 2) * 512
                        for dt in range(8):
                            t.matmul(pp[:, po:po + 512],
                                     xf[:, xo + dt * 512 + mt * 128: xo + dt * 512 + mt * 128 + 128],
                                     wm[:, dt * 512:(dt + 1) * 512],
                                     start=(dt == 0), stop=False)
                        t.matmul(pp[:, po:po + 512], xvs[:, mt * 128:(mt + 1) * 128],
                                 w_uc["v"][:, :], start=False, stop=False)
                        inst = t.matmul(pp[:, po:po + 512], ones_row[:, :], b_v[:, :],
                                        start=False, stop=True)
                    tT.inc(inst)
                v_vmm = tT.v

                @blk.vector
                def _(v, tc=tc, v_vmm=v_vmm):
                    v.wait_ge(tT.h, v_vmm)
                    v.wait_ge(dS.h, v_prjdma[tc])
                    for mt in range(4):
                        pp = p_a if mt < 2 else p_b
                        po = (mt % 2) * 512
                        inst = v.tensor_copy(prj[:, mt * 512:(mt + 1) * 512], pp[:, po:po + 512])
                    tV.inc(inst)
                v_vst = tV.v
                last_pab_evac = [v_vst]

                @blk.sync
                def _(sy, tc=tc, v_vst=v_vst):
                    sy.wait_ge(tV.h, v_vst)
                    for pr in range(4):
                        inst = sy.dma_start(
                            out=vtok.ap()[(pr * 64 + tc * 4) * 128:(pr * 64 + tc * 4 + 4) * 128, :]
                                .rearrange("(mt p) f -> p mt f", p=128),
                            in_=prj[:, :].rearrange("p (mt f) -> p mt f", mt=4)[
                                :, :, pr * 128:(pr + 1) * 128])
                        dS.inc16(inst)

                if "vt_d" in dbg_out:
                    @blk.gpsimd
                    def _(g, tc=tc, v_vst=v_vst):
                        g.wait_ge(tV.h, v_vst)
                        for pr in range(4):
                            dG.inc16(g.dma_start(
                                out=dbg_out["vt_d"][(pr * 64 + tc * 4) * 128:(pr * 64 + tc * 4 + 4) * 128, :]
                                    .rearrange("(mt p) f -> p mt f", p=128),
                                in_=prj[:, :].rearrange("p (mt f) -> p mt f", mt=4)[
                                    :, :, pr * 128:(pr + 1) * 128]))
        v_tc_pe[tc] = tT.v
        v_tc_ve[tc] = tV.v

    if "rms_d" in dbg_out:
        @blk.gpsimd
        def _(g):
            g.wait_ge(tV.h, tV.v)
            dG.inc16(g.dma_start(out=dbg_out["rms_d"], in_=rms_row[:, :]))

    bA = barrier()
    ctxA.close()

    ctxB = contextlib.ExitStack()

    def sbB(name, shape, dtype):
        return ctxB.enter_context(nc.sbuf_tensor(name, shape, dtype))

    a_xn = sbB("a_xn", [128, 2 * 8 * 256], BF16)
    a_xn2 = sbB("a_xn2", [128, 8 * 256], BF16)
    a_rbc = sbB("a_rbc", [128, 256], BF16)
    a_pq = sbB("a_pq", [128, 4 * 256], BF16)
    a_pk = sbB("a_pk", [128, 4 * 256], BF16)
    a_vc = sbB("a_vc", [128, 4 * 256], BF16)
    a_el = sbB("a_el", [128, 4 * 256], BF16)
    a_pkT = sbB("a_pkT", [128, 2 * 512], BF16)
    a_vcT = sbB("a_vcT", [128, 2 * 512], BF16)
    a_atm = sbB("a_atm", [128, 2 * 512], BF16)
    a_zbf = sbB("a_zbf", [128, 4 * 256], BF16)
    a_pt = sbB("a_pt", [128, 4 * 256], BF16)
    a_u = sbB("a_u", [128, 4 * 256], BF16)
    a_den = sbB("a_den", [128, 4 * 256], F32)
    a_den2 = sbB("a_den2", [128, 4 * 256], BF16)
    a_den3 = sbB("a_den3", [128, 4 * 256], F32)
    a_bc = sbB("a_bc", [128, 4 * 256], BF16)
    a_pre = sbB("a_pre", [128, 4 * 256], BF16)
    a_ms = sbB("a_ms", [128, 4 * 32], BF16)
    a_scst = sbB("a_scst", [128, 2 * 33], F32)
    a_wm = sbB("a_wm", [128, 2 * 256], F32)

    v_ch = {}
    v_axn_d = {}
    v_axn_cons = {}
    v_rbcd, v_xn2d = {}, {}
    v_prem_hist = [0]
    v_u_hist = [0]

    def emit_head(mm, v_pc_free):
        @blk.tensor
        def _(t, mm=mm, v_pc_free=v_pc_free):
            if mm == 0:
                wait_barrier(t, bA)
            elif v_pc_free:
                t.wait_ge(tV.h, v_pc_free)
            tT.inc(t.matmul(p_c[:, 0:256], ones_row[:, :],
                            rms_row[0:1, mm * 256:(mm + 1) * 256],
                            start=True, stop=True))
        v_rbcm = tT.v
        v_rbcd[mm] = v_rbcm

        @blk.vector
        def _(v, mm=mm, v_rbcm=v_rbcm):
            if mm == 0:
                wait_barrier(v, bA)
            v.wait_ge(tT.h, v_rbcm)
            tV.inc(v.tensor_copy(a_rbc[:, :], p_c[:, 0:256]))
            v.wait_ge(dS.h, v_axn_d[mm])
            v.wait_ge(tV.h, tV.v)
            if mm >= 1:
                v.wait_ge(tT.h, v_vmm_hist[0])
            xb_ = (mm % 2) * 2048
            for dt in range(8):
                inst = v.scalar_tensor_tensor(a_xn2[:, dt * 256:(dt + 1) * 256],
                                              a_xn[:, xb_ + dt * 256: xb_ + (dt + 1) * 256],
                                              c_nw[:, dt:dt + 1], a_rbc[:, :],
                                              OP.mult, OP.mult)
            tV.inc(inst)
        v_xn2d[mm] = tV.v
    v_vmm_hist = [0]

    @blk.sync
    def _(sy):
        wait_barrier(sy, bA)
        for dt in range(8):
            inst = sy.dma_start(
                out=a_xn[:, dt * 256:(dt + 1) * 256],
                in_=xn_tm.ap()[0:256, dt * 128:(dt + 1) * 128],
                transpose=True)
            dS.inc16(inst)
    v_axn_d[0] = dS.v

    for m in range(NCH):
        xb = (m % 2) * 2048

        @blk.sync
        def _(sy, m=m):
            if m + 1 < NCH:
                xb2 = ((m + 1) % 2) * 2048
                if m >= 1:
                    ca, cb = v_axn_cons[m - 1]
                    sy.wait_ge(tT.h, ca)
                    sy.wait_ge(tV.h, cb)
                for dt in range(8):
                    inst = sy.dma_start(
                        out=a_xn[:, xb2 + dt * 256: xb2 + (dt + 1) * 256],
                        in_=xn_tm.ap()[(m + 1) * 256:(m + 2) * 256, dt * 128:(dt + 1) * 128],
                        transpose=True)
                    dS.inc16(inst)
                v_axn_d[m + 1] = dS.v
        v_axn = v_axn_d[m]

        if m == 0:
            emit_head(0, None)
        v_rbc = v_rbcd[m]
        v_xn2 = v_xn2d[m]

        @blk.tensor
        def _(t, v_xn2=v_xn2, vp=v_prem_hist[0]):
            t.wait_ge(tV.h, v_xn2)
            if vp:
                t.wait_ge(tV.h, vp)
            for fs in range(4):
                for dt in range(8):
                    inst = t.matmul(p_a[:, fs * 256:(fs + 1) * 256],
                                    w["wq"][:, dt * 512 + fs * 128: dt * 512 + fs * 128 + 128],
                                    a_xn2[:, dt * 256:(dt + 1) * 256],
                                    start=(dt == 0), stop=(dt == 7))
            tT.inc(inst)
        v_qmm = tT.v

        @blk.vector
        def _(v, v_qmm=v_qmm):
            v.wait_ge(tT.h, v_qmm)
            tV.inc(v.tensor_scalar(a_el[:, :], p_a[:, :], 0.0, None, OP.min))
        v_min = tV.v

        @blk.scalar
        def _(s, m=m, v_min=v_min):
            if m == 0:
                wait_barrier(s, bA)
            s.wait_ge(tV.h, v_min)
            tA.inc(s.activation(a_el[:, :], a_el[:, :], AF.Exp))
        v_exp = tA.v

        @blk.vector
        def _(v, m=m, v_exp=v_exp):
            v.wait_ge(tA.h, v_exp)
            if m >= 1:
                v.wait_ge(dS.h, v_ch[m - 1][3])
            tV.inc(v.scalar_tensor_tensor(a_pq[:, :], p_a[:, :], 0.0, a_el[:, :],
                                          OP.max, OP.add))
        v_pq = tV.v

        @blk.tensor
        def _(t, v_xn2=v_xn2, vu=v_u_hist[0]):
            if vu:
                t.wait_ge(tV.h, vu)
            for fs in range(4):
                for dt in range(8):
                    inst = t.matmul(p_b[:, fs * 256:(fs + 1) * 256],
                                    w["wk"][:, dt * 512 + fs * 128: dt * 512 + fs * 128 + 128],
                                    a_xn2[:, dt * 256:(dt + 1) * 256],
                                    start=(dt == 0), stop=(dt == 7))
            tT.inc(inst)
        v_kmm = tT.v

        @blk.vector
        def _(v, v_kmm=v_kmm, v_pq=v_pq):
            v.wait_ge(tT.h, v_kmm)
            v.wait_ge(tV.h, v_pq)
            tV.inc(v.tensor_scalar(a_el[:, :], p_b[:, :], 0.0, None, OP.min))
        v_mink = tV.v

        @blk.scalar
        def _(s, v_mink=v_mink):
            s.wait_ge(tV.h, v_mink)
            tA.inc(s.activation(a_el[:, :], a_el[:, :], AF.Exp))
        v_expk = tA.v

        @blk.vector
        def _(v, v_expk=v_expk):
            v.wait_ge(tA.h, v_expk)
            tV.inc(v.scalar_tensor_tensor(a_pk[:, :], p_b[:, :], 0.0, a_el[:, :],
                                          OP.max, OP.add))
        v_pk = tV.v

        @blk.tensor
        def _(t, v_pq=v_pq, v_xn2=v_xn2):
            t.wait_ge(tV.h, v_pq)
            for fs in range(4):
                for dt in range(8):
                    inst = t.matmul(p_a[:, fs * 256:(fs + 1) * 256],
                                    w["wv"][:, dt * 512 + fs * 128: dt * 512 + fs * 128 + 128],
                                    a_xn2[:, dt * 256:(dt + 1) * 256],
                                    start=(dt == 0), stop=(dt == 7))
            tT.inc(inst)
        v_vmm = tT.v

        @blk.vector
        def _(v, v_vmm=v_vmm):
            v.wait_ge(tT.h, v_vmm)
            tV.inc(v.tensor_copy(a_vc[:, :], p_a[:, :]))
        v_vc = tV.v
        v_vmm_hist[0] = v_vmm
        if m + 1 < NCH:
            emit_head(m + 1, v_vc)

        tr_jobs = [("pk", fs, ct) for ct in range(2) for fs in range(4)] + \
                  [("vc", fs, ct) for ct in range(2) for fs in range(4)]
        v_trev = []
        v_zmm_l = [None]
        for r in range(4):
            @blk.tensor
            def _(t, r=r, v_pk=v_pk, v_vc=v_vc):
                t.wait_ge(tV.h, v_pk if r < 2 else v_vc)
                if r == 2:
                    t.wait_ge(tV.h, v_vc)
                if r >= 1:
                    t.wait_ge(tV.h, v_trev[r - 1])
                for ji in range(4):
                    which, fs, ct = tr_jobs[r * 4 + ji]
                    src = a_pk if which == "pk" else a_vc
                    inst = t.transpose(p_t[:, ji * 128:(ji + 1) * 128],
                                       src[:, fs * 256 + ct * 128: fs * 256 + ct * 128 + 128],
                                       c_identb[:, :])
                tT.inc(inst)
            v_trm = tT.v

            @blk.vector
            def _(v, r=r, v_trm=v_trm):
                v.wait_ge(tT.h, v_trm)
                for ji in range(4):
                    which, fs, ct = tr_jobs[r * 4 + ji]
                    dst = a_pkT if which == "pk" else a_vcT
                    inst = v.tensor_copy(dst[:, ct * 512 + fs * 128: ct * 512 + fs * 128 + 128],
                                         p_t[:, ji * 128:(ji + 1) * 128])
                tV.inc(inst)
            v_trev.append(tV.v)
            if r == 1:
                @blk.tensor
                def _(t, v_vc=v_vc, v_tr=v_trev[-1]):
                    t.wait_ge(tV.h, v_vc)
                    t.wait_ge(tV.h, v_tr)
                    for h in range(8):
                        po, j = hsl(h)
                        for ct in range(2):
                            inst = t.matmul(p_a[po:po + 64, j * 256:(j + 1) * 256],
                                            a_pkT[:, ct * 512 + j * 128 + po: ct * 512 + j * 128 + po + 64],
                                            c_trik[:, ct * 256:(ct + 1) * 256],
                                            start=(ct == 0), stop=(ct == 1))
                    tT.inc(inst)
                v_zmm_l[0] = tT.v
        v_tr = v_trev[-1]
        v_zmm = v_zmm_l[0]

        @blk.vector
        def _(v, v_zmm=v_zmm):
            v.wait_ge(tT.h, v_zmm)
            for j in range(4):
                inst = v.tensor_scalar(a_zbf[:, j * 256:(j + 1) * 256],
                                       p_a[:, j * 256:(j + 1) * 256],
                                       a_z[:, j:j + 1], None, OP.add)
            tV.inc(inst)
            v.wait_ge(tV.h, tV.v)
            tV.inc(v.memset(p_a[:, :], 0.0))
            v.wait_ge(tV.h, tV.v)
            tV.inc(v.tensor_mul(a_pt[:, :], a_pq[:, :], a_zbf[:, :]))
            v.wait_ge(tV.h, tV.v)
            for j in range(4):
                inst = v.reduce_sum(a_stat[:, j:j + 1], a_pk[:, j * 256:(j + 1) * 256], axis=AX)
            tV.inc(inst)
            v.wait_ge(tV.h, tV.v)
            tV.inc(v.tensor_add(a_z[:, 0:4], a_z[:, 0:4], a_stat[:, 0:4]))
        v_zup = tV.v

        @blk.tensor
        def _(t, v_tr=v_tr):
            t.wait_ge(tV.h, v_tr)
            for h in range(8):
                po, j = hsl(h)
                for ct in range(2):
                    inst = t.matmul(p_tf[po:po + 64, j * 64:(j + 1) * 64],
                                    a_pkT[:, ct * 512 + j * 128 + po: ct * 512 + j * 128 + po + 64],
                                    a_vcT[:, ct * 512 + j * 128 + po: ct * 512 + j * 128 + po + 64],
                                    start=(ct == 0), stop=(ct == 1))
            tT.inc(inst)
        v_sch = tT.v

        @blk.tensor
        def _(t, v_zup=v_zup):
            t.wait_ge(tV.h, v_zup)
            for h in range(8):
                po, j = hsl(h)
                inst = t.matmul(p_b[po:po + 1, j * 256:(j + 1) * 256],
                                ones_col[po:po + 64, 0:1],
                                a_pt[po:po + 64, j * 256:(j + 1) * 256],
                                start=True, stop=True,
                                tile_position=(po, po))
            tT.inc(inst)
        v_den_mm = tT.v

        @blk.vector
        def _(v, v_den_mm=v_den_mm):
            v.wait_ge(tT.h, v_den_mm)
            tV.inc(v.tensor_scalar(a_den[:, :], p_b[:, :], EPS_DEN, None, OP.add))
        v_den_ev = tV.v

        @blk.tensor
        def _(t, v_zup=v_zup):
            t.wait_ge(tV.h, v_zup)
            for h in range(8):
                po, j = hsl(h)
                inst = t.matmul(p_a[po:po + 64, j * 256:(j + 1) * 256],
                                a_scbf[po:po + 64, j * 64:(j + 1) * 64],
                                a_pq[po:po + 64, j * 256:(j + 1) * 256],
                                start=False, stop=False, skip_group_check=True)
            tT.inc(inst)
        v_inter = tT.v

        @blk.vector
        def _(v, m=m, v_sch=v_sch, v_inter=v_inter):
            v.wait_ge(tT.h, v_sch)
            v.wait_ge(tT.h, v_inter)
            tV.inc(v.tensor_add(a_scum[:, :], a_scum[:, :], p_tf[:, 0:256]))
            v.wait_ge(tV.h, tV.v)
            tV.inc(v.tensor_copy(a_scbf[:, :], a_scum[:, :]))
            v.wait_ge(tV.h, tV.v)
            for j in range(4):
                inst = v.tensor_copy(a_hist[:, (j * NCH + m) * 64:(j * NCH + m) * 64 + 64],
                                     a_scum[:, j * 64:(j + 1) * 64])
            tV.inc(inst)
        v_scup = tV.v

        v_at_ev = [None, None]
        horder = [3, 2, 1, 0, 7, 6, 5, 4]

        def emit_atm(h):
            po, j = hsl(h)
            reg = h % 2

            @blk.tensor
            def _(t, h=h, po=po, j=j, reg=reg, v_den_ev=v_den_ev):
                if v_at_ev[reg] is None:
                    t.wait_ge(tV.h, v_den_ev)
                else:
                    t.wait_ge(tV.h, v_at_ev[reg])
                for ds in range(2):
                    inst = t.matmul(p_b[:, reg * 512 + ds * 256: reg * 512 + (ds + 1) * 256],
                                    a_pk[po:po + 64, j * 256 + ds * 128: j * 256 + ds * 128 + 128],
                                    a_pq[po:po + 64, j * 256:(j + 1) * 256],
                                    start=True, stop=True)
                tT.inc(inst)
            v_atm = tT.v

            @blk.vector
            def _(v, reg=reg, v_atm=v_atm):
                v.wait_ge(tT.h, v_atm)
                tV.inc(v.tensor_mul(a_atm[:, reg * 512:(reg + 1) * 512],
                                    p_b[:, reg * 512:(reg + 1) * 512],
                                    c_trik[:, 0:512]))
            v_at_ev[reg] = tV.v

        def emit_vct(h):
            po, j = hsl(h)
            reg = h % 2

            @blk.tensor
            def _(t, h=h, po=po, j=j, reg=reg, vae=v_at_ev[reg]):
                t.wait_ge(tV.h, vae)
                for ds in range(2):
                    inst = t.matmul(p_a[po:po + 64, j * 256:(j + 1) * 256],
                                    a_vcT[:, ds * 512 + j * 128 + po: ds * 512 + j * 128 + po + 64],
                                    a_atm[:, reg * 512 + ds * 256: reg * 512 + (ds + 1) * 256],
                                    start=False, stop=(ds == 1), skip_group_check=True)
                tT.inc(inst)

        emit_atm(horder[0])
        for i in range(1, 8):
            emit_atm(horder[i])
            emit_vct(horder[i - 1])
        emit_vct(horder[7])
        v_intra = tT.v

        @blk.vector
        def _(v, v_den_ev=v_den_ev):
            v.wait_ge(tV.h, v_den_ev)
            tV.inc(v.reciprocal(a_den2[:, 0:512], a_den[:, 0:512]))
            tV.inc(v.reciprocal(a_den2[:, 512:1024], a_den[:, 512:1024]))
        v_den1, v_den = tV.v - 1, tV.v

        @blk.tensor
        def _(t, v_den1=v_den1, v_den=v_den, v_scup=v_scup):
            t.wait_ge(tV.h, v_den1)
            t.wait_ge(tV.h, v_scup)
            for h in (0, 1, 2, 3):
                po, j = hsl(h)
                inst = t.matmul(p_tf[po:po + 64, (j % 2) * 256:(j % 2) * 256 + 256],
                                ones_sq[po:po + 1, 0:64],
                                a_den2[po:po + 1, j * 256:(j + 1) * 256],
                                start=True, stop=True)
            t.wait_ge(tV.h, v_den)
            for h in (4, 5, 6, 7):
                po, j = hsl(h)
                inst = t.matmul(p_c[po:po + 64, (j % 2) * 256:(j % 2) * 256 + 256],
                                ones_sq[po:po + 1, 0:64],
                                a_den2[po:po + 1, j * 256:(j + 1) * 256],
                                start=True, stop=True)
            tT.inc(inst)
        v_bcm = tT.v

        @blk.vector
        def _(v, v_bcm=v_bcm):
            v.wait_ge(tT.h, v_bcm)
            for h in range(8):
                po, j = hsl(h)
                pp, co = (p_tf, (j % 2) * 256) if h < 4 else (p_c, (j % 2) * 256)
                inst = v.tensor_copy(a_bc[po:po + 64, j * 256:(j + 1) * 256],
                                     pp[po:po + 64, co:co + 256])
            tV.inc(inst)
        v_bc = tV.v

        @blk.vector
        def _(v, m=m, v_intra=v_intra, v_bc=v_bc):
            v.wait_ge(tT.h, v_intra)
            v.wait_ge(tV.h, v_bc)
            if m >= 1:
                v.wait_ge(dS.h, v_ch[m - 1][3])
            if "num_d" in dbg_out:
                tV.inc(v.tensor_copy(a_den[:, :], p_a[:, :]))
            tV.inc(v.tensor_mul(a_pre[:, :], p_a[:, :], a_bc[:, :]))
        v_prem = tV.v
        if "num_d" in dbg_out:
            @blk.gpsimd
            def _(g, m=m, v_prem=v_prem):
                g.wait_ge(tV.h, v_prem)
                dG.inc16(g.dma_start(out=dbg_out["num_d"][:, m * 1024:(m + 1) * 1024],
                                     in_=a_den[:, :]))

        @blk.tensor
        def _(t, va0=v_at_ev[0], va1=v_at_ev[1], v_axn=v_axn, xb=xb):
            t.wait_ge(tV.h, va0)
            t.wait_ge(tV.h, va1)
            for fs in range(4):
                for dt in range(8):
                    inst = t.matmul(p_b[:, fs * 256:(fs + 1) * 256],
                                    w["wu"][:, dt * 512 + fs * 128: dt * 512 + fs * 128 + 128],
                                    a_xn[:, xb + dt * 256: xb + (dt + 1) * 256],
                                    start=(dt == 0), stop=(dt == 7))
            tT.inc(inst)
        v_umm = tT.v

        @blk.vector
        def _(v, m=m, v_umm=v_umm, xb=xb):
            v.wait_ge(tT.h, v_umm)
            tV.inc(v.tensor_copy(a_u[:, :], p_b[:, :]))
            for j in range(4):
                inst = v.reduce_sum(a_ms[:, j * 32 + m: j * 32 + m + 1],
                                    a_xn[:, xb + j * 256: xb + (j + 1) * 256], axis=AX)
            tV.inc(inst)
        v_u = tV.v

        @blk.tensor
        def _(t, m=m, v_u=v_u, v_bc=v_bc, xb=xb):
            t.wait_ge(tV.h, v_u)
            t.wait_ge(tV.h, v_bc)
            if m > 0:
                for cs in range(2):
                    for fs in range(4):
                        inst = t.matmul(p_tf[0:128, cs * 32: cs * 32 + m],
                                        a_u[:, fs * 256 + cs * 128: fs * 256 + cs * 128 + 128],
                                        a_ms[:, fs * 32: fs * 32 + m],
                                        start=(fs == 0), stop=(fs == 3))
                tT.inc(inst)
            for cs in range(2):
                for fs in range(4):
                    inst = t.matmul(p_d[:, cs * 256:(cs + 1) * 256],
                                    a_u[:, fs * 256 + cs * 128: fs * 256 + cs * 128 + 128],
                                    a_xn[:, xb + fs * 256: xb + (fs + 1) * 256],
                                    start=(fs == 0), stop=(fs == 3))
            tT.inc(inst)
        v_wmm = tT.v

        @blk.vector
        def _(v, m=m, v_wmm=v_wmm):
            if m >= 1:
                v.wait_ge(dS.h, v_ch[m - 1][4])
            v.memset(a_scst[:, :], NEG_PAD)
            v.wait_ge(tT.h, v_wmm)
            if m > 0:
                for cs in range(2):
                    inst = v.tensor_scalar(a_scst[:, cs * 33: cs * 33 + m],
                                           p_tf[:, cs * 32: cs * 32 + m],
                                           1.0 / 8192.0, None, OP.mult)
                tV.inc(inst)
            tV.inc(v.tensor_mul(a_wm[:, :], p_d[:, :], c_triw[:, 0:512]))
            v.wait_ge(tV.h, tV.v)
            for cs in range(2):
                inst = v.reduce_sum(a_stat[:, 4 + cs:5 + cs],
                                    a_wm[:, cs * 256:(cs + 1) * 256], axis=AX)
            tV.inc(inst)
            v.wait_ge(tV.h, tV.v)
            for cs in range(2):
                inst = v.tensor_scalar(a_scst[:, cs * 33 + 32: cs * 33 + 33],
                                       a_stat[:, 4 + cs:5 + cs],
                                       c_cnt[:, cs:cs + 1], 1.0 / 32.0, OP.mult, OP.mult)
            tV.inc(inst)
        v_sc = tV.v

        @blk.sync
        def _(sy, m=m, v_prem=v_prem, v_pq=v_pq):
            sy.wait_ge(tV.h, v_prem)
            dS.inc16(sy.dma_start(
                out=pre_in.ap()[:, m * 256:(m + 1) * 256].rearrange("(j p) c -> p j c", p=128),
                in_=a_pre[:, :].rearrange("p (j c) -> p j c", j=4)))
            sy.wait_ge(tV.h, v_pq)
            dS.inc16(sy.dma_start(
                out=pqc.ap()[m * 4 * 128:(m * 4 + 4) * 128, :].rearrange("(j p) c -> p j c", p=128),
                in_=a_pq[:, :].rearrange("p (j c) -> p j c", j=4)))
        v_predma = dS.v

        @blk.sync
        def _(sy, m=m, v_sc=v_sc):
            sy.wait_ge(tV.h, v_sc)
            dS.inc16(sy.dma_start(
                out=sc_in.ap()[m * 256:(m + 1) * 256, :].rearrange("(cs p) n -> p cs n", p=128),
                in_=a_scst[:, :].rearrange("p (cs n) -> p cs n", cs=2)))
        v_scdma = dS.v

        if "pre_d" in dbg_out:
            @blk.gpsimd
            def _(g, m=m, v_prem=v_prem):
                g.wait_ge(tV.h, v_prem)
                dG.inc16(g.dma_start(
                    out=dbg_out["pre_d"][:, m * 256:(m + 1) * 256].rearrange("(j p) c -> p j c", p=128),
                    in_=a_pre[:, :].rearrange("p (j c) -> p j c", j=4)))
        if "bc_d" in dbg_out:
            @blk.gpsimd
            def _(g, m=m, v_bc=v_bc):
                g.wait_ge(tV.h, v_bc)
                dG.inc16(g.dma_start(out=dbg_out["bc_d"][:, m * 1024:(m + 1) * 1024],
                                     in_=a_bc[:, :]))
        if "den_d" in dbg_out:
            @blk.gpsimd
            def _(g, m=m, v_den=v_den, v_prem=v_prem):
                g.wait_ge(tV.h, v_den)
                dG.inc16(g.dma_start(out=dbg_out["den_d"][:, m * 1024:(m + 1) * 1024],
                                     in_=a_den2[:, :]))
        if "zz_d" in dbg_out:
            @blk.gpsimd
            def _(g, m=m, v_zup=v_zup):
                g.wait_ge(tV.h, v_zup)
                dG.inc16(g.dma_start(out=dbg_out["zz_d"][:, m * 1024:(m + 1) * 1024],
                                     in_=a_zbf[:, :]))
        if "atm_d" in dbg_out:
            @blk.gpsimd
            def _(g, m=m, v_intra=v_intra):
                g.wait_ge(tT.h, v_intra)
                dG.inc16(g.dma_start(out=dbg_out["atm_d"][:, m * 1024:(m + 1) * 1024],
                                     in_=a_atm[:, :]))

        v_prem_hist[0] = v_prem
        v_u_hist[0] = v_u
        v_axn_cons[m] = (v_wmm, v_u)
        v_ch[m] = (tT.v, tA.v, tV.v, v_predma, v_scdma)

    if "hist_d" in dbg_out:
        @blk.gpsimd
        def _(g):
            g.wait_ge(tV.h, tV.v)
            dG.inc16(g.dma_start(out=dbg_out["hist_d"], in_=a_hist[:, :]))
    if "pq_d" in dbg_out:
        @blk.gpsimd
        def _(g):
            g.wait_ge(dS.h, dS.v)
            dG.inc16(g.dma_start(out=dbg_out["pq_d"], in_=pqc.ap()[:, :]))
    if "sc_d" in dbg_out:
        @blk.gpsimd
        def _(g):
            g.wait_ge(dS.h, dS.v)
            dG.inc16(g.dma_start(out=dbg_out["sc_d"], in_=sc_in.ap()[:, :]))

    bB = barrier()
    ctxB.close()

    @blk.gpsimd
    def _(g):
        g.wait_ge(dS.h, bB[3])
        cc.inc(g.collective_compute(
            "AllGather", OP.bypass,
            replica_groups=[[0, 1], [2, 3], [4, 5], [6, 7]],
            ins=[pre_in.ap().opt()], outs=[pre_out.ap().opt()]))
        cc.inc(g.collective_compute(
            "AllReduce", OP.add,
            replica_groups=[[0, 1], [2, 3], [4, 5], [6, 7]],
            ins=[sc_in.ap().opt()], outs=[sc_out.ap().opt()]))
    v_cc2 = cc.v

    ctxC = contextlib.ExitStack()

    def sbC(name, shape, dtype):
        return ctxC.enter_context(nc.sbuf_tensor(name, shape, dtype))

    f_k = sbC("f_k", [128, T], BF16)
    f_v = sbC("f_v", [128, 64 * 130], BF16)
    f_q = sbC("f_q", [128, NTC * 512], BF16)
    f_nq = sbC("f_nq", [1, 16], F32)
    f_b2 = sbC("f_b2", [1, 16], BF16)
    f_bcall = sbC("f_bcall", [128, 16], F32)
    f_q2 = sbC("f_q2", [128, 2 * 512], BF16)
    f_nk = sbC("f_nk", [1, 16], F32)
    f_rb = sbC("f_rb", [65, 2048], BF16)
    f_dbc = sbC("f_dbc", [64, 512], BF16)
    f_ctx = sbC("f_ctx", [128, 512], BF16)
    f_cx = sbC("f_cx", [128, 2048], F32)
    f_pt = sbC("f_pt", [128, 2 * 1024], BF16)

    qk_v, exp_v, pv_v = {}, {}, {}
    ki = 0
    v_ctxev = None
    v_ctxdma = [0]
    v_B_ctd = [0]
    prev_B = None
    pair_done = None

    for pr in range(4):
        @blk.sync
        def _(sy, pr=pr, pd=pair_done):
            if pr == 0:
                wait_barrier(sy, bA)
                wait_barrier(sy, bB)
            else:
                sy.wait_ge(tT.h, pd[2])
            dS.inc16(sy.dma_start(
                out=f_k[:, :].rearrange("p (tc c) -> p tc c", tc=NTC),
                in_=kf.ap()[pr * NTC * 128:(pr + 1) * NTC * 128, :]
                    .rearrange("(tc p) c -> p tc c", p=128)))
        v_kd = dS.v

        @blk.sync
        def _(sy, pr=pr):
            dS.inc16(sy.dma_start(
                out=f_q[:, :].rearrange("p (tc c) -> p tc c", tc=NTC),
                in_=qf.ap()[pr * NTC * 128:(pr + 1) * NTC * 128, :]
                    .rearrange("(tc p) c -> p tc c", p=128)))
        v_qd = dS.v

        @blk.sync
        def _(sy, pr=pr):
            dS.inc16(sy.dma_start(
                out=f_v[:, :].rearrange("p (kt s) -> p kt s", s=130)[:, :, 0:64],
                in_=vtok.ap()[pr * 64 * 128:(pr + 1) * 64 * 128, :]
                    .rearrange("(kt p) f -> p kt f", p=128)[:, :, 0:64]))
            dS.inc16(sy.dma_start(
                out=f_v[:, :].rearrange("p (kt s) -> p kt s", s=130)[:, :, 65:129],
                in_=vtok.ap()[pr * 64 * 128:(pr + 1) * 64 * 128, :]
                    .rearrange("(kt p) f -> p kt f", p=128)[:, :, 64:128]))
        v_kv = dS.v

        @blk.vector
        def _(v, pr=pr, v_kv=v_kv):
            if pr == 0:
                wait_barrier(v, bB)
            v.wait_ge(dS.h, v_kv)
            v.memset(f_v[:, :].rearrange("p (kt s) -> p kt s", s=130)[:, :, 64:65], 1.0)
            tV.inc(v.memset(f_v[:, :].rearrange("p (kt s) -> p kt s", s=130)[:, :, 129:130], 1.0))
        v_vones = tV.v

        v_nkmm_h = {}
        v_nkred_h = {}
        for tc in range(NTC):
            qb = (tc % 2) * 512
            pp = p_tf if tc % 2 == 0 else p_c

            @blk.vector
            def _(v, pr=pr, tc=tc, v_kv=v_kd, pd=pair_done, qb=qb):
                v.wait_ge(dS.h, v_kv)
                if tc >= 2:
                    v.wait_ge(tT.h, v_nkmm_h[tc - 2])
                elif pd is not None:
                    v.wait_ge(tT.h, pd[0])
                tV.inc(v.tensor_mul(f_q2[:, qb:qb + 512],
                                    f_k[:, tc * 512:(tc + 1) * 512],
                                    f_k[:, tc * 512:(tc + 1) * 512]))
            v_k2 = tV.v

            @blk.tensor
            def _(t, pr=pr, tc=tc, v_k2=v_k2, pd=pair_done, qb=qb, pp=pp):
                if pr == 0 and tc == 0:
                    wait_barrier(t, bB)
                t.wait_ge(tV.h, v_k2)
                if tc >= 2:
                    t.wait_ge(tV.h, v_nkred_h[tc - 2])
                elif pr > 0:
                    t.wait_ge(tV.h, pd[1])
                tT.inc(t.matmul(pp[0:1, 0:512], ones_col[:, :], f_q2[:, qb:qb + 512],
                                start=True, stop=True))
            v_nkmm_h[tc] = tT.v

            @blk.vector
            def _(v, tc=tc, vm=v_nkmm_h[tc], pp=pp):
                v.wait_ge(tT.h, vm)
                tV.inc(v.reduce_max(f_nk[0:1, tc:tc + 1], pp[0:1, 0:512], axis=AX))
            v_nkred_h[tc] = tV.v
        v_nkmm = [v_nkmm_h[NTC - 1]]
        v_nkred = [v_nkred_h[NTC - 1]]

        @blk.vector
        def _(v):
            v.wait_ge(tV.h, tV.v)
            tV.inc(v.reduce_max(a_stat[0:1, 4:5], f_nk[0:1, 0:16], axis=AX))
        v_bk2 = tV.v

        v_nqs = {}
        v_nqr_h = {}
        for gi2 in range(NTC):
            qb = (gi2 % 2) * 512
            pp = p_tf if gi2 % 2 == 0 else p_c

            @blk.vector
            def _(v, gi2=gi2, v_kv=v_qd, qb=qb, vkm=v_nkmm[0]):
                v.wait_ge(dS.h, v_kv)
                if gi2 >= 2:
                    v.wait_ge(tT.h, v_nqs[gi2 - 2])
                else:
                    v.wait_ge(tT.h, vkm)
                tV.inc(v.tensor_mul(f_q2[:, qb:qb + 512],
                                    f_q[:, gi2 * 512:(gi2 + 1) * 512],
                                    f_q[:, gi2 * 512:(gi2 + 1) * 512]))
            v_q2m = tV.v

            @blk.tensor
            def _(t, gi2=gi2, v_q2m=v_q2m, qb=qb, pp=pp, vkr=v_nkred[0]):
                t.wait_ge(tV.h, v_q2m)
                if gi2 >= 2:
                    t.wait_ge(tV.h, v_nqr_h[gi2 - 2])
                else:
                    t.wait_ge(tV.h, vkr)
                tT.inc(t.matmul(pp[0:1, 0:512], ones_col[:, :], f_q2[:, qb:qb + 512],
                                start=True, stop=True))
            v_nqs[gi2] = tT.v

            @blk.vector
            def _(v, gi2=gi2, vm=v_nqs[gi2], pp=pp):
                v.wait_ge(tT.h, vm)
                tV.inc(v.reduce_max(f_nq[0:1, gi2:gi2 + 1], pp[0:1, 0:512], axis=AX))
            v_nqr_h[gi2] = tV.v
        v_nqr = [v_nqr_h[NTC - 1]]

        @blk.vector
        def _(v, v_bk2=v_bk2):
            v.wait_ge(tV.h, tV.v)
            v.wait_ge(tV.h, v_bk2)
            tV.inc(v.tensor_scalar(a_stat[0:1, 5:6], a_stat[0:1, 4:5], -1.0 / 16.0,
                                   None, OP.mult))
            v.wait_ge(tV.h, tV.v)
            tV.inc(v.tensor_scalar(f_b2[0:1, :], f_nq[0:1, :], -4.0,
                                   a_stat[0:1, 5:6], OP.mult, OP.add))
        v_brow = tV.v

        @blk.tensor
        def _(t, v_brow=v_brow):
            t.wait_ge(tV.h, v_brow)
            tT.inc(t.matmul(p_tf[:, 256:272], ones_row[:, :], f_b2[0:1, :],
                            start=True, stop=True))
        v_bcm = tT.v

        @blk.vector
        def _(v, v_bcm=v_bcm, va=tA.v):
            v.wait_ge(tT.h, v_bcm)
            if va:
                v.wait_ge(tA.h, va)
            tV.inc(v.tensor_copy(f_bcall[:, :], p_tf[:, 256:272]))
        v_bcall = tV.v

        v_bcf = [None]
        exp_base_pair = tA.v
        ki_pair0 = ki
        for kt2 in range(NTC * 64):
            exp_v[ki_pair0 + kt2] = exp_base_pair + kt2 + 1
        for gi in range(NTC):
            ga = gi

            ki0 = ki

            @blk.tensor
            def _(t, ga=ga, ki0=ki0, vb=v_bcall, vce=v_ctxev, vvo=v_vones):
                for kt in range(64):
                    ki_ = ki0 + kt
                    pi = ki_ % 2
                    P = p_a if pi == 0 else p_b
                    if kt == 0:
                        t.wait_ge(tV.h, vb)
                    if ki_ >= 2:
                        t.wait_ge(tA.h, exp_v[ki_ - 2])
                    t.matmul(P[:, 0:512], f_k[0:64, kt * 128:(kt + 1) * 128],
                             f_q[0:64, ga * 512:(ga + 1) * 512], start=True, stop=True,
                             tile_position=(0, 0))
                    tT.inc(t.matmul(P[:, 512:1024], f_k[64:128, kt * 128:(kt + 1) * 128],
                                    f_q[64:128, ga * 512:(ga + 1) * 512], start=True,
                                    stop=True, tile_position=(64, 0)))
                    qk_v[ki_] = tT.v
                    if kt >= 1:
                        kp = ki0 + kt - 1
                        pp = kp % 2
                        t.wait_ge(tA.h, exp_v[kp])
                        if kt == 1:
                            t.wait_ge(tV.h, vvo)
                            if vce is not None:
                                t.wait_ge(tV.h, vce)
                        t.matmul(p_c[0:65, :], f_v[:, (kt - 1) * 130: (kt - 1) * 130 + 65],
                                 f_pt[:, pp * 1024: pp * 1024 + 512],
                                 start=(kt == 1), stop=False)
                        tT.inc(t.matmul(p_d[0:65, :],
                                        f_v[:, (kt - 1) * 130 + 65: (kt - 1) * 130 + 130],
                                        f_pt[:, pp * 1024 + 512:(pp + 1) * 1024],
                                        start=(kt == 1), stop=False))
                        pv_v[kp] = tT.v
                kp = ki0 + 63
                pp = kp % 2
                t.wait_ge(tA.h, exp_v[kp])
                t.matmul(p_c[0:65, :], f_v[:, 63 * 130: 63 * 130 + 65],
                         f_pt[:, pp * 1024: pp * 1024 + 512], start=False, stop=True)
                tT.inc(t.matmul(p_d[0:65, :], f_v[:, 63 * 130 + 65: 63 * 130 + 130],
                                f_pt[:, pp * 1024 + 512:(pp + 1) * 1024],
                                start=False, stop=True))
                pv_v[kp] = tT.v

            ki += 64

            pg = gi % 2
            @blk.vector
            def _(v, kk=ki, pg=pg):
                v.wait_ge(tT.h, pv_v[kk - 1])
                v.tensor_copy(f_cx[0:65, pg * 1024: pg * 1024 + 512], p_c[0:65, :])
                tV.inc(v.tensor_copy(f_cx[0:65, pg * 1024 + 512:(pg + 1) * 1024],
                                     p_d[0:65, :]))
            v_ctxev = tV.v

            @blk.vector
            def _(v, pg=pg, ve=v_ctxev):
                v.wait_ge(tV.h, ve)
                tV.inc(v.reciprocal(f_rb[0:1, pg * 1024: pg * 1024 + 512],
                                    f_cx[64:65, pg * 1024: pg * 1024 + 512]))
                tV.inc(v.reciprocal(f_rb[64:65, pg * 1024 + 512:(pg + 1) * 1024],
                                    f_cx[64:65, pg * 1024 + 512:(pg + 1) * 1024]))
            v_rp1, v_rp2 = tV.v - 1, tV.v

            if prev_B is not None:
                prev_B()

            def mk_B(pr=pr, gi=gi, pg=pg, v_rp1=v_rp1, v_rp2=v_rp2):
                def emit():
                    vprev = v_B_ctd[0]

                    @blk.tensor
                    def _(t):
                        t.wait_ge(tV.h, v_rp1)
                        if vprev:
                            t.wait_ge(tV.h, vprev)
                        tT.inc(t.matmul(p_tf[0:64, 0:512], ones_row[0:1, 0:64],
                                        f_rb[0:1, pg * 1024: pg * 1024 + 512],
                                        start=True, stop=True))
                    v_db1 = tT.v

                    @blk.tensor
                    def _(t):
                        t.wait_ge(tV.h, v_rp2)
                        tT.inc(t.matmul(p_tf[64:128, 0:512], ones_r2[64:65, 0:64],
                                        f_rb[64:65, pg * 1024 + 512:(pg + 1) * 1024],
                                        start=True, stop=True))
                    v_db2 = tT.v

                    @blk.vector
                    def _(v, vcd=v_ctxdma[0]):
                        v.wait_ge(tT.h, v_db1)
                        if vcd:
                            v.wait_ge(dS.h, vcd)
                        tV.inc(v.tensor_copy(f_dbc[:, :], p_tf[0:64, 0:512]))
                        v.wait_ge(tV.h, tV.v)
                        tV.inc(v.tensor_mul(f_ctx[0:64, :],
                                            f_cx[0:64, pg * 1024: pg * 1024 + 512],
                                            f_dbc[:, :]))
                        v.wait_ge(tT.h, v_db2)
                        v.wait_ge(tV.h, tV.v)
                        tV.inc(v.tensor_copy(f_dbc[:, :], p_tf[64:128, 0:512]))
                        v.wait_ge(tV.h, tV.v)
                        tV.inc(v.tensor_mul(f_ctx[64:128, :],
                                            f_cx[0:64, pg * 1024 + 512:(pg + 1) * 1024],
                                            f_dbc[:, :]))
                    v_ctd = tV.v
                    v_B_ctd[0] = v_ctd

                    @blk.sync
                    def _(sy):
                        sy.wait_ge(tV.h, v_ctd)
                        dS.inc16(sy.dma_start(
                            out=ctx_in.ap()[pr * 128:(pr + 1) * 128,
                                            gi * 512:(gi + 1) * 512],
                            in_=f_ctx[:, :]))
                    v_ctxdma[0] = dS.v
                return emit
            prev_B = mk_B()
        prev_B()
        prev_B = None
        @blk.scalar
        def _(s, pr=pr, ki_pair0=ki_pair0):
            if pr == 0:
                wait_barrier(s, bB)
            for kt2 in range(NTC * 64):
                ki_ = ki_pair0 + kt2
                gi2 = kt2 // 64
                pi = ki_ % 2
                P = p_a if pi == 0 else p_b
                s.wait_ge(tT.h, qk_v[ki_])
                if ki_ >= 2:
                    assert pv_v[ki_ - 2] <= qk_v[ki_]
                inst = s.activation(f_pt[:, pi * 1024:(pi + 1) * 1024], P[:, :],
                                    AF.Exp, bias=f_bcall[:, gi2:gi2 + 1])
                tA.inc(inst)
                assert tA.v == exp_v[ki_]
        pair_done = (tT.v, tV.v, pv_v[ki - 1])

    if "ctx_d" in dbg_out:
        @blk.gpsimd
        def _(g):
            g.wait_ge(dS.h, dS.v)
            dG.inc16(g.dma_start(out=dbg_out["ctx_d"], in_=ctx_in.ap()[:, :]))

    bC = barrier()
    ctxC.close()

    @blk.gpsimd
    def _(g):
        g.wait_ge(dS.h, bC[3])
        cc.inc(g.collective_compute(
            "AllGather", OP.bypass,
            replica_groups=[[0, 1], [2, 3], [4, 5], [6, 7]],
            ins=[ctx_in.ap().opt()], outs=[ctx_out.ap().opt()]))
    v_cc3 = cc.v

    ctxD = contextlib.ExitStack()

    def sbD(name, shape, dtype):
        return ctxD.enter_context(nc.sbuf_tensor(name, shape, dtype))

    g_scs = sbD("g_scs", [128, 64 * 33], F32)
    g_exp = sbD("g_exp", [128, 2 * 33], F32)
    g_mx = sbD("g_mx", [128, 64], F32)
    g_gst = sbD("g_gst", [128, 64 * 33], BF16)
    g_rt = sbD("g_rt", [33, 256], BF16)
    c_pre = sbD("c_pre", [128, 2 * 8 * 256], BF16)
    c_xnf = sbD("c_xnf", [128, 2 * 4 * 256], BF16)
    c_pq = sbD("c_pq", [128, 2 * 4 * 256], BF16)
    c_gbc = sbD("c_gbc", [128, 256], BF16)
    c_spq = sbD("c_spq", [128, 2 * 4 * 256], BF16)
    c_xo = sbD("c_xo", [128, 4 * 256], F32)
    c_s = sbD("c_s", [128, 4 * 256], F32)
    c_ctx = sbD("c_ctx", [128, 2 * 8 * 256], BF16)
    c_xvo = sbD("c_xvo", [128, 256], BF16)
    c_g = sbD("c_g", [128, 4 * 256], F32)
    c_xr = sbD("c_xr", [128, 2 * 2 * 512], F32)
    c_out = sbD("c_out", [128, 2 * 512], F32)

    @blk.sync
    def _(sy):
        wait_barrier(sy, bC)
        sy.wait_ge(cc.h, v_cc2)
        dS.inc16(sy.dma_start(out=g_scs[:, :].rearrange("p (a n) -> p a n", a=NT),
                              in_=sc_out.ap().rearrange("(a p) n -> p a n", p=128)))
    v_scg = dS.v

    v_gt_done = {}
    @blk.vector
    def _(v):
        wait_barrier(v, bC)
        v.wait_ge(dS.h, v_scg)
        for a in range(NT):
            inst = v.reduce_max(g_mx[:, a:a + 1], g_scs[:, a * 33:(a + 1) * 33],
                                axis=AX, negate=True)
        tV.inc(inst)
    v_mxall = tV.v
    dve_base = tV.v
    exp_base = tA.v
    gexp_v = {a: exp_base + a + 1 for a in range(NT)}
    gdve_v = {a: dve_base + 3 * (a + 1) for a in range(NT)}

    @blk.scalar
    def _(s):
        wait_barrier(s, bC)
        s.wait_ge(tV.h, v_mxall)
        for a in range(NT):
            if a >= 2:
                s.wait_ge(tV.h, gdve_v[a - 2])
            inst = s.activation(g_exp[:, (a % 2) * 33:(a % 2) * 33 + 33],
                                g_scs[:, a * 33:(a + 1) * 33], AF.Exp,
                                bias=g_mx[:, a:a + 1])
            tA.inc(inst)
            assert tA.v == gexp_v[a]

    @blk.vector
    def _(v):
        for a in range(NT):
            v.wait_ge(tA.h, gexp_v[a])
            sl = (a % 2) * 33
            tV.inc(v.reduce_sum(st_sm[:, 1:2], g_exp[:, sl:sl + 33], axis=AX))
            v.wait_ge(tV.h, tV.v)
            tV.inc(v.reciprocal(st_sm[:, 2:3], st_sm[:, 1:2]))
            v.wait_ge(tV.h, tV.v)
            tV.inc(v.tensor_scalar(g_gst[:, a * 33:(a + 1) * 33], g_exp[:, sl:sl + 33],
                                   st_sm[:, 2:3], None, OP.mult))
            assert tV.v == gdve_v[a]
            v_gt_done[a] = tV.v

    if "gates_d" in dbg_out:
        @blk.gpsimd
        def _(g):
            g.wait_ge(tV.h, tV.v)
            dG.inc16(g.dma_start(out=dbg_out["gates_d"], in_=g_gst[:, :]))

    v_cm = {}
    v_outdma = [0, 0]
    fin = {}
    v_ldm, v_ldcm = {}, {}

    def emit_ph6_loads(mm, first):
        pm = mm % 2

        @blk.sync
        def _(sy, mm=mm, pm=pm, first=first):
            if first:
                wait_barrier(sy, bC)
                sy.wait_ge(cc.h, v_cc2)
            elif mm >= 2:
                sy.wait_ge(tT.h, fin["wo"])
                sy.wait_ge(tV.h, fin["xo"])
                sy.wait_ge(tV.h, fin["spq"])
                sy.wait_ge(tV.h, fin["out"])
            for dt in range(8):
                dS.inc16(sy.dma_start(
                    out=c_pre[:, pm * 2048 + dt * 256: pm * 2048 + (dt + 1) * 256],
                    in_=pre_out.ap()[dt * 128:(dt + 1) * 128, mm * 256:(mm + 1) * 256]))
            for dt in range(4):
                dS.inc16(sy.dma_start(
                    out=c_xnf[:, pm * 1024 + dt * 256: pm * 1024 + (dt + 1) * 256],
                    in_=xn_tm.ap()[mm * 256:(mm + 1) * 256, dt * 128:(dt + 1) * 128],
                    transpose=True))
            dS.inc16(sy.dma_start(
                out=c_pq[:, pm * 1024:(pm + 1) * 1024].rearrange("p (j c) -> p j c", j=4),
                in_=pqc.ap()[mm * 4 * 128:(mm * 4 + 4) * 128, :]
                    .rearrange("(j p) c -> p j c", p=128)))
            for cs in range(2):
                dS.inc16(sy.dma_start(
                    out=c_xr[:, pm * 1024 + cs * 512: pm * 1024 + (cs + 1) * 512],
                    in_=x[mm * 256 + cs * 128: mm * 256 + (cs + 1) * 128, 0:512]))
        v_ldm[mm] = dS.v

        @blk.sync
        def _(sy, mm=mm, pm=pm, first=first):
            sy.wait_ge(cc.h, v_cc3)
            if not first and mm >= 2:
                sy.wait_ge(tT.h, fin["omm"])
            for dt in range(8):
                dS.inc16(sy.dma_start(
                    out=c_ctx[:, pm * 2048 + dt * 256: pm * 2048 + (dt + 1) * 256],
                    in_=ctx_out.ap()[dt * 128:(dt + 1) * 128, mm * 256:(mm + 1) * 256]))
        v_ldcm[mm] = dS.v

    emit_ph6_loads(0, True)
    for m in range(NCH):
        pm = m % 2
        if m + 1 < NCH:
            emit_ph6_loads(m + 1, False)
        v_ld = v_ldm[m]
        v_ldc = v_ldcm[m]

        @blk.tensor
        def _(t, m=m):
            if m == 0:
                wait_barrier(t, bC)
            t.wait_ge(tV.h, v_gt_done[2 * m + 1])
            if m >= 1:
                t.wait_ge(tV.h, fin["grt_tv"])
            for tt in range(2):
                inst = t.transpose(p_t[0:33, tt * 128:(tt + 1) * 128],
                                   g_gst[:, (2 * m + tt) * 33:(2 * m + tt + 1) * 33],
                                   c_identb[:, :])
            tT.inc(inst)
        v_gtm = tT.v

        @blk.vector
        def _(v, m=m, v_gtm=v_gtm):
            if m == 0:
                wait_barrier(v, bC)
            v.wait_ge(tT.h, v_gtm)
            if m >= 1:
                v.wait_ge(tT.h, fin["grt"])
            tV.inc(v.tensor_copy(g_rt[:, :], p_t[0:33, 0:256]))
        v_grt = tV.v
        fin["grt_tv"] = v_grt

        if m > 0:
            @blk.vector
            def _(v, m=m):
                if "g" in fin:
                    v.wait_ge(tA.h, fin["g"])
                tV.inc(v.memset(p_b[:, :], 0.0))
        v_pbz = tV.v

        @blk.tensor
        def _(t, m=m, v_ld=v_ld, pm=pm):
            t.wait_ge(dS.h, v_ld)
            if m >= 1:
                t.wait_ge(tV.h, fin["xo"])
            for fs in range(4):
                for dt in range(8):
                    inst = t.matmul(p_a[:, fs * 256:(fs + 1) * 256],
                                    w["wo"][:, dt * 512 + fs * 128: dt * 512 + fs * 128 + 128],
                                    c_pre[:, pm * 2048 + dt * 256: pm * 2048 + (dt + 1) * 256],
                                    start=(dt == 0), stop=(dt == 7))
            tT.inc(inst)
        v_wo = tT.v
        fin["wo"] = v_wo

        v_hr = None
        v_gbcev = {}
        v_selv = {}
        v_hrv = {}
        if m == 0:
            fin["spq"] = tV.v
        for n in range(m):
            sel = p_c if n % 2 == 0 else p_d
            sb = (n % 2) * 1024

            @blk.tensor
            def _(t, m=m, n=n, sel=sel, v_grt=v_grt, vz=v_pbz):
                t.wait_ge(tV.h, v_grt)
                if n >= 2:
                    t.wait_ge(tV.h, v_gbcev[n - 2])
                tT.inc(t.matmul(sel[:, 0:256], c_e33[:, n * 128:(n + 1) * 128],
                                g_rt[:, :], start=True, stop=True))
                v_selv[n] = tT.v
                if n >= 1:
                    nn = n - 1
                    t.wait_ge(tV.h, v_gbcev[nn])
                    if nn == 0:
                        t.wait_ge(tV.h, vz)
                    for h in range(8):
                        po, j = hsl(h)
                        inst = t.matmul(p_b[po:po + 64, j * 256:(j + 1) * 256],
                                        a_hist[po:po + 64, (j * NCH + nn) * 64:(j * NCH + nn) * 64 + 64],
                                        c_spq[po:po + 64, (nn % 2) * 1024 + j * 256:
                                              (nn % 2) * 1024 + (j + 1) * 256],
                                        start=False, stop=False, skip_group_check=True)
                    tT.inc(inst)
                    v_hrv[nn] = tT.v

            @blk.vector
            def _(v, m=m, n=n, sel=sel, vs=v_selv[n], pm=pm, sb=sb):
                v.wait_ge(tT.h, vs)
                if n >= 2:
                    v.wait_ge(tT.h, v_hrv[n - 2])
                for j in range(4):
                    inst = v.tensor_mul(c_spq[:, sb + j * 256: sb + (j + 1) * 256],
                                        c_pq[:, pm * 1024 + j * 256: pm * 1024 + (j + 1) * 256],
                                        sel[:, 0:256])
                tV.inc(inst)
            v_gbcev[n] = tV.v
            fin["spq"] = tV.v

        if m > 0:
            @blk.tensor
            def _(t, m=m, vg=v_gbcev[m - 1], vz=v_pbz):
                t.wait_ge(tV.h, vg)
                if m == 1:
                    t.wait_ge(tV.h, vz)
                for h in range(8):
                    po, j = hsl(h)
                    inst = t.matmul(p_b[po:po + 64, j * 256:(j + 1) * 256],
                                    a_hist[po:po + 64, ((j * NCH) + m - 1) * 64:((j * NCH) + m - 1) * 64 + 64],
                                    c_spq[po:po + 64, ((m - 1) % 2) * 1024 + j * 256:
                                          ((m - 1) % 2) * 1024 + (j + 1) * 256],
                                    start=False, stop=True, skip_group_check=True)
                tT.inc(inst)
            v_hr = tT.v

        @blk.vector
        def _(v, m=m, v_wo=v_wo, v_ld=v_ld, pm=pm):
            v.wait_ge(tT.h, v_wo)
            v.wait_ge(dS.h, v_ld)
            tV.inc(v.tensor_add(c_xo[:, :], p_a[:, :],
                                c_xnf[:, pm * 1024:(pm + 1) * 1024]))
        v_xo = tV.v
        fin["xo"] = v_xo

        @blk.tensor
        def _(t, m=m, v_grt=v_grt, v_xo=v_xo):
            t.wait_ge(tV.h, v_grt)
            if m >= 2:
                t.wait_ge(tV.h, v_gbcev.get(m - 2, v_xo))
            sel = p_c if m % 2 == 0 else p_d
            tT.inc(t.matmul(sel[:, 0:256], c_e33[:, 32 * 128:33 * 128], g_rt[:, :],
                            start=True, stop=True))
        v_gcm = tT.v
        fin["grt"] = v_gcm

        @blk.vector
        def _(v, m=m, v_gcm=v_gcm, v_xo=v_xo, v_hr=v_hr):
            v.wait_ge(tT.h, v_gcm)
            if m >= 1:
                v.wait_ge(tT.h, fin["sgtr"])
            sel = p_c if m % 2 == 0 else p_d
            tV.inc(v.tensor_copy(c_gbc[:, :], sel[:, 0:256]))
            v.wait_ge(tV.h, tV.v)
            v.wait_ge(tV.h, v_xo)
            for j in range(4):
                inst = v.tensor_mul(c_s[:, j * 256:(j + 1) * 256],
                                    c_xo[:, j * 256:(j + 1) * 256], c_gbc[:, :])
            tV.inc(inst)
            if m > 0:
                v.wait_ge(tT.h, v_hr)
                v.wait_ge(tV.h, tV.v)
                tV.inc(v.tensor_add(c_s[:, :], c_s[:, :], p_b[:, :]))
        v_s = tV.v

        @blk.tensor
        def _(t, m=m, v_ldc=v_ldc, v_s=v_s, pm=pm):
            t.wait_ge(dS.h, v_ldc)
            t.wait_ge(tV.h, v_s)
            sel = p_d if m % 2 == 0 else p_c
            for dt in range(8):
                inst = t.matmul(sel[:, 0:256], w_vc["o"][:, dt * 128:(dt + 1) * 128],
                                c_ctx[:, pm * 2048 + dt * 256: pm * 2048 + (dt + 1) * 256],
                                start=(dt == 0), stop=(dt == 7))
            tT.inc(inst)
        v_xvo = tT.v

        @blk.vector
        def _(v, m=m, v_xvo=v_xvo):
            v.wait_ge(tT.h, v_xvo)
            sel = p_d if m % 2 == 0 else p_c
            tV.inc(v.tensor_scalar(c_xvo[:, :], sel[:, 0:256], scl128["o"][:, :],
                                   None, OP.mult))
        v_xvoe = tV.v

        @blk.tensor
        def _(t, m=m, v_xvoe=v_xvoe, v_s=v_s, pm=pm):
            t.wait_ge(tV.h, v_xvoe)
            t.wait_ge(tV.h, v_s)
            for fs in range(4):
                for dt in range(8):
                    t.matmul(p_b[:, fs * 256:(fs + 1) * 256],
                             w["fo"][:, dt * 512 + fs * 128: dt * 512 + fs * 128 + 128],
                             c_ctx[:, pm * 2048 + dt * 256: pm * 2048 + (dt + 1) * 256],
                             start=(dt == 0), stop=False)
                inst = t.matmul(p_b[:, fs * 256:(fs + 1) * 256],
                                w_uc["o"][:, fs * 128:(fs + 1) * 128], c_xvo[:, :],
                                start=False, stop=True)
            tT.inc(inst)
        v_omm = tT.v
        fin["omm"] = v_omm

        @blk.scalar
        def _(s, m=m, v_omm=v_omm):
            if m == 0:
                wait_barrier(s, bC)
            s.wait_ge(tT.h, v_omm)
            for fs in range(4):
                inst = s.activation(c_g[:, fs * 256:(fs + 1) * 256],
                                    p_b[:, fs * 256:(fs + 1) * 256], AF.Silu,
                                    bias=b_o[:, fs:fs + 1])
            tA.inc(inst)
        v_g = tA.v
        fin["g"] = v_g

        @blk.vector
        def _(v, m=m, v_g=v_g, v_s=v_s):
            v.wait_ge(tA.h, v_g)
            tV.inc(v.tensor_mul(c_s[:, :], c_s[:, :], c_g[:, :]))
        v_sg = tV.v

        if "s_d" in dbg_out or "g_d" in dbg_out:
            @blk.gpsimd
            def _(g, m=m, v_sg=v_sg):
                g.wait_ge(tV.h, v_sg)
                if "s_d" in dbg_out:
                    dG.inc16(g.dma_start(out=dbg_out["s_d"][:, m * 1024:(m + 1) * 1024],
                                         in_=c_s[:, :]))
                if "g_d" in dbg_out:
                    dG.inc16(g.dma_start(out=dbg_out["g_d"][:, m * 1024:(m + 1) * 1024],
                                         in_=c_g[:, :]))

        v_oev = []
        for cs in range(2):
            @blk.tensor
            def _(t, m=m, cs=cs, v_sg=v_sg):
                t.wait_ge(tV.h, v_sg)
                if cs == 1:
                    t.wait_ge(tV.h, v_oev[0])
                for j in range(4):
                    inst = t.transpose(p_tf[:, j * 128:(j + 1) * 128],
                                       c_s[:, j * 256 + cs * 128: j * 256 + cs * 128 + 128],
                                       c_identf[:, :])
                tT.inc(inst)
            v_trm = tT.v
            fin["sgtr"] = v_trm

            @blk.vector
            def _(v, m=m, cs=cs, v_trm=v_trm, v_ld=v_ld, pm=pm):
                v.wait_ge(tT.h, v_trm)
                v.wait_ge(dS.h, v_ld)
                if v_outdma[cs]:
                    v.wait_ge(dS.h, v_outdma[cs])
                tV.inc(v.tensor_add(c_out[:, cs * 512:(cs + 1) * 512], p_tf[:, 0:512],
                                    c_xr[:, pm * 1024 + cs * 512: pm * 1024 + (cs + 1) * 512]))
            v_oev.append(tV.v)

            @blk.sync
            def _(sy, m=m, cs=cs, vo=v_oev[cs]):
                sy.wait_ge(tV.h, vo)
                dS.inc16(sy.dma_start(
                    out=out[m * 256 + cs * 128: m * 256 + (cs + 1) * 128, :],
                    in_=c_out[:, cs * 512:(cs + 1) * 512]))
            v_outdma[cs] = dS.v

        fin["out"] = v_oev[1]
        v_cm[m] = (tT.v, tA.v, tV.v)

    @blk.sync
    def _(sy):
        sy.wait_ge(dS.h, dS.v)
        sy.wait_ge(dG.h, dG.v)
        sy.wait_ge(tT.h, tT.v)
        sy.wait_ge(tA.h, tA.v)
        sy.wait_ge(tV.h, tV.v)
        sy.wait_ge(cc.h, cc.v)

    ctxD.close()
    ctxG.close()
    return nc, dbg_out


def _host_inputs(inputs, c):
    b, hg = c // 2, c % 2
    F = np.arange(512 * hg, 512 * hg + 512)
    notF = np.arange(512 * (1 - hg), 512 * (1 - hg) + 512)
    perm = np.concatenate([F, notF])
    g = lambda n: np.asarray(inputs[n], np.float32)
    d = {}
    d["x"] = np.ascontiguousarray(g("L0_attn_x")[b][:, perm])
    d["wqT"] = np.ascontiguousarray(g("L0_attn_Wq").T[perm][:, F])
    d["wkT"] = np.ascontiguousarray(g("L0_attn_Wk").T[perm][:, F])
    d["wvT"] = np.ascontiguousarray(g("L0_attn_Wv").T[perm][:, F])
    d["wuT"] = np.ascontiguousarray(g("L0_attn_Wu").T[perm][:, F])
    d["woT"] = np.ascontiguousarray(g("L0_attn_Wo").T[:, F])
    for p in "qkv":
        d[f"f{p}T"] = np.ascontiguousarray(g(f"L0_ffn_{p}Wm").T[perm][:, F])
        d[f"vcT_{p}"] = np.ascontiguousarray(g(f"L0_ffn_{p}V").reshape(ER, D).T[perm])
    d["foT"] = np.ascontiguousarray(g("L0_ffn_oWm").T[:, F])
    d["vcT_o"] = np.ascontiguousarray(g("L0_ffn_oV").reshape(ER, D).T)
    for p in "qkvo":
        d[f"ucat_{p}"] = np.ascontiguousarray(
            g(f"L0_ffn_{p}U").transpose(0, 2, 1).reshape(ER, D)[:, F])
        d[f"srow_{p}"] = np.ascontiguousarray(g(f"L0_ffn_{p}S").reshape(1, ER))
    for p in "qko":
        d[f"bcol_{p}"] = np.ascontiguousarray(g(f"L0_ffn_{p}b")[F].reshape(4, 128).T)
    d["vb_row"] = np.ascontiguousarray(g("L0_ffn_vb")[F].reshape(1, 512))
    d["nw_pa"] = np.ascontiguousarray(g("L0_attn_norm_w")[perm].reshape(8, 128).T)
    d["gw1T"] = np.ascontiguousarray(g("L0_ffn_gw1").T[perm])
    d["gb1"] = np.ascontiguousarray(g("L0_ffn_gb1").reshape(1, GH))
    d["gw2T"] = np.ascontiguousarray(g("L0_ffn_gw2").T)
    d["gb2"] = np.ascontiguousarray(g("L0_ffn_gb2").reshape(1, E))
    d["identb"] = np.eye(128, dtype=ml_dtypes.bfloat16)
    d["identf"] = np.eye(128, dtype=np.float32)
    tri = (np.arange(SEG)[:, None] >= np.arange(SEG)[None, :])
    d["triw"] = tri.astype(np.float32)
    d["trik"] = np.ascontiguousarray(tri.T.astype(ml_dtypes.bfloat16))
    d["cntinv"] = np.ascontiguousarray(
        (1.0 / np.arange(1, SEG + 1, dtype=np.float32)).reshape(2, 128).T)
    e33a = np.zeros((33, 33 * 128), ml_dtypes.bfloat16)
    for n in range(33):
        e33a[n, n * 128:(n + 1) * 128] = 1
    d["e33"] = e33a
    e16a = np.zeros((E, 128), ml_dtypes.bfloat16)
    for e in range(E):
        e16a[e, e * 16:(e + 1) * 16] = 1
    d["e16"] = e16a
    return d


_CACHE = {}


def kernel(**inputs):
    key = tuple(DBG)
    if key not in _CACHE:
        _CACHE[key] = build_nc(dbg=DBG)
    nc, dbg_out = _CACHE[key]
    in_maps = [_host_inputs(inputs, c) for c in range(8)]
    res = run_bass_kernel_spmd(nc, in_maps, core_ids=list(range(8)))
    outa = np.empty((B, T, D), np.float32)
    for c in range(8):
        b, hg = c // 2, c % 2
        outa[b, :, 512 * hg:512 * hg + 512] = res.results[c]["out"]
    kernel.last_results = res.results
    return outa
